# version 03
# baseline (speedup 1.0000x reference)
import numpy as np
import ml_dtypes
from contextlib import ExitStack

import concourse.bass as bass
import concourse.mybir as mybir
from concourse.bass_utils import run_bass_kernel_spmd

F32 = mybir.dt.float32
BF16 = mybir.dt.bfloat16
ALU = mybir.AluOpType
ACT = mybir.ActivationFunctionType
AX = mybir.AxisListType.X

D = 2048
T = 4096
OWN = 2048
NH = 16
HID = 8192
NDC = D // 128
NHC = HID // 128
SQD = 1.0 / np.sqrt(128.0)
LN128 = float(np.log(128.0))
RS2 = float(2.0 ** -0.5)
SOFTC = 25.0
EPS = 1e-5

DEBUG = False
LAST_CH = None

ENGS = ("sync", "tensor", "vector", "scalar", "gpsimd")


class Chan:
    def __init__(self, stack, nc, name):
        self.sem = stack.enter_context(nc.semaphore(name))
        self.n = 0

    def bump(self, k=1):
        self.n += k
        return self.n


class Phase:

    def __init__(self, nc):
        self.nc = nc
        self.ops = {k: [] for k in ENGS}
        self.mark = {}

    def op(self, eng, fn):
        self.ops[eng].append(fn)

    def wait(self, eng, ch, val):
        if val is None or val <= 0:
            return
        key = (eng, id(ch))
        if self.mark.get(key, -1) >= val:
            return
        self.mark[key] = val
        self.ops[eng].append(lambda e, s=ch.sem, v=val: e.wait_ge(s, v))

    def call(self, eng, meth, *args, inc=None, inck=1, **kw):
        if inc is not None:
            inc.bump(inck)

            def f(e, meth=meth, args=args, kw=kw, sem=inc.sem, k=inck):
                try:
                    getattr(e, meth)(*args, **kw).then_inc(sem, k)
                except Exception:
                    print("FAIL", meth, [getattr(a, "shape", a) for a in args], kw)
                    raise
        else:
            def f(e, meth=meth, args=args, kw=kw):
                try:
                    getattr(e, meth)(*args, **kw)
                except Exception:
                    print("FAIL", meth, [getattr(a, "shape", a) for a in args], kw)
                    raise
        self.ops[eng].append(f)
        return inc.n if inc is not None else None

    def mm(self, out, lhsT, rhs, start, stop, inc=None):
        return self.call("tensor", "matmul", out, lhsT, rhs, start=start,
                         stop=stop, skip_group_check=True, inc=inc)

    def dma(self, eng, ch, out, in_, **kw):
        return self.call(eng, "dma_start", inc=ch, inck=16, out=out, in_=in_, **kw)

    def close(self):
        with self.nc.Block() as blk:
            for name in ENGS:
                fns = self.ops[name]
                if fns:
                    getattr(blk, name)(lambda e, fns=fns: [f(e) for f in fns])


class Guard:

    def __init__(self):
        self.d = {}

    def wait(self, ph, eng, key):
        if key in self.d:
            c, v = self.d[key]
            ph.wait(eng, c, v)

    def set(self, key, ch, val):
        self.d[key] = (ch, val)


def build(debug=False):
    nc = bass.Bass("TRN2", target_bir_lowering=False)
    st = ExitStack()
    scratch_kind = "ExternalOutput" if debug else "Internal"

    xf = nc.dram_tensor("xf", [D, T], F32, kind="ExternalInput").ap()
    wqkvT = nc.dram_tensor("wqkvT", [D, 3 * D], F32, kind="ExternalInput").ap()
    wprojT = nc.dram_tensor("wprojT", [D, D], F32, kind="ExternalInput").ap()
    w1T = nc.dram_tensor("w1T", [D, HID], F32, kind="ExternalInput").ap()
    w2T = nc.dram_tensor("w2T", [HID, D], F32, kind="ExternalInput").ap()
    w3T = nc.dram_tensor("w3T", [D, HID], F32, kind="ExternalInput").ap()
    tmask = nc.dram_tensor("tmask", [128, 32 * 128], F32, kind="ExternalInput").ap()
    hconst = nc.dram_tensor("hconst", [128, 128], BF16, kind="ExternalInput").ap()
    out_d = nc.dram_tensor("out", [D, OWN], F32, kind="ExternalOutput").ap()

    kspill = nc.dram_tensor("kspill", [NH, 128, T], BF16, kind=scratch_kind).ap()
    vspill = nc.dram_tensor("vspill", [NH, 128, T], BF16, kind=scratch_kind).ap()
    qspill = nc.dram_tensor("qspill", [NH, 128, OWN], BF16, kind=scratch_kind).ap()
    aspill = nc.dram_tensor("aspill", [D, OWN], F32, kind=scratch_kind).ap()
    lnrows = nc.dram_tensor("lnrows", [2, OWN], F32, kind=scratch_kind).ap()
    fspill = nc.dram_tensor("fspill", [D, OWN], F32, kind=scratch_kind).ap()
    dbg = {}
    if debug:
        dbg["ln1"] = nc.dram_tensor("dbg_ln1", [D, T], BF16, kind="ExternalOutput").ap()
        dbg["attnout"] = nc.dram_tensor("dbg_attnout", [128, NH * OWN], BF16,
                                        kind="ExternalOutput").ap()
        dbg["pT"] = nc.dram_tensor("dbg_pT", [128, 34816], BF16,
                                   kind="ExternalOutput").ap()

    ch = {}
    for name in ("ldx", "ldw", "ldm", "mm", "v", "sc", "spk", "spv", "spq",
                 "spa", "spf", "out", "lda", "ldf"):
        ch[name] = Chan(st, nc, "ch_" + name)

    ones_f = st.enter_context(nc.sbuf_tensor("ones_f", [128, 1], F32))
    ones_b = st.enter_context(nc.sbuf_tensor("ones_b", [128, 1], BF16))
    onesrow = st.enter_context(nc.sbuf_tensor("onesrow", [1, 128], F32))
    hc_t = st.enter_context(nc.sbuf_tensor("hc_t", [128, 128], BF16))
    bconst = st.enter_context(nc.sbuf_tensor("bconst", [128, 1], F32))

    def ln_rows(ph, ps_sx, ps_sq, mrow, rrow, vrow, trow, n):
        ph.wait("vector", ch["mm"], ch["mm"].n)
        ph.call("vector", "tensor_scalar", mrow, ps_sx, 1.0 / n, None, ALU.mult,
                inc=ch["v"])
        ph.call("vector", "tensor_scalar", vrow, ps_sq, 1.0 / n, EPS, ALU.mult,
                ALU.add, inc=ch["v"])
        ph.call("vector", "tensor_tensor", trow, mrow, mrow, ALU.mult, inc=ch["v"])
        vv = ph.call("vector", "tensor_tensor", vrow, vrow, trow, ALU.subtract,
                     inc=ch["v"])
        ph.wait("scalar", ch["v"], vv)
        ph.call("scalar", "activation", trow, vrow, ACT.Ln, inc=ch["sc"])
        rv = ph.call("scalar", "activation", rrow, trow, ACT.Exp, scale=-0.5,
                     inc=ch["sc"])
        return vv, rv

    ph = Phase(nc)
    for c0 in ch.values():
        ph.call("sync", "sem_clear", c0.sem)
    ph.close()

    ph = Phase(nc)
    ph.call("vector", "memset", ones_f[:], 1.0, inc=ch["v"])
    ph.call("vector", "memset", ones_b[:], 1.0, inc=ch["v"])
    ph.call("vector", "memset", onesrow[:], 1.0, inc=ch["v"])
    ph.call("vector", "memset", bconst[:], -SOFTC, inc=ch["v"])
    ph.dma("sync", ch["ldm"], hc_t[:], hconst)
    ph.wait("sync", ch["ldm"], ch["ldm"].n)
    ph.wait("sync", ch["v"], ch["v"].n)
    ph.close()

    with ExitStack() as sa1:
        xs_t = [sa1.enter_context(nc.sbuf_tensor(f"xs{i}", [128, NDC * 256], F32))
                for i in range(2)]
        ln1 = sa1.enter_context(nc.sbuf_tensor("ln1", [128, NDC * OWN], BF16))
        sqb = [sa1.enter_context(nc.sbuf_tensor(f"sqb{i}", [128, 256], F32))
               for i in range(2)]
        rows = sa1.enter_context(nc.sbuf_tensor("rows", [1, 4 * 256], F32))
        tmp1 = sa1.enter_context(nc.sbuf_tensor("tmp1", [128, 256], F32))
        wq_t = [sa1.enter_context(nc.sbuf_tensor(f"wq{i}", [128, NDC * 128], BF16))
                for i in range(2)]
        wk_t = [sa1.enter_context(nc.sbuf_tensor(f"wk{i}", [128, NDC * 128], BF16))
                for i in range(2)]
        wv_t = [sa1.enter_context(nc.sbuf_tensor(f"wv{i}", [128, NDC * 128], BF16))
                for i in range(2)]
        kst = [sa1.enter_context(nc.sbuf_tensor(f"kst{i}", [128, 2048], BF16))
               for i in range(2)]
        vst = [sa1.enter_context(nc.sbuf_tensor(f"vst{i}", [128, 2048], BF16))
               for i in range(2)]
        qst = [sa1.enter_context(nc.sbuf_tensor(f"qst{i}", [128, 2048], BF16))
               for i in range(2)]
        ps_sx = sa1.enter_context(nc.psum_tensor("ps_sx", [1, 512], F32))
        ps_sq = sa1.enter_context(nc.psum_tensor("ps_sq", [1, 512], F32))
        ps_mb = sa1.enter_context(nc.psum_tensor("ps_mb", [128, 512], F32))
        ps_rb = sa1.enter_context(nc.psum_tensor("ps_rb", [128, 512], F32))
        ps_k = [sa1.enter_context(nc.psum_tensor(f"ps_k{i}", [128, 512], F32))
                for i in range(2)]
        g = Guard()

        for part in range(2):
            ph = Phase(nc)
            for s in range(8):
                buf = s % 2
                col0 = part * OWN + 256 * s
                g.wait(ph, "sync", ("xs", buf))
                ldv = None
                for i in range(4):
                    dcs = slice(4 * i, 4 * i + 4)
                    ldv = ph.dma(
                        "sync", ch["ldx"],
                        xs_t[buf][:].rearrange("p (c t) -> p c t", c=NDC)[:, dcs, :],
                        xf.rearrange("(c p) t -> p c t", p=128)[:, dcs,
                                                               col0:col0 + 256])
                g.wait(ph, "tensor", ("ps_sx",))
                g.wait(ph, "tensor", ("ps_sq",))
                ph.wait("scalar", ch["ldx"], ldv)
                ph.wait("tensor", ch["ldx"], ldv)
                mmq = []
                for dc in range(NDC):
                    xchunk = xs_t[buf][:, 256 * dc:256 * dc + 256]
                    if dc >= 2:
                        ph.wait("scalar", ch["mm"], mmq[dc - 2])
                    sqv = ph.call("scalar", "activation", sqb[dc % 2][:], xchunk,
                                  ACT.Square, inc=ch["sc"])
                    ph.mm(ps_sx[:, 0:256], ones_f[:], xchunk, dc == 0,
                          dc == NDC - 1,
                          inc=(ch["mm"] if dc == NDC - 1 else None))
                    ph.wait("tensor", ch["sc"], sqv)
                    mmq.append(ph.mm(ps_sq[:, 0:256], ones_f[:], sqb[dc % 2][:],
                                     dc == 0, dc == NDC - 1, inc=ch["mm"]))
                mrow, rrow = rows[:, 0:256], rows[:, 256:512]
                vrow, trow = rows[:, 512:768], rows[:, 768:1024]
                vv, rv = ln_rows(ph, ps_sx[:, 0:256], ps_sq[:, 0:256], mrow,
                                 rrow, vrow, trow, D)
                g.set(("ps_sx",), ch["v"], ch["v"].n)
                g.set(("ps_sq",), ch["v"], ch["v"].n)
                g.wait(ph, "tensor", ("ps_mb",))
                g.wait(ph, "tensor", ("ps_rb",))
                ph.wait("tensor", ch["v"], vv)
                ph.wait("tensor", ch["sc"], rv)
                ph.mm(ps_mb[:, 0:256], onesrow[:], mrow, True, True)
                bcv = ph.mm(ps_rb[:, 0:256], onesrow[:], rrow, True, True,
                            inc=ch["mm"])
                ph.wait("vector", ch["mm"], bcv)
                av = None
                for dc in range(NDC):
                    xchunk = xs_t[buf][:, 256 * dc:256 * dc + 256]
                    ph.call("vector", "tensor_tensor", tmp1[:], xchunk,
                            ps_mb[:, 0:256], ALU.subtract)
                    av = ph.call("vector", "tensor_tensor",
                                 ln1[:, OWN * dc + 256 * s:OWN * dc + 256 * s + 256],
                                 tmp1[:], ps_rb[:, 0:256], ALU.mult, inc=ch["v"])
                g.set(("xs", buf), ch["v"], av)
                g.set(("ps_mb",), ch["v"], av)
                g.set(("ps_rb",), ch["v"], av)

            if debug:
                ph.wait("scalar", ch["v"], ch["v"].n)
                dv = ph.dma("scalar", ch["spa"],
                            dbg["ln1"].rearrange("(c p) t -> p c t", p=128)
                            [:, :, part * OWN:(part + 1) * OWN],
                            ln1[:].rearrange("p (c t) -> p c t", c=NDC))
                ph.wait("scalar", ch["spa"], dv)

            ph.wait("tensor", ch["v"], ch["v"].n)
            for h in range(NH):
                buf = h % 2
                for w_t, col in ((wq_t, 128 * h), (wk_t, D + 128 * h),
                                 (wv_t, 2 * D + 128 * h)):
                    g.wait(ph, "gpsimd", ("wbuf", id(w_t), buf))
                    for i in range(4):
                        dcs = slice(4 * i, 4 * i + 4)
                        ldv = ph.dma(
                            "gpsimd", ch["ldw"],
                            w_t[buf][:].rearrange("p (c o) -> p c o", c=NDC)
                            [:, dcs, :],
                            wqkvT.rearrange("(c p) o -> p c o", p=128)
                            [:, dcs, col:col + 128])
                ph.wait("tensor", ch["ldw"], ch["ldw"].n)
                mats = [(wk_t[buf], kst[buf]), (wv_t[buf], vst[buf])]
                if part == 0:
                    mats.append((wq_t[buf], qst[buf]))
                for (w_t_b, stage) in mats:
                    g.wait(ph, "scalar", ("stage", id(stage)))
                    for tw in range(4):
                        pb = tw % 2
                        g.wait(ph, "tensor", ("ps_k", pb))
                        mv = None
                        for dc in range(NDC):
                            mv = ph.mm(
                                ps_k[pb][:], w_t_b[:, 128 * dc:128 * dc + 128],
                                ln1[:, OWN * dc + 512 * tw:OWN * dc + 512 * tw + 512],
                                dc == 0, dc == NDC - 1,
                                inc=(ch["mm"] if dc == NDC - 1 else None))
                        ph.wait("scalar", ch["mm"], mv)
                        cv = ph.call("scalar", "copy",
                                     stage[:, 512 * tw:512 * tw + 512],
                                     ps_k[pb][:], inc=ch["sc"])
                        g.set(("ps_k", pb), ch["sc"], cv)
                    if stage is kst[buf]:
                        spch, dst = ch["spk"], kspill[h, :, part * OWN:(part + 1) * OWN]
                    elif stage is vst[buf]:
                        spch, dst = ch["spv"], vspill[h, :, part * OWN:(part + 1) * OWN]
                    else:
                        spch, dst = ch["spq"], qspill[h, :, :]
                    sv = ph.dma("scalar", spch, dst, stage[:])
                    g.set(("stage", id(stage)), spch, sv)
                for w_t in (wq_t, wk_t, wv_t):
                    g.set(("wbuf", id(w_t), buf), ch["mm"], ch["mm"].n)
            ph.close()

    with ExitStack() as sattn:
        attnout = sattn.enter_context(
            nc.sbuf_tensor("attnout", [128, NH * OWN], BF16))

        off = {}
        acc = 0
        for c in range(32):
            off[c] = acc
            acc += OWN - 128 * (c % 16)
        PTW = acc

        with ExitStack() as sa2:
            ttiles = sa2.enter_context(nc.sbuf_tensor("ttiles", [128, 32 * 128], F32))
            kT_t = sa2.enter_context(nc.sbuf_tensor("kT_t", [128, T], BF16))
            vT_t = sa2.enter_context(nc.sbuf_tensor("vT_t", [128, T], BF16))
            vN_t = sa2.enter_context(nc.sbuf_tensor("vN_t", [128, T], BF16))
            qT_t = sa2.enter_context(nc.sbuf_tensor("qT_t", [128, OWN], BF16))
            pT = sa2.enter_context(nc.sbuf_tensor("pT", [128, PTW], BF16))
            rs_r = sa2.enter_context(nc.sbuf_tensor("rs_r", [1, 512], F32))
            rsb = sa2.enter_context(nc.sbuf_tensor("rsb", [128, 512], F32))
            ps_sA = sa2.enter_context(nc.psum_tensor("ps_sA", [128, 2048], F32))
            ps_sB = sa2.enter_context(nc.psum_tensor("ps_sB", [128, 1024], F32))
            ps_av = sa2.enter_context(nc.psum_tensor("ps_av", [128, 512], F32))
            ps_sum = sa2.enter_context(nc.psum_tensor("ps_sum", [1, 512], F32))
            g = Guard()

            ph = Phase(nc)
            tmv = None
            for i in range(2):
                cs = slice(16 * i, 16 * i + 16)
                tmv = ph.dma("sync", ch["ldm"],
                             ttiles[:].rearrange("p (c f) -> p c f", c=32)[:, cs, :],
                             tmask.rearrange("p (c f) -> p c f", c=32)[:, cs, :])
            ph.wait("vector", ch["ldm"], tmv)
            ph.wait("sync", ch["spk"], ch["spk"].n)
            ph.wait("sync", ch["spv"], ch["spv"].n)
            ph.wait("sync", ch["spq"], ch["spq"].n)

            def tidx(c):
                return 2 * (c % 16) + (0 if c < 16 else 1)

            for h in range(NH):
                g.wait(ph, "sync", ("kT",))
                lk = ph.dma("sync", ch["lda"], kT_t[:], kspill[h])
                g.wait(ph, "sync", ("qT",))
                lq = ph.dma("sync", ch["lda"], qT_t[:], qspill[h])
                g.wait(ph, "sync", ("vT",))
                lv = ph.dma("sync", ch["lda"], vT_t[:], vspill[h])
                ph.wait("tensor", ch["lda"], lq)

                g.wait(ph, "scalar", ("pT",))
                order = []
                for i2 in range(8):
                    order += [i2, i2 + 8]
                order = order + [c2 + 16 for c2 in order]
                for c in order:
                    i = c % 16
                    qs = 128 * i
                    if i < 8:
                        ps, base = ps_sA, 0
                    else:
                        ps, base = ps_sB, 1024
                    g.wait(ph, "tensor", ("ps_sc", base))
                    mv = None
                    s0 = qs
                    while s0 < OWN:
                        s1 = min(OWN, 512 * (s0 // 512 + 1))
                        mv = ph.mm(ps[:, s0 - base:s1 - base],
                                   kT_t[:, 128 * c:128 * c + 128],
                                   qT_t[:, s0:s1], True, True, inc=ch["mm"])
                        s0 = s1
                    ph.wait("vector", ch["mm"], mv)
                    mkv = ph.call(
                        "vector", "tensor_tensor", ps[:, qs - base:qs - base + 128],
                        ps[:, qs - base:qs - base + 128],
                        ttiles[:, tidx(c) * 128:tidx(c) * 128 + 128],
                        ALU.min, inc=ch["v"])
                    ph.wait("scalar", ch["v"], mkv)
                    ev = ph.call("scalar", "activation",
                                 pT[:, off[c]:off[c] + OWN - qs],
                                 ps[:, qs - base:OWN - base], ACT.Exp, scale=SQD,
                                 bias=bconst[:], inc=ch["sc"])
                    g.set(("ps_sc", base), ch["sc"], ev)

                ph.wait("sync", ch["lda"], lv)
                g.wait(ph, "sync", ("vN",))
                tvv = ph.call("sync", "dma_start_transpose",
                              vN_t[:].rearrange("p (c d) -> p c d", c=32),
                              vT_t[:].rearrange("p (c k) -> p c k", c=32),
                              inc=ch["lda"], inck=16)
                g.set(("vT",), ch["lda"], tvv)
                ph.wait("tensor", ch["sc"], ch["sc"].n)
                ph.wait("tensor", ch["lda"], tvv)
                for w in range(4):
                    chunks = list(range(0, 4 * w + 4)) + \
                        list(range(16, 16 + 4 * w + 4))
                    g.wait(ph, "tensor", ("ps_av",))
                    g.wait(ph, "tensor", ("ps_sum",))
                    first = True
                    avv = None
                    for c in chunks:
                        qs = 128 * (c % 16)
                        if qs <= 512 * w:
                            p0 = off[c] + 512 * w - qs
                            rhs = pT[:, p0:p0 + 512]
                            o_av, o_sm = ps_av[:, 0:512], ps_sum[:, 0:512]
                        else:
                            wd = 512 * (w + 1) - qs
                            rhs = pT[:, off[c]:off[c] + wd]
                            o_av = ps_av[:, 512 - wd:512]
                            o_sm = ps_sum[:, 512 - wd:512]
                        last = (c == chunks[-1])
                        ph.mm(o_av, vN_t[:, 128 * c:128 * c + 128], rhs,
                              first, last, inc=(ch["mm"] if last else None))
                        avv = ph.mm(o_sm, ones_b[:], rhs, first, last,
                                    inc=(ch["mm"] if last else None))
                        first = False
                    ph.wait("vector", ch["mm"], avv)
                    rv = ph.call("vector", "reciprocal", rs_r[:], ps_sum[:],
                                 inc=ch["v"])
                    g.set(("ps_sum",), ch["v"], rv)
                    g.wait(ph, "tensor", ("ps_sc", 0))
                    ph.wait("tensor", ch["v"], rv)
                    mv = ph.mm(ps_sA[:, 0:512], onesrow[:], rs_r[:], True, True,
                               inc=ch["mm"])
                    ph.wait("vector", ch["mm"], mv)
                    ph.call("vector", "tensor_copy", rsb[:], ps_sA[:, 0:512],
                            inc=ch["v"])
                    cv = ph.call("vector", "tensor_tensor",
                                 attnout[:, OWN * h + 512 * w:OWN * h + 512 * w + 512],
                                 rsb[:], ps_av[:], ALU.mult, inc=ch["v"])
                    g.set(("ps_sc", 0), ch["v"], cv)
                    g.set(("ps_av",), ch["v"], cv)
                g.set(("pT",), ch["mm"], ch["mm"].n)
                g.set(("kT",), ch["mm"], ch["mm"].n)
                g.set(("qT",), ch["mm"], ch["mm"].n)
                g.set(("vN",), ch["mm"], ch["mm"].n)
            if debug:
                ph.wait("scalar", ch["v"], ch["v"].n)
                dv = ph.dma("scalar", ch["spa"], dbg["attnout"], attnout[:])
                dv = ph.dma("scalar", ch["spa"], dbg["pT"], pT[:])
                ph.wait("scalar", ch["spa"], dv)
            ph.close()

        with ExitStack() as sa3:
            wp_t = sa3.enter_context(nc.sbuf_tensor("wp_t", [128, NDC * D], BF16))
            xw_t = sa3.enter_context(nc.sbuf_tensor("xw_t", [128, NDC * 256], F32))
            projw = sa3.enter_context(nc.sbuf_tensor("projw", [128, NDC * 256], F32))
            awin = sa3.enter_context(nc.sbuf_tensor("awin", [128, NDC * 256], F32))
            sqb3 = [sa3.enter_context(nc.sbuf_tensor(f"sqb3{i}", [128, 256], F32))
                    for i in range(2)]
            tmp3 = sa3.enter_context(nc.sbuf_tensor("tmp3", [128, 256], F32))
            rows3 = sa3.enter_context(nc.sbuf_tensor("rows3", [1, 4 * 256], F32))
            ps_p = [sa3.enter_context(nc.psum_tensor(f"ps_p{i}", [128, 256], F32))
                    for i in range(2)]
            ps_sx = sa3.enter_context(nc.psum_tensor("ps3_sx", [1, 256], F32))
            ps_sq = sa3.enter_context(nc.psum_tensor("ps3_sq", [1, 256], F32))
            ps_mb = sa3.enter_context(nc.psum_tensor("ps3_mb", [128, 256], F32))
            ps_rb = sa3.enter_context(nc.psum_tensor("ps3_rb", [128, 256], F32))
            ps_3x = sa3.enter_context(nc.psum_tensor("ps3_3x", [1, 256], F32))
            ps_3q = sa3.enter_context(nc.psum_tensor("ps3_3q", [1, 256], F32))
            g = Guard()

            ph = Phase(nc)
            for i in range(4):
                fcs = slice(4 * i, 4 * i + 4)
                lw = ph.dma("gpsimd", ch["ldw"],
                            wp_t[:].rearrange("p (c o) -> p c o", c=NDC)[:, fcs, :],
                            wprojT.rearrange("(c p) o -> p c o", p=128)[:, fcs, :])
            for w in range(8):
                g.wait(ph, "sync", ("xw",))
                lx = None
                for i in range(4):
                    dcs = slice(4 * i, 4 * i + 4)
                    lx = ph.dma("sync", ch["ldx"],
                                xw_t[:].rearrange("p (c t) -> p c t", c=NDC)
                                [:, dcs, :],
                                xf.rearrange("(c p) t -> p c t", p=128)
                                [:, dcs, 256 * w:256 * w + 256])
                ph.wait("tensor", ch["ldw"], lw)
                g.wait(ph, "tensor", ("ps_sx",))
                g.wait(ph, "tensor", ("ps_sq",))
                prevq = None
                for oc in range(NDC):
                    pb = oc % 2
                    g.wait(ph, "tensor", ("ps_p", pb))
                    mv = None
                    for fc in range(NDC):
                        mv = ph.mm(
                            ps_p[pb][:],
                            wp_t[:, D * fc + 128 * oc:D * fc + 128 * oc + 128],
                            attnout[:, OWN * fc + 256 * w:OWN * fc + 256 * w + 256],
                            fc == 0, fc == NDC - 1,
                            inc=(ch["mm"] if fc == NDC - 1 else None))
                    ph.wait("vector", ch["mm"], mv)
                    pv = ph.call("vector", "tensor_copy",
                                 projw[:, 256 * oc:256 * oc + 256], ps_p[pb][:],
                                 inc=ch["v"])
                    g.set(("ps_p", pb), ch["v"], pv)
                    ph.wait("scalar", ch["v"], pv)
                    ph.wait("scalar", ch["mm"], prevq)
                    sv = ph.call("scalar", "activation", sqb3[oc % 2][:],
                                 projw[:, 256 * oc:256 * oc + 256], ACT.Square,
                                 inc=ch["sc"])
                    ph.wait("tensor", ch["v"], pv)
                    ph.mm(ps_sx[:], ones_f[:], projw[:, 256 * oc:256 * oc + 256],
                          oc == 0, oc == NDC - 1,
                          inc=(ch["mm"] if oc == NDC - 1 else None))
                    ph.wait("tensor", ch["sc"], sv)
                    prevq = ph.mm(ps_sq[:], ones_f[:], sqb3[oc % 2][:], oc == 0,
                                  oc == NDC - 1, inc=ch["mm"])
                mrow, rrow = rows3[:, 0:256], rows3[:, 256:512]
                vrow, trow = rows3[:, 512:768], rows3[:, 768:1024]
                vv, rv = ln_rows(ph, ps_sx[:], ps_sq[:], mrow, rrow, vrow, trow, D)
                g.set(("ps_sx",), ch["v"], ch["v"].n)
                g.set(("ps_sq",), ch["v"], ch["v"].n)
                g.wait(ph, "tensor", ("ps_mb",))
                g.wait(ph, "tensor", ("ps_rb",))
                ph.wait("tensor", ch["v"], vv)
                ph.wait("tensor", ch["sc"], rv)
                ph.mm(ps_mb[:], onesrow[:], mrow, True, True)
                bcv = ph.mm(ps_rb[:], onesrow[:], rrow, True, True, inc=ch["mm"])
                ph.wait("vector", ch["mm"], bcv)
                ph.wait("vector", ch["ldx"], lx)
                g.wait(ph, "tensor", ("ps_3x",))
                g.wait(ph, "tensor", ("ps_3q",))
                g.wait(ph, "vector", ("awin",))
                av = None
                prevq = None
                for dc in range(NDC):
                    pchunk = projw[:, 256 * dc:256 * dc + 256]
                    ph.call("vector", "tensor_tensor", tmp3[:], pchunk, ps_mb[:],
                            ALU.subtract)
                    ph.call("vector", "tensor_tensor", tmp3[:], tmp3[:], ps_rb[:],
                            ALU.mult)
                    av = ph.call("vector", "tensor_tensor",
                                 awin[:, 256 * dc:256 * dc + 256], tmp3[:],
                                 xw_t[:, 256 * dc:256 * dc + 256], ALU.add,
                                 inc=ch["v"])
                    ph.wait("scalar", ch["v"], av)
                    ph.wait("scalar", ch["mm"], prevq)
                    sv = ph.call("scalar", "activation", sqb3[dc % 2][:],
                                 awin[:, 256 * dc:256 * dc + 256], ACT.Square,
                                 inc=ch["sc"])
                    ph.wait("tensor", ch["v"], av)
                    ph.mm(ps_3x[:], ones_f[:], awin[:, 256 * dc:256 * dc + 256],
                          dc == 0, dc == NDC - 1,
                          inc=(ch["mm"] if dc == NDC - 1 else None))
                    ph.wait("tensor", ch["sc"], sv)
                    prevq = ph.mm(ps_3q[:], ones_f[:], sqb3[dc % 2][:], dc == 0,
                                  dc == NDC - 1, inc=ch["mm"])
                g.set(("ps_mb",), ch["v"], av)
                g.set(("ps_rb",), ch["v"], av)
                g.set(("xw",), ch["v"], av)
                vv, rv = ln_rows(ph, ps_3x[:], ps_3q[:], mrow, rrow, vrow, trow, D)
                g.set(("ps_3x",), ch["v"], ch["v"].n)
                g.set(("ps_3q",), ch["v"], ch["v"].n)
                ph.wait("scalar", ch["v"], vv)
                ph.dma("scalar", ch["spa"], lnrows[0:1, 256 * w:256 * w + 256],
                       mrow)
                ph.dma("scalar", ch["spa"], lnrows[1:2, 256 * w:256 * w + 256],
                       rrow)
                ph.wait("scalar", ch["v"], av)
                sv = None
                for i in range(4):
                    dcs = slice(4 * i, 4 * i + 4)
                    sv = ph.dma("scalar", ch["spa"],
                                aspill.rearrange("(c p) t -> p c t", p=128)
                                [:, dcs, 256 * w:256 * w + 256],
                                awin[:].rearrange("p (c t) -> p c t", c=NDC)
                                [:, dcs, :])
                g.set(("awin",), ch["spa"], sv)
            ph.close()

    for rt in range(2):
        tc0 = 1024 * rt
        with ExitStack() as sf:
            m4_t = sf.enter_context(nc.sbuf_tensor(f"m4_t{rt}", [1, 1024], F32))
            r4_t = sf.enter_context(nc.sbuf_tensor(f"r4_t{rt}", [1, 1024], F32))
            hd = sf.enter_context(nc.sbuf_tensor(f"hd{rt}", [128, NHC * 1024], BF16))
            s_ln3 = sf.enter_context(ExitStack())
            ln3 = s_ln3.enter_context(nc.sbuf_tensor(f"ln3{rt}", [128, NDC * 1024], BF16))

            with ExitStack() as s1:
                aT_t = s1.enter_context(nc.sbuf_tensor(f"aT_t{rt}", [128, NDC * 512], F32))
                tmpf = s1.enter_context(nc.sbuf_tensor(f"tmpf{rt}", [128, 512], F32))
                lnr = s1.enter_context(nc.sbuf_tensor(f"lnr{rt}", [1, 1024], F32))
                ps_mb = s1.enter_context(nc.psum_tensor(f"psf_mb{rt}", [128, 512], F32))
                ps_rb = s1.enter_context(nc.psum_tensor(f"psf_rb{rt}", [128, 512], F32))
                ph = Phase(nc)
                ph.wait("sync", ch["spa"], ch["spa"].n)
                for seg in range(2):
                    col = tc0 + 512 * seg
                    if seg > 0:
                        ph.wait("sync", ch["v"], ch["v"].n)
                    lr = ph.dma("sync", ch["ldm"], lnr[:, 0:512],
                                lnrows[0:1, col:col + 512])
                    lr = ph.dma("sync", ch["ldm"], lnr[:, 512:1024],
                                lnrows[1:2, col:col + 512])
                    ph.wait("tensor", ch["ldm"], lr)
                    la = None
                    for i in range(4):
                        dcs = slice(4 * i, 4 * i + 4)
                        la = ph.dma("sync", ch["lda"],
                                    aT_t[:].rearrange("p (c t) -> p c t", c=NDC)
                                    [:, dcs, :],
                                    aspill.rearrange("(c p) t -> p c t", p=128)
                                    [:, dcs, col:col + 512])
                    if seg > 0:
                        ph.wait("tensor", ch["v"], ch["v"].n)
                    ph.mm(ps_mb[:], onesrow[:], lnr[:, 0:512], True, True)
                    bcv = ph.mm(ps_rb[:], onesrow[:], lnr[:, 512:1024],
                                True, True, inc=ch["mm"])
                    ph.wait("vector", ch["mm"], bcv)
                    ph.wait("vector", ch["lda"], la)
                    for dc in range(NDC):
                        ph.call("vector", "tensor_tensor", tmpf[:],
                                aT_t[:, 512 * dc:512 * dc + 512], ps_mb[:],
                                ALU.subtract)
                        ph.call("vector", "tensor_tensor",
                                ln3[:, 1024 * dc + 512 * seg:1024 * dc + 512 * seg + 512],
                                tmpf[:], ps_rb[:], ALU.mult, inc=ch["v"])
                ph.close()

            with ExitStack() as s2:
                w1g = [s2.enter_context(nc.sbuf_tensor(f"w1g{rt}_{i}", [128, NDC * 256], BF16))
                       for i in range(2)]
                w3g = [s2.enter_context(nc.sbuf_tensor(f"w3g{rt}_{i}", [128, NDC * 256], BF16))
                       for i in range(2)]
                tmpg = [s2.enter_context(nc.sbuf_tensor(f"tmpg{rt}_{i}", [128, 512], F32))
                        for i in range(2)]
                ps_g = [s2.enter_context(nc.psum_tensor(f"psf_g{rt}_{i}", [128, 512], F32))
                        for i in range(2)]
                ps_u = [s2.enter_context(nc.psum_tensor(f"psf_u{rt}_{i}", [128, 512], F32))
                        for i in range(2)]
                g = Guard()
                ph = Phase(nc)
                nib = 0
                for hg in range(32):
                    buf = hg % 2
                    hcol = 256 * hg
                    g.wait(ph, "gpsimd", ("w1g", buf))
                    lw = None
                    for wsrc, wdst in ((w1T, w1g[buf]), (w3T, w3g[buf])):
                        for i in range(2):
                            dcs = slice(8 * i, 8 * i + 8)
                            lw = ph.dma("gpsimd", ch["ldw"],
                                        wdst[:].rearrange("p (c o) -> p c o", c=NDC)
                                        [:, dcs, :],
                                        wsrc.rearrange("(c p) o -> p c o", p=128)
                                        [:, dcs, hcol:hcol + 256])
                    ph.wait("tensor", ch["ldw"], lw)
                    for hcl in range(2):
                        c = 2 * hg + hcl
                        for hf in range(2):
                            pb = nib % 2
                            nib += 1
                            g.wait(ph, "tensor", ("ps_g", pb))
                            g.wait(ph, "tensor", ("ps_u", pb))
                            uv = None
                            for dc in range(NDC):
                                l1 = w1g[buf][:, 256 * dc + 128 * hcl:
                                              256 * dc + 128 * hcl + 128]
                                l3 = w3g[buf][:, 256 * dc + 128 * hcl:
                                              256 * dc + 128 * hcl + 128]
                                rhs = ln3[:, 1024 * dc + 512 * hf:
                                          1024 * dc + 512 * hf + 512]
                                lastf = (dc == NDC - 1)
                                ph.mm(ps_g[pb][:], l1, rhs, dc == 0, lastf,
                                      inc=(ch["mm"] if lastf else None))
                                uv = ph.mm(ps_u[pb][:], l3, rhs, dc == 0, lastf,
                                           inc=(ch["mm"] if lastf else None))
                            g.wait(ph, "scalar", ("tmpg", pb))
                            ph.wait("scalar", ch["mm"], uv)
                            sv = ph.call("scalar", "activation", tmpg[pb][:],
                                         ps_g[pb][:], ACT.Silu, inc=ch["sc"])
                            g.set(("ps_g", pb), ch["sc"], sv)
                            ph.wait("vector", ch["sc"], sv)
                            mv = ph.call("vector", "tensor_tensor",
                                         hd[:, 1024 * c + 512 * hf:
                                            1024 * c + 512 * hf + 512],
                                         tmpg[pb][:], ps_u[pb][:], ALU.mult,
                                         inc=ch["v"])
                            g.set(("ps_u", pb), ch["v"], mv)
                            g.set(("tmpg", pb), ch["v"], mv)
                    g.set(("w1g", buf), ch["mm"], ch["mm"].n)
                ph.close()
            s_ln3.close()

            with ExitStack() as s3:
                z_t = s3.enter_context(nc.sbuf_tensor(f"z_t{rt}", [128, 64 * 256], F32))
                zs_t = s3.enter_context(nc.sbuf_tensor(f"zs_t{rt}", [128, 4 * 256], F32))
                ps_f = [s3.enter_context(nc.psum_tensor(f"psf_f{rt}_{i}", [128, 256], F32))
                        for i in range(2)]
                g = Guard()
                ph = Phase(nc)
                for ts in range(4):
                    scv = None
                    for c in range(NHC):
                        pb = c % 2
                        g.wait(ph, "tensor", ("ps_f", pb))
                        mv = ph.mm(ps_f[pb][:], hc_t[:],
                                   hd[:, 1024 * c + 256 * ts:1024 * c + 256 * ts + 256],
                                   True, True, inc=ch["mm"])
                        ph.wait("scalar", ch["mm"], mv)
                        scv = ph.call("scalar", "mul",
                                      z_t[:, 256 * c:256 * c + 256], ps_f[pb][:],
                                      RS2, inc=ch["sc"])
                        g.set(("ps_f", pb), ch["sc"], scv)
                    ph.wait("vector", ch["sc"], scv)
                    zv = z_t[:].rearrange("p (c t) -> p c t", c=64)
                    lastv = None
                    for d in (32, 16, 8, 4, 2, 1):
                        ilist = [i2 for i2 in range(64) if (i2 & d) == 0]
                        for g0 in range(0, 32, 4):
                            ib = ilist[g0:g0 + 4]
                            if d >= 4:
                                iap = zv[:, ib[0]:ib[0] + 4, :]
                                jap = zv[:, ib[0] + d:ib[0] + d + 4, :]
                                sap = zs_t[:].rearrange("p (c t) -> p c t", c=4)
                            elif d == 2:
                                zz = z_t[:].rearrange(
                                    "p (g two r t) -> p g two r t", two=2, r=2,
                                    t=256)
                                gg0 = ib[0] // 4
                                iap = zz[:, gg0:gg0 + 2, 0, :, :]
                                jap = zz[:, gg0:gg0 + 2, 1, :, :]
                                sap = zs_t[:].rearrange(
                                    "p (g r t) -> p g r t", g=2, r=2, t=256)
                            else:
                                zz = z_t[:].rearrange(
                                    "p (g two t) -> p g two t", two=2, t=256)
                                gg0 = ib[0] // 2
                                iap = zz[:, gg0:gg0 + 4, 0, :]
                                jap = zz[:, gg0:gg0 + 4, 1, :]
                                sap = zs_t[:].rearrange("p (c t) -> p c t", c=4)
                            ph.call("vector", "tensor_copy", sap, iap)
                            if d > 1:
                                ph.call("vector", "tensor_tensor", iap, sap, jap,
                                        ALU.add)
                                lastv = ph.call("vector", "tensor_tensor", jap,
                                                sap, jap, ALU.subtract,
                                                inc=ch["v"])
                            else:
                                hdz = hd[:].rearrange(
                                    "p (g two t) -> p g two t", two=2, t=1024)
                                hi = hdz[:, gg0:gg0 + 4, 0,
                                         256 * ts:256 * ts + 256]
                                hj = hdz[:, gg0:gg0 + 4, 1,
                                         256 * ts:256 * ts + 256]
                                ph.call("vector", "tensor_tensor", hi, sap, jap,
                                        ALU.add)
                                lastv = ph.call("vector", "tensor_tensor", hj,
                                                sap, jap, ALU.subtract,
                                                inc=ch["v"])
                    ph.wait("tensor", ch["v"], lastv)
                ph.close()

            with ExitStack() as s4:
                w2g = [s4.enter_context(nc.sbuf_tensor(f"w2g{rt}_{i}", [128, NHC * 128], BF16))
                       for i in range(2)]
                fst = [s4.enter_context(nc.sbuf_tensor(f"fst{rt}_{i}", [128, 256], F32))
                       for i in range(2)]
                sqb4 = s4.enter_context(nc.sbuf_tensor(f"sqb4{rt}", [128, 256], F32))
                rows4 = s4.enter_context(nc.sbuf_tensor(f"rows4{rt}", [1, 2 * 256], F32))
                ps_w = [s4.enter_context(nc.psum_tensor(f"psf_w{rt}_{i}", [128, 256], F32))
                        for i in range(2)]
                ps_st = s4.enter_context(nc.psum_tensor(f"psf_st{rt}", [1, 2048], F32))
                g = Guard()
                ph = Phase(nc)
                prevq = None
                for oc in range(NDC):
                    buf = oc % 2
                    g.wait(ph, "gpsimd", ("w2g", buf))
                    lw = None
                    for i in range(8):
                        hcs = slice(8 * i, 8 * i + 8)
                        lw = ph.dma("gpsimd", ch["ldw"],
                                    w2g[buf][:].rearrange("p (c o) -> p c o", c=NHC)
                                    [:, hcs, :],
                                    w2T.rearrange("(c p) o -> p c o", p=128)
                                    [:, hcs, 128 * oc:128 * oc + 128])
                    ph.wait("tensor", ch["ldw"], lw)
                    for ts in range(4):
                        pb = ts % 2
                        g.wait(ph, "tensor", ("ps_w", pb))
                        mv = None
                        for c in range(NHC):
                            mv = ph.mm(ps_w[pb][:],
                                       w2g[buf][:, 128 * c:128 * c + 128],
                                       hd[:, 1024 * c + 256 * ts:
                                          1024 * c + 256 * ts + 256],
                                       c == 0, c == NHC - 1,
                                       inc=(ch["mm"] if c == NHC - 1 else None))
                        ph.wait("vector", ch["mm"], mv)
                        g.wait(ph, "vector", ("fst", pb))
                        g.wait(ph, "vector", ("fstmm", pb))
                        fv = ph.call("vector", "tensor_copy", fst[pb][:],
                                     ps_w[pb][:], inc=ch["v"])
                        g.set(("ps_w", pb), ch["v"], fv)
                        ph.wait("scalar", ch["v"], fv)
                        ph.wait("scalar", ch["mm"], prevq)
                        sv = ph.call("scalar", "activation", sqb4[:], fst[pb][:],
                                     ACT.Square, inc=ch["sc"])
                        ph.wait("tensor", ch["v"], fv)
                        ph.mm(ps_st[:, 512 * ts:512 * ts + 256], ones_f[:],
                              fst[pb][:], oc == 0, oc == NDC - 1, inc=ch["mm"])
                        ph.wait("tensor", ch["sc"], sv)
                        prevq = ph.mm(ps_st[:, 512 * ts + 256:512 * ts + 512],
                                      ones_f[:], sqb4[:], oc == 0, oc == NDC - 1,
                                      inc=ch["mm"])
                        g.set(("fstmm", pb), ch["mm"], prevq)
                        dv = ph.dma("scalar", ch["spf"],
                                    fspill[128 * oc:128 * oc + 128,
                                           tc0 + 256 * ts:tc0 + 256 * ts + 256],
                                    fst[pb][:])
                        g.set(("fst", pb), ch["spf"], dv)
                    g.set(("w2g", buf), ch["mm"], ch["mm"].n)
                for ts in range(4):
                    vrow, trow = rows4[:, 0:256], rows4[:, 256:512]
                    ln_rows(ph, ps_st[:, 512 * ts:512 * ts + 256],
                            ps_st[:, 512 * ts + 256:512 * ts + 512],
                            m4_t[:, 256 * ts:256 * ts + 256],
                            r4_t[:, 256 * ts:256 * ts + 256], vrow, trow, D)
                ph.close()

            with ExitStack() as s5:
                fw_t = s5.enter_context(nc.sbuf_tensor(f"fw_t{rt}", [128, NDC * 256], F32))
                aw_t = s5.enter_context(nc.sbuf_tensor(f"aw_t{rt}", [128, NDC * 256], F32))
                ow_t = s5.enter_context(nc.sbuf_tensor(f"ow_t{rt}", [128, NDC * 256], F32))
                tmpo = s5.enter_context(nc.sbuf_tensor(f"tmpo{rt}", [128, 256], F32))
                ps_mb = s5.enter_context(nc.psum_tensor(f"ps5_mb{rt}", [128, 256], F32))
                ps_rb = s5.enter_context(nc.psum_tensor(f"ps5_rb{rt}", [128, 256], F32))
                ph = Phase(nc)
                ph.wait("sync", ch["spf"], ch["spf"].n)
                for ts in range(4):
                    col = tc0 + 256 * ts
                    la = None
                    for i in range(2):
                        dcs = slice(8 * i, 8 * i + 8)
                        ph.dma("sync", ch["ldf"],
                               fw_t[:].rearrange("p (c t) -> p c t", c=NDC)
                               [:, dcs, :],
                               fspill.rearrange("(c p) t -> p c t", p=128)
                               [:, dcs, col:col + 256])
                        la = ph.dma("sync", ch["ldf"],
                                    aw_t[:].rearrange("p (c t) -> p c t", c=NDC)
                                    [:, dcs, :],
                                    aspill.rearrange("(c p) t -> p c t", p=128)
                                    [:, dcs, col:col + 256])
                    if ts > 0:
                        ph.wait("tensor", ch["v"], ch["v"].n)
                    ph.mm(ps_mb[:], onesrow[:], m4_t[:, 256 * ts:256 * ts + 256],
                          True, True)
                    bcv = ph.mm(ps_rb[:], onesrow[:],
                                r4_t[:, 256 * ts:256 * ts + 256], True, True,
                                inc=ch["mm"])
                    ph.wait("vector", ch["mm"], bcv)
                    ph.wait("vector", ch["ldf"], la)
                    ph.wait("vector", ch["out"], ch["out"].n)
                    ov = None
                    for dc in range(NDC):
                        ph.call("vector", "tensor_tensor", tmpo[:],
                                fw_t[:, 256 * dc:256 * dc + 256], ps_mb[:],
                                ALU.subtract)
                        ph.call("vector", "tensor_tensor", tmpo[:], tmpo[:],
                                ps_rb[:], ALU.mult)
                        ov = ph.call("vector", "tensor_tensor",
                                     ow_t[:, 256 * dc:256 * dc + 256], tmpo[:],
                                     aw_t[:, 256 * dc:256 * dc + 256], ALU.add,
                                     inc=ch["v"])
                    ph.wait("sync", ch["v"], ov)
                    for i in range(2):
                        dcs = slice(8 * i, 8 * i + 8)
                        ph.dma("sync", ch["out"],
                               out_d.rearrange("(c p) t -> p c t", p=128)
                               [:, dcs, col:col + 256],
                               ow_t[:].rearrange("p (c t) -> p c t", c=NDC)
                               [:, dcs, :])
                if rt == 1:
                    ph.wait("sync", ch["out"], ch["out"].n)
                    for nm in ("spk", "spv", "spq", "spa", "spf"):
                        ph.wait("sync", ch[nm], ch[nm].n)
                ph.close()

    st.close()
    global LAST_CH
    LAST_CH = {k: (c.sem.num, c.n) for k, c in ch.items()}
    return nc


_NC_CACHE = {}


def _get_nc(debug):
    key = bool(debug)
    if key not in _NC_CACHE:
        _NC_CACHE[key] = build(debug=key)
    return _NC_CACHE[key]


def _hadamard128():
    idx = np.arange(128)
    anded = idx[:, None] & idx[None, :]
    pc = np.zeros((128, 128), dtype=np.int64)
    for b in range(7):
        pc += (anded >> b) & 1
    return np.where(pc % 2 == 0, 1.0, -1.0).astype(np.float32)


def kernel(L0_attn_x, L0_attn_W_qkv, L0_attn_W_proj, L0_ffn_W1, L0_ffn_W2,
           L0_ffn_W3):
    x = np.ascontiguousarray(np.asarray(L0_attn_x, dtype=np.float32))
    wqkvT = np.ascontiguousarray(np.asarray(L0_attn_W_qkv, np.float32).T)
    wprojT = np.ascontiguousarray(np.asarray(L0_attn_W_proj, np.float32).T)
    w1T = np.ascontiguousarray(np.asarray(L0_ffn_W1, np.float32).T)
    w2T = np.ascontiguousarray(np.asarray(L0_ffn_W2, np.float32).T)
    w3T = np.ascontiguousarray(np.asarray(L0_ffn_W3, np.float32).T)
    hmat = (_hadamard128() * (2.0 ** -6)).astype(ml_dtypes.bfloat16)

    nc = _get_nc(DEBUG)
    in_maps = []
    owns = []
    for c in range(8):
        b, e = c >> 1, c & 1
        blocks = np.arange(T).reshape(16, 2, 128)
        own = blocks[:, e, :].ravel()
        other = blocks[:, 1 - e, :].ravel()
        key_order = np.concatenate([own, other])
        owns.append((b, own))
        xfc = np.ascontiguousarray(x[b].T[:, key_order])
        qg = own.reshape(16, 128).astype(np.float64)
        kg = key_order.reshape(32, 128).astype(np.float64)
        tm = np.empty((128, 32, 128), np.float32)
        for j in range(16):
            for d, c in ((0, j), (1, 16 + j)):
                tm[:, 2 * j + d, :] = (
                    (qg[j][None, :] + 0.5 - kg[c][:, None]) * 1e6)
        in_maps.append({
            "xf": xfc, "wqkvT": wqkvT, "wprojT": wprojT, "w1T": w1T,
            "w2T": w2T, "w3T": w3T, "tmask": tm.reshape(128, 32 * 128),
            "hconst": hmat,
        })

    res = run_bass_kernel_spmd(nc, in_maps, core_ids=list(range(8)))
    out = np.empty((4, T, D), dtype=np.float32)
    for c in range(8):
        b, own = owns[c]
        out[b][own, :] = res.results[c]["out"].T
    if DEBUG:
        kernel.last_results = res.results
    return out


# version 04
# speedup vs baseline: 1.0156x; 1.0156x over previous
import numpy as np
import ml_dtypes
from contextlib import ExitStack

import concourse.bass as bass
import concourse.mybir as mybir
from concourse.bass_utils import run_bass_kernel_spmd

F32 = mybir.dt.float32
BF16 = mybir.dt.bfloat16
ALU = mybir.AluOpType
ACT = mybir.ActivationFunctionType
AX = mybir.AxisListType.X

D = 2048
T = 4096
OWN = 2048
NH = 16
HID = 8192
NDC = D // 128
NHC = HID // 128
SQD = 1.0 / np.sqrt(128.0)
LN128 = float(np.log(128.0))
RS2 = float(2.0 ** -0.5)
SOFTC = 25.0
EPS = 1e-5

DEBUG = False
LAST_CH = None

ENGS = ("sync", "tensor", "vector", "scalar", "gpsimd")


class Chan:
    def __init__(self, stack, nc, name):
        self.sem = stack.enter_context(nc.semaphore(name))
        self.n = 0

    def bump(self, k=1):
        self.n += k
        return self.n


class Phase:

    def __init__(self, nc):
        self.nc = nc
        self.ops = {k: [] for k in ENGS}
        self.mark = {}

    def op(self, eng, fn):
        self.ops[eng].append(fn)

    def wait(self, eng, ch, val):
        if val is None or val <= 0:
            return
        key = (eng, id(ch))
        if self.mark.get(key, -1) >= val:
            return
        self.mark[key] = val
        self.ops[eng].append(lambda e, s=ch.sem, v=val: e.wait_ge(s, v))

    def call(self, eng, meth, *args, inc=None, inck=1, **kw):
        if inc is not None:
            inc.bump(inck)

            def f(e, meth=meth, args=args, kw=kw, sem=inc.sem, k=inck):
                try:
                    getattr(e, meth)(*args, **kw).then_inc(sem, k)
                except Exception:
                    print("FAIL", meth, [getattr(a, "shape", a) for a in args], kw)
                    raise
        else:
            def f(e, meth=meth, args=args, kw=kw):
                try:
                    getattr(e, meth)(*args, **kw)
                except Exception:
                    print("FAIL", meth, [getattr(a, "shape", a) for a in args], kw)
                    raise
        self.ops[eng].append(f)
        return inc.n if inc is not None else None

    def mm(self, out, lhsT, rhs, start, stop, inc=None):
        return self.call("tensor", "matmul", out, lhsT, rhs, start=start,
                         stop=stop, skip_group_check=True, inc=inc)

    def dma(self, eng, ch, out, in_, **kw):
        return self.call(eng, "dma_start", inc=ch, inck=16, out=out, in_=in_, **kw)

    def close(self):
        with self.nc.Block() as blk:
            for name in ENGS:
                fns = self.ops[name]
                if fns:
                    getattr(blk, name)(lambda e, fns=fns: [f(e) for f in fns])


class Guard:

    def __init__(self):
        self.d = {}

    def wait(self, ph, eng, key):
        if key in self.d:
            c, v = self.d[key]
            ph.wait(eng, c, v)

    def set(self, key, ch, val):
        self.d[key] = (ch, val)


def build(debug=False):
    nc = bass.Bass("TRN2", target_bir_lowering=False)
    st = ExitStack()
    scratch_kind = "ExternalOutput" if debug else "Internal"

    xf = nc.dram_tensor("xf", [D, T], F32, kind="ExternalInput").ap()
    wqkvT = nc.dram_tensor("wqkvT", [D, 3 * D], F32, kind="ExternalInput").ap()
    wprojT = nc.dram_tensor("wprojT", [D, D], F32, kind="ExternalInput").ap()
    w1T = nc.dram_tensor("w1T", [D, HID], F32, kind="ExternalInput").ap()
    w2T = nc.dram_tensor("w2T", [HID, D], F32, kind="ExternalInput").ap()
    w3T = nc.dram_tensor("w3T", [D, HID], F32, kind="ExternalInput").ap()
    tmask = nc.dram_tensor("tmask", [128, 32 * 128], F32, kind="ExternalInput").ap()
    hconst = nc.dram_tensor("hconst", [128, 128], BF16, kind="ExternalInput").ap()
    out_d = nc.dram_tensor("out", [D, OWN], F32, kind="ExternalOutput").ap()

    kspill = nc.dram_tensor("kspill", [NH, 128, T], BF16, kind=scratch_kind).ap()
    vspill = nc.dram_tensor("vspill", [NH, 128, T], BF16, kind=scratch_kind).ap()
    qspill = nc.dram_tensor("qspill", [NH, 128, OWN], BF16, kind=scratch_kind).ap()
    aspill = nc.dram_tensor("aspill", [D, OWN], F32, kind=scratch_kind).ap()
    lnrows = nc.dram_tensor("lnrows", [2, OWN], F32, kind=scratch_kind).ap()
    fspill = nc.dram_tensor("fspill", [D, OWN], F32, kind=scratch_kind).ap()
    dbg = {}
    if debug:
        dbg["ln1"] = nc.dram_tensor("dbg_ln1", [D, T], BF16, kind="ExternalOutput").ap()
        dbg["attnout"] = nc.dram_tensor("dbg_attnout", [128, NH * OWN], BF16,
                                        kind="ExternalOutput").ap()
        dbg["pT"] = nc.dram_tensor("dbg_pT", [128, 34816], BF16,
                                   kind="ExternalOutput").ap()

    ch = {}
    for name in ("ldx", "ldw", "ldm", "mm", "v", "sc", "spk", "spv", "spq",
                 "spa", "spf", "out", "lda", "ldf"):
        ch[name] = Chan(st, nc, "ch_" + name)

    ones_f = st.enter_context(nc.sbuf_tensor("ones_f", [128, 1], F32))
    ones_b = st.enter_context(nc.sbuf_tensor("ones_b", [128, 1], BF16))
    onesrow = st.enter_context(nc.sbuf_tensor("onesrow", [1, 128], F32))
    hc_t = st.enter_context(nc.sbuf_tensor("hc_t", [128, 128], BF16))
    bconst = st.enter_context(nc.sbuf_tensor("bconst", [128, 1], F32))

    def ln_rows(ph, ps_sx, ps_sq, mrow, rrow, vrow, trow, n):
        ph.wait("vector", ch["mm"], ch["mm"].n)
        ph.call("vector", "tensor_scalar", mrow, ps_sx, 1.0 / n, None, ALU.mult,
                inc=ch["v"])
        ph.call("vector", "tensor_scalar", vrow, ps_sq, 1.0 / n, EPS, ALU.mult,
                ALU.add, inc=ch["v"])
        ph.call("vector", "tensor_tensor", trow, mrow, mrow, ALU.mult, inc=ch["v"])
        vv = ph.call("vector", "tensor_tensor", vrow, vrow, trow, ALU.subtract,
                     inc=ch["v"])
        ph.wait("scalar", ch["v"], vv)
        ph.call("scalar", "activation", trow, vrow, ACT.Ln, inc=ch["sc"])
        rv = ph.call("scalar", "activation", rrow, trow, ACT.Exp, scale=-0.5,
                     inc=ch["sc"])
        return vv, rv

    ph = Phase(nc)
    for c0 in ch.values():
        ph.call("sync", "sem_clear", c0.sem)
    ph.close()

    ph = Phase(nc)
    ph.call("vector", "memset", ones_f[:], 1.0, inc=ch["v"])
    ph.call("vector", "memset", ones_b[:], 1.0, inc=ch["v"])
    ph.call("vector", "memset", onesrow[:], 1.0, inc=ch["v"])
    ph.call("vector", "memset", bconst[:], -SOFTC, inc=ch["v"])
    ph.dma("sync", ch["ldm"], hc_t[:], hconst)
    ph.wait("sync", ch["ldm"], ch["ldm"].n)
    ph.wait("sync", ch["v"], ch["v"].n)
    ph.close()

    with ExitStack() as sa1:
        xs_t = [sa1.enter_context(nc.sbuf_tensor(f"xs{i}", [128, NDC * 256], F32))
                for i in range(2)]
        ln1 = sa1.enter_context(nc.sbuf_tensor("ln1", [128, NDC * OWN], BF16))
        sqb = [sa1.enter_context(nc.sbuf_tensor(f"sqb{i}", [128, 256], F32))
               for i in range(2)]
        rows = sa1.enter_context(nc.sbuf_tensor("rows", [1, 4 * 256], F32))
        tmp1 = sa1.enter_context(nc.sbuf_tensor("tmp1", [128, 256], F32))
        wq_t = [sa1.enter_context(nc.sbuf_tensor(f"wq{i}", [128, NDC * 128], BF16))
                for i in range(2)]
        wk_t = [sa1.enter_context(nc.sbuf_tensor(f"wk{i}", [128, NDC * 128], BF16))
                for i in range(2)]
        wv_t = [sa1.enter_context(nc.sbuf_tensor(f"wv{i}", [128, NDC * 128], BF16))
                for i in range(2)]
        kst = [sa1.enter_context(nc.sbuf_tensor(f"kst{i}", [128, 2048], BF16))
               for i in range(2)]
        vst = [sa1.enter_context(nc.sbuf_tensor(f"vst{i}", [128, 2048], BF16))
               for i in range(2)]
        qst = [sa1.enter_context(nc.sbuf_tensor(f"qst{i}", [128, 2048], BF16))
               for i in range(2)]
        ps_sx = sa1.enter_context(nc.psum_tensor("ps_sx", [1, 512], F32))
        ps_sq = sa1.enter_context(nc.psum_tensor("ps_sq", [1, 512], F32))
        ps_mb = sa1.enter_context(nc.psum_tensor("ps_mb", [128, 512], F32))
        ps_rb = sa1.enter_context(nc.psum_tensor("ps_rb", [128, 512], F32))
        ps_k = [sa1.enter_context(nc.psum_tensor(f"ps_k{i}", [128, 512], F32))
                for i in range(2)]
        g = Guard()

        for part in range(2):
            ph = Phase(nc)
            for s in range(8):
                buf = s % 2
                col0 = part * OWN + 256 * s
                g.wait(ph, "sync", ("xs", buf))
                ldv = None
                for i in range(4):
                    dcs = slice(4 * i, 4 * i + 4)
                    ldv = ph.dma(
                        "sync", ch["ldx"],
                        xs_t[buf][:].rearrange("p (c t) -> p c t", c=NDC)[:, dcs, :],
                        xf.rearrange("(c p) t -> p c t", p=128)[:, dcs,
                                                               col0:col0 + 256])
                g.wait(ph, "tensor", ("ps_sx",))
                g.wait(ph, "tensor", ("ps_sq",))
                ph.wait("scalar", ch["ldx"], ldv)
                ph.wait("tensor", ch["ldx"], ldv)
                mmq = []
                for dc in range(NDC):
                    xchunk = xs_t[buf][:, 256 * dc:256 * dc + 256]
                    if dc >= 2:
                        ph.wait("scalar", ch["mm"], mmq[dc - 2])
                    sqv = ph.call("scalar", "activation", sqb[dc % 2][:], xchunk,
                                  ACT.Square, inc=ch["sc"])
                    ph.mm(ps_sx[:, 0:256], ones_f[:], xchunk, dc == 0,
                          dc == NDC - 1,
                          inc=(ch["mm"] if dc == NDC - 1 else None))
                    ph.wait("tensor", ch["sc"], sqv)
                    mmq.append(ph.mm(ps_sq[:, 0:256], ones_f[:], sqb[dc % 2][:],
                                     dc == 0, dc == NDC - 1, inc=ch["mm"]))
                mrow, rrow = rows[:, 0:256], rows[:, 256:512]
                vrow, trow = rows[:, 512:768], rows[:, 768:1024]
                vv, rv = ln_rows(ph, ps_sx[:, 0:256], ps_sq[:, 0:256], mrow,
                                 rrow, vrow, trow, D)
                g.set(("ps_sx",), ch["v"], ch["v"].n)
                g.set(("ps_sq",), ch["v"], ch["v"].n)
                g.wait(ph, "tensor", ("ps_mb",))
                g.wait(ph, "tensor", ("ps_rb",))
                ph.wait("tensor", ch["v"], vv)
                ph.wait("tensor", ch["sc"], rv)
                ph.mm(ps_mb[:, 0:256], onesrow[:], mrow, True, True)
                bcv = ph.mm(ps_rb[:, 0:256], onesrow[:], rrow, True, True,
                            inc=ch["mm"])
                ph.wait("vector", ch["mm"], bcv)
                av = None
                for dc in range(NDC):
                    xchunk = xs_t[buf][:, 256 * dc:256 * dc + 256]
                    ph.call("vector", "tensor_tensor", tmp1[:], xchunk,
                            ps_mb[:, 0:256], ALU.subtract)
                    av = ph.call("vector", "tensor_tensor",
                                 ln1[:, OWN * dc + 256 * s:OWN * dc + 256 * s + 256],
                                 tmp1[:], ps_rb[:, 0:256], ALU.mult, inc=ch["v"])
                g.set(("xs", buf), ch["v"], av)
                g.set(("ps_mb",), ch["v"], av)
                g.set(("ps_rb",), ch["v"], av)

            if debug:
                ph.wait("scalar", ch["v"], ch["v"].n)
                dv = ph.dma("scalar", ch["spa"],
                            dbg["ln1"].rearrange("(c p) t -> p c t", p=128)
                            [:, :, part * OWN:(part + 1) * OWN],
                            ln1[:].rearrange("p (c t) -> p c t", c=NDC))
                ph.wait("scalar", ch["spa"], dv)

            ph.wait("tensor", ch["v"], ch["v"].n)
            for h in range(NH):
                buf = h % 2
                for w_t, col in ((wq_t, 128 * h), (wk_t, D + 128 * h),
                                 (wv_t, 2 * D + 128 * h)):
                    g.wait(ph, "gpsimd", ("wbuf", id(w_t), buf))
                    for i in range(4):
                        dcs = slice(4 * i, 4 * i + 4)
                        ldv = ph.dma(
                            "gpsimd", ch["ldw"],
                            w_t[buf][:].rearrange("p (c o) -> p c o", c=NDC)
                            [:, dcs, :],
                            wqkvT.rearrange("(c p) o -> p c o", p=128)
                            [:, dcs, col:col + 128])
                ph.wait("tensor", ch["ldw"], ch["ldw"].n)
                mats = [(wk_t[buf], kst[buf]), (wv_t[buf], vst[buf])]
                if part == 0:
                    mats.append((wq_t[buf], qst[buf]))
                for (w_t_b, stage) in mats:
                    g.wait(ph, "scalar", ("stage", id(stage)))
                    for tw in range(4):
                        pb = tw % 2
                        g.wait(ph, "tensor", ("ps_k", pb))
                        mv = None
                        for dc in range(NDC):
                            mv = ph.mm(
                                ps_k[pb][:], w_t_b[:, 128 * dc:128 * dc + 128],
                                ln1[:, OWN * dc + 512 * tw:OWN * dc + 512 * tw + 512],
                                dc == 0, dc == NDC - 1,
                                inc=(ch["mm"] if dc == NDC - 1 else None))
                        ph.wait("scalar", ch["mm"], mv)
                        cv = ph.call("scalar", "copy",
                                     stage[:, 512 * tw:512 * tw + 512],
                                     ps_k[pb][:], inc=ch["sc"])
                        g.set(("ps_k", pb), ch["sc"], cv)
                    if stage is kst[buf]:
                        spch, dst = ch["spk"], kspill[h, :, part * OWN:(part + 1) * OWN]
                    elif stage is vst[buf]:
                        spch, dst = ch["spv"], vspill[h, :, part * OWN:(part + 1) * OWN]
                    else:
                        spch, dst = ch["spq"], qspill[h, :, :]
                    sv = ph.dma("scalar", spch, dst, stage[:])
                    g.set(("stage", id(stage)), spch, sv)
                for w_t in (wq_t, wk_t, wv_t):
                    g.set(("wbuf", id(w_t), buf), ch["mm"], ch["mm"].n)
            ph.close()

    with ExitStack() as sattn:
        attnout = sattn.enter_context(
            nc.sbuf_tensor("attnout", [128, NH * OWN], BF16))

        off = {}
        acc = 0
        for c in range(32):
            off[c] = acc
            acc += OWN - 128 * (c % 16)
        PTW = acc

        with ExitStack() as sa2:
            ttiles = sa2.enter_context(nc.sbuf_tensor("ttiles", [128, 32 * 128], F32))
            kT_t = [sa2.enter_context(nc.sbuf_tensor(f"kT_t{i}", [128, T], BF16))
                    for i in range(2)]
            vT_t = sa2.enter_context(nc.sbuf_tensor("vT_t", [128, T], BF16))
            vN_t = sa2.enter_context(nc.sbuf_tensor("vN_t", [128, T], BF16))
            qT_t = [sa2.enter_context(nc.sbuf_tensor(f"qT_t{i}", [128, OWN], BF16))
                    for i in range(2)]
            pT = sa2.enter_context(nc.sbuf_tensor("pT", [128, PTW], BF16))
            rs_r = sa2.enter_context(nc.sbuf_tensor("rs_r", [1, 512], F32))
            rsb = sa2.enter_context(nc.sbuf_tensor("rsb", [128, 512], F32))
            ps_sA = sa2.enter_context(nc.psum_tensor("ps_sA", [128, 2048], F32))
            ps_sB = sa2.enter_context(nc.psum_tensor("ps_sB", [128, 1024], F32))
            ps_av = sa2.enter_context(nc.psum_tensor("ps_av", [128, 512], F32))
            ps_sum = sa2.enter_context(nc.psum_tensor("ps_sum", [1, 512], F32))
            g = Guard()

            ph = Phase(nc)
            tmv = None
            for i in range(2):
                cs = slice(16 * i, 16 * i + 16)
                tmv = ph.dma("sync", ch["ldm"],
                             ttiles[:].rearrange("p (c f) -> p c f", c=32)[:, cs, :],
                             tmask.rearrange("p (c f) -> p c f", c=32)[:, cs, :])
            ph.wait("vector", ch["ldm"], tmv)
            ph.wait("sync", ch["spk"], ch["spk"].n)
            ph.wait("sync", ch["spv"], ch["spv"].n)
            ph.wait("sync", ch["spq"], ch["spq"].n)

            def tidx(c):
                return 2 * (c % 16) + (0 if c < 16 else 1)

            for h in range(NH):
                buf = h % 2
                g.wait(ph, "sync", ("kT", buf))
                lk = ph.dma("sync", ch["lda"], kT_t[buf][:], kspill[h])
                g.wait(ph, "sync", ("qT", buf))
                lq = ph.dma("sync", ch["lda"], qT_t[buf][:], qspill[h])
                g.wait(ph, "sync", ("vT",))
                lv = ph.dma("sync", ch["lda"], vT_t[:], vspill[h])
                ph.wait("sync", ch["lda"], lv)
                g.wait(ph, "sync", ("vN",))
                tvv = ph.call("sync", "dma_start_transpose",
                              vN_t[:].rearrange("p (c d) -> p c d", c=32),
                              vT_t[:].rearrange("p (c k) -> p c k", c=32),
                              inc=ch["lda"], inck=16)
                g.set(("vT",), ch["lda"], tvv)
                ph.wait("tensor", ch["lda"], lq)
                g.wait(ph, "scalar", ("pT",))

                pending = []

                def flush(n):
                    for _ in range(min(n, len(pending))):
                        pending.pop(0)()

                def av_units(w, buf=buf):
                    chunks = list(range(0, 4 * w + 4)) +                         list(range(16, 16 + 4 * w + 4))
                    units = []

                    def mkmm(c, first, last, w=w, buf=buf):
                        def f():
                            if first:
                                g.wait(ph, "tensor", ("ps_av",))
                                g.wait(ph, "tensor", ("ps_sum",))
                                ph.wait("tensor", ch["lda"], tvv)
                            qs = 128 * (c % 16)
                            if qs <= 512 * w:
                                p0 = off[c] + 512 * w - qs
                                rhs = pT[:, p0:p0 + 512]
                                o_av, o_sm = ps_av[:, 0:512], ps_sum[:, 0:512]
                            else:
                                wd = 512 * (w + 1) - qs
                                rhs = pT[:, off[c]:off[c] + wd]
                                o_av = ps_av[:, 512 - wd:512]
                                o_sm = ps_sum[:, 512 - wd:512]
                            ph.mm(o_av, vN_t[:, 128 * c:128 * c + 128], rhs,
                                  first, last, inc=(ch["mm"] if last else None))
                            mv = ph.mm(o_sm, ones_b[:], rhs, first, last,
                                       inc=(ch["mm"] if last else None))
                            if last:
                                g.set(("avmm",), ch["mm"], mv)
                        return f

                    for ci, c in enumerate(chunks):
                        units.append(mkmm(c, ci == 0, ci == len(chunks) - 1))

                    def epilogue(w=w, h=h):
                        c2, v2 = g.d[("avmm",)]
                        ph.wait("vector", c2, v2)
                        rv = ph.call("vector", "reciprocal", rs_r[:], ps_sum[:],
                                     inc=ch["v"])
                        g.set(("ps_sum",), ch["v"], rv)
                        g.wait(ph, "tensor", ("ps_sc", 0))
                        ph.wait("tensor", ch["v"], rv)
                        mv = ph.mm(ps_sA[:, 0:512], onesrow[:], rs_r[:], True,
                                   True, inc=ch["mm"])
                        ph.wait("vector", ch["mm"], mv)
                        ph.call("vector", "tensor_copy", rsb[:],
                                ps_sA[:, 0:512], inc=ch["v"])
                        cv = ph.call("vector", "tensor_tensor",
                                     attnout[:, OWN * h + 512 * w:
                                             OWN * h + 512 * w + 512],
                                     rsb[:], ps_av[:], ALU.mult, inc=ch["v"])
                        g.set(("ps_sc", 0), ch["v"], cv)
                        g.set(("ps_av",), ch["v"], cv)
                    units.append(epilogue)
                    return units

                for w in range(4):
                    grp = []
                    for k2 in range(4):
                        grp += [4 * w + k2, 16 + 4 * w + k2]
                    for c in grp:
                        i = c % 16
                        qs = 128 * i
                        if i < 8:
                            ps, base = ps_sA, 0
                        else:
                            ps, base = ps_sB, 1024
                        g.wait(ph, "tensor", ("ps_sc", base))
                        mv = None
                        s0 = qs
                        while s0 < OWN:
                            s1 = min(OWN, 512 * (s0 // 512 + 1))
                            mv = ph.mm(ps[:, s0 - base:s1 - base],
                                       kT_t[buf][:, 128 * c:128 * c + 128],
                                       qT_t[buf][:, s0:s1], True, True,
                                       inc=ch["mm"])
                            s0 = s1
                        ph.wait("vector", ch["mm"], mv)
                        mkv = ph.call(
                            "vector", "tensor_tensor",
                            ps[:, qs - base:qs - base + 128],
                            ps[:, qs - base:qs - base + 128],
                            ttiles[:, tidx(c) * 128:tidx(c) * 128 + 128],
                            ALU.min, inc=ch["v"])
                        ph.wait("scalar", ch["v"], mkv)
                        ev = ph.call("scalar", "activation",
                                     pT[:, off[c]:off[c] + OWN - qs],
                                     ps[:, qs - base:OWN - base], ACT.Exp,
                                     scale=SQD, bias=bconst[:], inc=ch["sc"])
                        g.set(("ps_sc", base), ch["sc"], ev)
                        flush(3)
                    pending += av_units(w)
                flush(len(pending))
                g.set(("pT",), ch["mm"], ch["mm"].n)
                g.set(("kT", buf), ch["mm"], ch["mm"].n)
                g.set(("qT", buf), ch["mm"], ch["mm"].n)
                g.set(("vN",), ch["mm"], ch["mm"].n)
            if debug:
                ph.wait("scalar", ch["v"], ch["v"].n)
                dv = ph.dma("scalar", ch["spa"], dbg["attnout"], attnout[:])
                dv = ph.dma("scalar", ch["spa"], dbg["pT"], pT[:])
                ph.wait("scalar", ch["spa"], dv)
            ph.close()

        with ExitStack() as sa3:
            wp_t = sa3.enter_context(nc.sbuf_tensor("wp_t", [128, NDC * D], BF16))
            xw_t = sa3.enter_context(nc.sbuf_tensor("xw_t", [128, NDC * 256], F32))
            projw = sa3.enter_context(nc.sbuf_tensor("projw", [128, NDC * 256], F32))
            awin = sa3.enter_context(nc.sbuf_tensor("awin", [128, NDC * 256], F32))
            sqb3 = [sa3.enter_context(nc.sbuf_tensor(f"sqb3{i}", [128, 256], F32))
                    for i in range(2)]
            tmp3 = sa3.enter_context(nc.sbuf_tensor("tmp3", [128, 256], F32))
            rows3 = sa3.enter_context(nc.sbuf_tensor("rows3", [1, 4 * 256], F32))
            ps_p = [sa3.enter_context(nc.psum_tensor(f"ps_p{i}", [128, 256], F32))
                    for i in range(2)]
            ps_sx = sa3.enter_context(nc.psum_tensor("ps3_sx", [1, 256], F32))
            ps_sq = sa3.enter_context(nc.psum_tensor("ps3_sq", [1, 256], F32))
            ps_mb = sa3.enter_context(nc.psum_tensor("ps3_mb", [128, 256], F32))
            ps_rb = sa3.enter_context(nc.psum_tensor("ps3_rb", [128, 256], F32))
            ps_3x = sa3.enter_context(nc.psum_tensor("ps3_3x", [1, 256], F32))
            ps_3q = sa3.enter_context(nc.psum_tensor("ps3_3q", [1, 256], F32))
            g = Guard()

            ph = Phase(nc)
            for i in range(4):
                fcs = slice(4 * i, 4 * i + 4)
                lw = ph.dma("gpsimd", ch["ldw"],
                            wp_t[:].rearrange("p (c o) -> p c o", c=NDC)[:, fcs, :],
                            wprojT.rearrange("(c p) o -> p c o", p=128)[:, fcs, :])
            for w in range(8):
                g.wait(ph, "sync", ("xw",))
                lx = None
                for i in range(4):
                    dcs = slice(4 * i, 4 * i + 4)
                    lx = ph.dma("sync", ch["ldx"],
                                xw_t[:].rearrange("p (c t) -> p c t", c=NDC)
                                [:, dcs, :],
                                xf.rearrange("(c p) t -> p c t", p=128)
                                [:, dcs, 256 * w:256 * w + 256])
                ph.wait("tensor", ch["ldw"], lw)
                g.wait(ph, "tensor", ("ps_sx",))
                g.wait(ph, "tensor", ("ps_sq",))
                prevq = None
                for oc in range(NDC):
                    pb = oc % 2
                    g.wait(ph, "tensor", ("ps_p", pb))
                    mv = None
                    for fc in range(NDC):
                        mv = ph.mm(
                            ps_p[pb][:],
                            wp_t[:, D * fc + 128 * oc:D * fc + 128 * oc + 128],
                            attnout[:, OWN * fc + 256 * w:OWN * fc + 256 * w + 256],
                            fc == 0, fc == NDC - 1,
                            inc=(ch["mm"] if fc == NDC - 1 else None))
                    ph.wait("vector", ch["mm"], mv)
                    pv = ph.call("vector", "tensor_copy",
                                 projw[:, 256 * oc:256 * oc + 256], ps_p[pb][:],
                                 inc=ch["v"])
                    g.set(("ps_p", pb), ch["v"], pv)
                    ph.wait("scalar", ch["v"], pv)
                    ph.wait("scalar", ch["mm"], prevq)
                    sv = ph.call("scalar", "activation", sqb3[oc % 2][:],
                                 projw[:, 256 * oc:256 * oc + 256], ACT.Square,
                                 inc=ch["sc"])
                    ph.wait("tensor", ch["v"], pv)
                    ph.mm(ps_sx[:], ones_f[:], projw[:, 256 * oc:256 * oc + 256],
                          oc == 0, oc == NDC - 1,
                          inc=(ch["mm"] if oc == NDC - 1 else None))
                    ph.wait("tensor", ch["sc"], sv)
                    prevq = ph.mm(ps_sq[:], ones_f[:], sqb3[oc % 2][:], oc == 0,
                                  oc == NDC - 1, inc=ch["mm"])
                mrow, rrow = rows3[:, 0:256], rows3[:, 256:512]
                vrow, trow = rows3[:, 512:768], rows3[:, 768:1024]
                vv, rv = ln_rows(ph, ps_sx[:], ps_sq[:], mrow, rrow, vrow, trow, D)
                g.set(("ps_sx",), ch["v"], ch["v"].n)
                g.set(("ps_sq",), ch["v"], ch["v"].n)
                g.wait(ph, "tensor", ("ps_mb",))
                g.wait(ph, "tensor", ("ps_rb",))
                ph.wait("tensor", ch["v"], vv)
                ph.wait("tensor", ch["sc"], rv)
                ph.mm(ps_mb[:], onesrow[:], mrow, True, True)
                bcv = ph.mm(ps_rb[:], onesrow[:], rrow, True, True, inc=ch["mm"])
                ph.wait("vector", ch["mm"], bcv)
                ph.wait("vector", ch["ldx"], lx)
                g.wait(ph, "tensor", ("ps_3x",))
                g.wait(ph, "tensor", ("ps_3q",))
                g.wait(ph, "vector", ("awin",))
                av = None
                prevq = None
                for dc in range(NDC):
                    pchunk = projw[:, 256 * dc:256 * dc + 256]
                    ph.call("vector", "tensor_tensor", tmp3[:], pchunk, ps_mb[:],
                            ALU.subtract)
                    ph.call("vector", "tensor_tensor", tmp3[:], tmp3[:], ps_rb[:],
                            ALU.mult)
                    av = ph.call("vector", "tensor_tensor",
                                 awin[:, 256 * dc:256 * dc + 256], tmp3[:],
                                 xw_t[:, 256 * dc:256 * dc + 256], ALU.add,
                                 inc=ch["v"])
                    ph.wait("scalar", ch["v"], av)
                    ph.wait("scalar", ch["mm"], prevq)
                    sv = ph.call("scalar", "activation", sqb3[dc % 2][:],
                                 awin[:, 256 * dc:256 * dc + 256], ACT.Square,
                                 inc=ch["sc"])
                    ph.wait("tensor", ch["v"], av)
                    ph.mm(ps_3x[:], ones_f[:], awin[:, 256 * dc:256 * dc + 256],
                          dc == 0, dc == NDC - 1,
                          inc=(ch["mm"] if dc == NDC - 1 else None))
                    ph.wait("tensor", ch["sc"], sv)
                    prevq = ph.mm(ps_3q[:], ones_f[:], sqb3[dc % 2][:], dc == 0,
                                  dc == NDC - 1, inc=ch["mm"])
                g.set(("ps_mb",), ch["v"], av)
                g.set(("ps_rb",), ch["v"], av)
                g.set(("xw",), ch["v"], av)
                vv, rv = ln_rows(ph, ps_3x[:], ps_3q[:], mrow, rrow, vrow, trow, D)
                g.set(("ps_3x",), ch["v"], ch["v"].n)
                g.set(("ps_3q",), ch["v"], ch["v"].n)
                ph.wait("scalar", ch["v"], vv)
                ph.dma("scalar", ch["spa"], lnrows[0:1, 256 * w:256 * w + 256],
                       mrow)
                ph.dma("scalar", ch["spa"], lnrows[1:2, 256 * w:256 * w + 256],
                       rrow)
                ph.wait("scalar", ch["v"], av)
                sv = None
                for i in range(4):
                    dcs = slice(4 * i, 4 * i + 4)
                    sv = ph.dma("scalar", ch["spa"],
                                aspill.rearrange("(c p) t -> p c t", p=128)
                                [:, dcs, 256 * w:256 * w + 256],
                                awin[:].rearrange("p (c t) -> p c t", c=NDC)
                                [:, dcs, :])
                g.set(("awin",), ch["spa"], sv)
            ph.close()

    for rt in range(2):
        tc0 = 1024 * rt
        with ExitStack() as sf:
            m4_t = sf.enter_context(nc.sbuf_tensor(f"m4_t{rt}", [1, 1024], F32))
            r4_t = sf.enter_context(nc.sbuf_tensor(f"r4_t{rt}", [1, 1024], F32))
            hd = sf.enter_context(nc.sbuf_tensor(f"hd{rt}", [128, NHC * 1024], BF16))
            s_ln3 = sf.enter_context(ExitStack())
            ln3 = s_ln3.enter_context(nc.sbuf_tensor(f"ln3{rt}", [128, NDC * 1024], BF16))

            with ExitStack() as s1:
                aT_t = s1.enter_context(nc.sbuf_tensor(f"aT_t{rt}", [128, NDC * 512], F32))
                tmpf = s1.enter_context(nc.sbuf_tensor(f"tmpf{rt}", [128, 512], F32))
                lnr = s1.enter_context(nc.sbuf_tensor(f"lnr{rt}", [1, 1024], F32))
                ps_mb = s1.enter_context(nc.psum_tensor(f"psf_mb{rt}", [128, 512], F32))
                ps_rb = s1.enter_context(nc.psum_tensor(f"psf_rb{rt}", [128, 512], F32))
                ph = Phase(nc)
                ph.wait("sync", ch["spa"], ch["spa"].n)
                for seg in range(2):
                    col = tc0 + 512 * seg
                    if seg > 0:
                        ph.wait("sync", ch["v"], ch["v"].n)
                    lr = ph.dma("sync", ch["ldm"], lnr[:, 0:512],
                                lnrows[0:1, col:col + 512])
                    lr = ph.dma("sync", ch["ldm"], lnr[:, 512:1024],
                                lnrows[1:2, col:col + 512])
                    ph.wait("tensor", ch["ldm"], lr)
                    la = None
                    for i in range(4):
                        dcs = slice(4 * i, 4 * i + 4)
                        la = ph.dma("sync", ch["lda"],
                                    aT_t[:].rearrange("p (c t) -> p c t", c=NDC)
                                    [:, dcs, :],
                                    aspill.rearrange("(c p) t -> p c t", p=128)
                                    [:, dcs, col:col + 512])
                    if seg > 0:
                        ph.wait("tensor", ch["v"], ch["v"].n)
                    ph.mm(ps_mb[:], onesrow[:], lnr[:, 0:512], True, True)
                    bcv = ph.mm(ps_rb[:], onesrow[:], lnr[:, 512:1024],
                                True, True, inc=ch["mm"])
                    ph.wait("vector", ch["mm"], bcv)
                    ph.wait("vector", ch["lda"], la)
                    for dc in range(NDC):
                        ph.call("vector", "tensor_tensor", tmpf[:],
                                aT_t[:, 512 * dc:512 * dc + 512], ps_mb[:],
                                ALU.subtract)
                        ph.call("vector", "tensor_tensor",
                                ln3[:, 1024 * dc + 512 * seg:1024 * dc + 512 * seg + 512],
                                tmpf[:], ps_rb[:], ALU.mult, inc=ch["v"])
                ph.close()

            with ExitStack() as s2:
                w1g = [s2.enter_context(nc.sbuf_tensor(f"w1g{rt}_{i}", [128, NDC * 256], BF16))
                       for i in range(2)]
                w3g = [s2.enter_context(nc.sbuf_tensor(f"w3g{rt}_{i}", [128, NDC * 256], BF16))
                       for i in range(2)]
                tmpg = [s2.enter_context(nc.sbuf_tensor(f"tmpg{rt}_{i}", [128, 512], F32))
                        for i in range(2)]
                ps_g = [s2.enter_context(nc.psum_tensor(f"psf_g{rt}_{i}", [128, 512], F32))
                        for i in range(2)]
                ps_u = [s2.enter_context(nc.psum_tensor(f"psf_u{rt}_{i}", [128, 512], F32))
                        for i in range(2)]
                g = Guard()
                ph = Phase(nc)
                nib = 0
                for hg in range(32):
                    buf = hg % 2
                    hcol = 256 * hg
                    g.wait(ph, "gpsimd", ("w1g", buf))
                    lw = None
                    for wsrc, wdst in ((w1T, w1g[buf]), (w3T, w3g[buf])):
                        for i in range(2):
                            dcs = slice(8 * i, 8 * i + 8)
                            lw = ph.dma("gpsimd", ch["ldw"],
                                        wdst[:].rearrange("p (c o) -> p c o", c=NDC)
                                        [:, dcs, :],
                                        wsrc.rearrange("(c p) o -> p c o", p=128)
                                        [:, dcs, hcol:hcol + 256])
                    ph.wait("tensor", ch["ldw"], lw)
                    for hcl in range(2):
                        c = 2 * hg + hcl
                        for hf in range(2):
                            pb = nib % 2
                            nib += 1
                            g.wait(ph, "tensor", ("ps_g", pb))
                            g.wait(ph, "tensor", ("ps_u", pb))
                            uv = None
                            for dc in range(NDC):
                                l1 = w1g[buf][:, 256 * dc + 128 * hcl:
                                              256 * dc + 128 * hcl + 128]
                                l3 = w3g[buf][:, 256 * dc + 128 * hcl:
                                              256 * dc + 128 * hcl + 128]
                                rhs = ln3[:, 1024 * dc + 512 * hf:
                                          1024 * dc + 512 * hf + 512]
                                lastf = (dc == NDC - 1)
                                ph.mm(ps_g[pb][:], l1, rhs, dc == 0, lastf,
                                      inc=(ch["mm"] if lastf else None))
                                uv = ph.mm(ps_u[pb][:], l3, rhs, dc == 0, lastf,
                                           inc=(ch["mm"] if lastf else None))
                            g.wait(ph, "scalar", ("tmpg", pb))
                            ph.wait("scalar", ch["mm"], uv)
                            sv = ph.call("scalar", "activation", tmpg[pb][:],
                                         ps_g[pb][:], ACT.Silu, inc=ch["sc"])
                            g.set(("ps_g", pb), ch["sc"], sv)
                            ph.wait("vector", ch["sc"], sv)
                            mv = ph.call("vector", "tensor_tensor",
                                         hd[:, 1024 * c + 512 * hf:
                                            1024 * c + 512 * hf + 512],
                                         tmpg[pb][:], ps_u[pb][:], ALU.mult,
                                         inc=ch["v"])
                            g.set(("ps_u", pb), ch["v"], mv)
                            g.set(("tmpg", pb), ch["v"], mv)
                    g.set(("w1g", buf), ch["mm"], ch["mm"].n)
                ph.close()
            s_ln3.close()

            with ExitStack() as s3:
                z_t = s3.enter_context(nc.sbuf_tensor(f"z_t{rt}", [128, 64 * 256], F32))
                zs_t = s3.enter_context(nc.sbuf_tensor(f"zs_t{rt}", [128, 4 * 256], F32))
                ps_f = [s3.enter_context(nc.psum_tensor(f"psf_f{rt}_{i}", [128, 256], F32))
                        for i in range(2)]
                g = Guard()
                ph = Phase(nc)
                for ts in range(4):
                    scv = None
                    for c in range(NHC):
                        pb = c % 2
                        g.wait(ph, "tensor", ("ps_f", pb))
                        mv = ph.mm(ps_f[pb][:], hc_t[:],
                                   hd[:, 1024 * c + 256 * ts:1024 * c + 256 * ts + 256],
                                   True, True, inc=ch["mm"])
                        ph.wait("scalar", ch["mm"], mv)
                        scv = ph.call("scalar", "mul",
                                      z_t[:, 256 * c:256 * c + 256], ps_f[pb][:],
                                      RS2, inc=ch["sc"])
                        g.set(("ps_f", pb), ch["sc"], scv)
                    ph.wait("vector", ch["sc"], scv)
                    zv = z_t[:].rearrange("p (c t) -> p c t", c=64)
                    lastv = None
                    for d in (32, 16, 8, 4, 2, 1):
                        ilist = [i2 for i2 in range(64) if (i2 & d) == 0]
                        for g0 in range(0, 32, 4):
                            ib = ilist[g0:g0 + 4]
                            if d >= 4:
                                iap = zv[:, ib[0]:ib[0] + 4, :]
                                jap = zv[:, ib[0] + d:ib[0] + d + 4, :]
                                sap = zs_t[:].rearrange("p (c t) -> p c t", c=4)
                            elif d == 2:
                                zz = z_t[:].rearrange(
                                    "p (g two r t) -> p g two r t", two=2, r=2,
                                    t=256)
                                gg0 = ib[0] // 4
                                iap = zz[:, gg0:gg0 + 2, 0, :, :]
                                jap = zz[:, gg0:gg0 + 2, 1, :, :]
                                sap = zs_t[:].rearrange(
                                    "p (g r t) -> p g r t", g=2, r=2, t=256)
                            else:
                                zz = z_t[:].rearrange(
                                    "p (g two t) -> p g two t", two=2, t=256)
                                gg0 = ib[0] // 2
                                iap = zz[:, gg0:gg0 + 4, 0, :]
                                jap = zz[:, gg0:gg0 + 4, 1, :]
                                sap = zs_t[:].rearrange("p (c t) -> p c t", c=4)
                            ph.call("vector", "tensor_copy", sap, iap)
                            if d > 1:
                                ph.call("vector", "tensor_tensor", iap, sap, jap,
                                        ALU.add)
                                lastv = ph.call("vector", "tensor_tensor", jap,
                                                sap, jap, ALU.subtract,
                                                inc=ch["v"])
                            else:
                                hdz = hd[:].rearrange(
                                    "p (g two t) -> p g two t", two=2, t=1024)
                                hi = hdz[:, gg0:gg0 + 4, 0,
                                         256 * ts:256 * ts + 256]
                                hj = hdz[:, gg0:gg0 + 4, 1,
                                         256 * ts:256 * ts + 256]
                                ph.call("vector", "tensor_tensor", hi, sap, jap,
                                        ALU.add)
                                lastv = ph.call("vector", "tensor_tensor", hj,
                                                sap, jap, ALU.subtract,
                                                inc=ch["v"])
                    ph.wait("tensor", ch["v"], lastv)
                ph.close()

            with ExitStack() as s4:
                w2g = [s4.enter_context(nc.sbuf_tensor(f"w2g{rt}_{i}", [128, NHC * 128], BF16))
                       for i in range(2)]
                fst = [s4.enter_context(nc.sbuf_tensor(f"fst{rt}_{i}", [128, 256], F32))
                       for i in range(2)]
                sqb4 = s4.enter_context(nc.sbuf_tensor(f"sqb4{rt}", [128, 256], F32))
                rows4 = s4.enter_context(nc.sbuf_tensor(f"rows4{rt}", [1, 2 * 256], F32))
                ps_w = [s4.enter_context(nc.psum_tensor(f"psf_w{rt}_{i}", [128, 256], F32))
                        for i in range(2)]
                ps_st = s4.enter_context(nc.psum_tensor(f"psf_st{rt}", [1, 2048], F32))
                g = Guard()
                ph = Phase(nc)
                prevq = None
                for oc in range(NDC):
                    buf = oc % 2
                    g.wait(ph, "gpsimd", ("w2g", buf))
                    lw = None
                    for i in range(8):
                        hcs = slice(8 * i, 8 * i + 8)
                        lw = ph.dma("gpsimd", ch["ldw"],
                                    w2g[buf][:].rearrange("p (c o) -> p c o", c=NHC)
                                    [:, hcs, :],
                                    w2T.rearrange("(c p) o -> p c o", p=128)
                                    [:, hcs, 128 * oc:128 * oc + 128])
                    ph.wait("tensor", ch["ldw"], lw)
                    for ts in range(4):
                        pb = ts % 2
                        g.wait(ph, "tensor", ("ps_w", pb))
                        mv = None
                        for c in range(NHC):
                            mv = ph.mm(ps_w[pb][:],
                                       w2g[buf][:, 128 * c:128 * c + 128],
                                       hd[:, 1024 * c + 256 * ts:
                                          1024 * c + 256 * ts + 256],
                                       c == 0, c == NHC - 1,
                                       inc=(ch["mm"] if c == NHC - 1 else None))
                        ph.wait("vector", ch["mm"], mv)
                        g.wait(ph, "vector", ("fst", pb))
                        g.wait(ph, "vector", ("fstmm", pb))
                        fv = ph.call("vector", "tensor_copy", fst[pb][:],
                                     ps_w[pb][:], inc=ch["v"])
                        g.set(("ps_w", pb), ch["v"], fv)
                        ph.wait("scalar", ch["v"], fv)
                        ph.wait("scalar", ch["mm"], prevq)
                        sv = ph.call("scalar", "activation", sqb4[:], fst[pb][:],
                                     ACT.Square, inc=ch["sc"])
                        ph.wait("tensor", ch["v"], fv)
                        ph.mm(ps_st[:, 512 * ts:512 * ts + 256], ones_f[:],
                              fst[pb][:], oc == 0, oc == NDC - 1, inc=ch["mm"])
                        ph.wait("tensor", ch["sc"], sv)
                        prevq = ph.mm(ps_st[:, 512 * ts + 256:512 * ts + 512],
                                      ones_f[:], sqb4[:], oc == 0, oc == NDC - 1,
                                      inc=ch["mm"])
                        g.set(("fstmm", pb), ch["mm"], prevq)
                        dv = ph.dma("scalar", ch["spf"],
                                    fspill[128 * oc:128 * oc + 128,
                                           tc0 + 256 * ts:tc0 + 256 * ts + 256],
                                    fst[pb][:])
                        g.set(("fst", pb), ch["spf"], dv)
                    g.set(("w2g", buf), ch["mm"], ch["mm"].n)
                for ts in range(4):
                    vrow, trow = rows4[:, 0:256], rows4[:, 256:512]
                    ln_rows(ph, ps_st[:, 512 * ts:512 * ts + 256],
                            ps_st[:, 512 * ts + 256:512 * ts + 512],
                            m4_t[:, 256 * ts:256 * ts + 256],
                            r4_t[:, 256 * ts:256 * ts + 256], vrow, trow, D)
                ph.close()

            with ExitStack() as s5:
                fw_t = s5.enter_context(nc.sbuf_tensor(f"fw_t{rt}", [128, NDC * 256], F32))
                aw_t = s5.enter_context(nc.sbuf_tensor(f"aw_t{rt}", [128, NDC * 256], F32))
                ow_t = s5.enter_context(nc.sbuf_tensor(f"ow_t{rt}", [128, NDC * 256], F32))
                tmpo = s5.enter_context(nc.sbuf_tensor(f"tmpo{rt}", [128, 256], F32))
                ps_mb = s5.enter_context(nc.psum_tensor(f"ps5_mb{rt}", [128, 256], F32))
                ps_rb = s5.enter_context(nc.psum_tensor(f"ps5_rb{rt}", [128, 256], F32))
                ph = Phase(nc)
                ph.wait("sync", ch["spf"], ch["spf"].n)
                for ts in range(4):
                    col = tc0 + 256 * ts
                    la = None
                    for i in range(2):
                        dcs = slice(8 * i, 8 * i + 8)
                        ph.dma("sync", ch["ldf"],
                               fw_t[:].rearrange("p (c t) -> p c t", c=NDC)
                               [:, dcs, :],
                               fspill.rearrange("(c p) t -> p c t", p=128)
                               [:, dcs, col:col + 256])
                        la = ph.dma("sync", ch["ldf"],
                                    aw_t[:].rearrange("p (c t) -> p c t", c=NDC)
                                    [:, dcs, :],
                                    aspill.rearrange("(c p) t -> p c t", p=128)
                                    [:, dcs, col:col + 256])
                    if ts > 0:
                        ph.wait("tensor", ch["v"], ch["v"].n)
                    ph.mm(ps_mb[:], onesrow[:], m4_t[:, 256 * ts:256 * ts + 256],
                          True, True)
                    bcv = ph.mm(ps_rb[:], onesrow[:],
                                r4_t[:, 256 * ts:256 * ts + 256], True, True,
                                inc=ch["mm"])
                    ph.wait("vector", ch["mm"], bcv)
                    ph.wait("vector", ch["ldf"], la)
                    ph.wait("vector", ch["out"], ch["out"].n)
                    ov = None
                    for dc in range(NDC):
                        ph.call("vector", "tensor_tensor", tmpo[:],
                                fw_t[:, 256 * dc:256 * dc + 256], ps_mb[:],
                                ALU.subtract)
                        ph.call("vector", "tensor_tensor", tmpo[:], tmpo[:],
                                ps_rb[:], ALU.mult)
                        ov = ph.call("vector", "tensor_tensor",
                                     ow_t[:, 256 * dc:256 * dc + 256], tmpo[:],
                                     aw_t[:, 256 * dc:256 * dc + 256], ALU.add,
                                     inc=ch["v"])
                    ph.wait("sync", ch["v"], ov)
                    for i in range(2):
                        dcs = slice(8 * i, 8 * i + 8)
                        ph.dma("sync", ch["out"],
                               out_d.rearrange("(c p) t -> p c t", p=128)
                               [:, dcs, col:col + 256],
                               ow_t[:].rearrange("p (c t) -> p c t", c=NDC)
                               [:, dcs, :])
                if rt == 1:
                    ph.wait("sync", ch["out"], ch["out"].n)
                    for nm in ("spk", "spv", "spq", "spa", "spf"):
                        ph.wait("sync", ch[nm], ch[nm].n)
                ph.close()

    st.close()
    global LAST_CH
    LAST_CH = {k: (c.sem.num, c.n) for k, c in ch.items()}
    return nc


_NC_CACHE = {}


def _get_nc(debug):
    key = bool(debug)
    if key not in _NC_CACHE:
        _NC_CACHE[key] = build(debug=key)
    return _NC_CACHE[key]


def _hadamard128():
    idx = np.arange(128)
    anded = idx[:, None] & idx[None, :]
    pc = np.zeros((128, 128), dtype=np.int64)
    for b in range(7):
        pc += (anded >> b) & 1
    return np.where(pc % 2 == 0, 1.0, -1.0).astype(np.float32)


def kernel(L0_attn_x, L0_attn_W_qkv, L0_attn_W_proj, L0_ffn_W1, L0_ffn_W2,
           L0_ffn_W3):
    x = np.ascontiguousarray(np.asarray(L0_attn_x, dtype=np.float32))
    wqkvT = np.ascontiguousarray(np.asarray(L0_attn_W_qkv, np.float32).T)
    wprojT = np.ascontiguousarray(np.asarray(L0_attn_W_proj, np.float32).T)
    w1T = np.ascontiguousarray(np.asarray(L0_ffn_W1, np.float32).T)
    w2T = np.ascontiguousarray(np.asarray(L0_ffn_W2, np.float32).T)
    w3T = np.ascontiguousarray(np.asarray(L0_ffn_W3, np.float32).T)
    hmat = (_hadamard128() * (2.0 ** -6)).astype(ml_dtypes.bfloat16)

    nc = _get_nc(DEBUG)
    in_maps = []
    owns = []
    for c in range(8):
        b, e = c >> 1, c & 1
        blocks = np.arange(T).reshape(16, 2, 128)
        own = blocks[:, e, :].ravel()
        other = blocks[:, 1 - e, :].ravel()
        key_order = np.concatenate([own, other])
        owns.append((b, own))
        xfc = np.ascontiguousarray(x[b].T[:, key_order])
        qg = own.reshape(16, 128).astype(np.float64)
        kg = key_order.reshape(32, 128).astype(np.float64)
        tm = np.empty((128, 32, 128), np.float32)
        for j in range(16):
            for d, c in ((0, j), (1, 16 + j)):
                tm[:, 2 * j + d, :] = (
                    (qg[j][None, :] + 0.5 - kg[c][:, None]) * 1e6)
        in_maps.append({
            "xf": xfc, "wqkvT": wqkvT, "wprojT": wprojT, "w1T": w1T,
            "w2T": w2T, "w3T": w3T, "tmask": tm.reshape(128, 32 * 128),
            "hconst": hmat,
        })

    res = run_bass_kernel_spmd(nc, in_maps, core_ids=list(range(8)))
    out = np.empty((4, T, D), dtype=np.float32)
    for c in range(8):
        b, own = owns[c]
        out[b][own, :] = res.results[c]["out"].T
    if DEBUG:
        kernel.last_results = res.results
    return out


# version 05
# speedup vs baseline: 1.0598x; 1.0435x over previous
import numpy as np
import ml_dtypes
from contextlib import ExitStack

import concourse.bass as bass
import concourse.mybir as mybir
from concourse.bass_utils import run_bass_kernel_spmd

F32 = mybir.dt.float32
BF16 = mybir.dt.bfloat16
ALU = mybir.AluOpType
ACT = mybir.ActivationFunctionType
AX = mybir.AxisListType.X

D = 2048
T = 4096
OWN = 2048
NH = 16
HID = 8192
NDC = D // 128
NHC = HID // 128
SQD = 1.0 / np.sqrt(128.0)
LN128 = float(np.log(128.0))
RS2 = float(2.0 ** -0.5)
SOFTC = 25.0
EPS = 1e-5

DEBUG = False
LAST_CH = None

ENGS = ("sync", "tensor", "vector", "scalar", "gpsimd")


class Chan:
    def __init__(self, stack, nc, name):
        self.sem = stack.enter_context(nc.semaphore(name))
        self.n = 0

    def bump(self, k=1):
        self.n += k
        return self.n


class Phase:

    def __init__(self, nc):
        self.nc = nc
        self.ops = {k: [] for k in ENGS}
        self.mark = {}

    def op(self, eng, fn):
        self.ops[eng].append(fn)

    def wait(self, eng, ch, val):
        if val is None or val <= 0:
            return
        key = (eng, id(ch))
        if self.mark.get(key, -1) >= val:
            return
        self.mark[key] = val
        self.ops[eng].append(lambda e, s=ch.sem, v=val: e.wait_ge(s, v))

    def call(self, eng, meth, *args, inc=None, inck=1, **kw):
        if inc is not None:
            inc.bump(inck)

            def f(e, meth=meth, args=args, kw=kw, sem=inc.sem, k=inck):
                try:
                    getattr(e, meth)(*args, **kw).then_inc(sem, k)
                except Exception:
                    print("FAIL", meth, [getattr(a, "shape", a) for a in args], kw)
                    raise
        else:
            def f(e, meth=meth, args=args, kw=kw):
                try:
                    getattr(e, meth)(*args, **kw)
                except Exception:
                    print("FAIL", meth, [getattr(a, "shape", a) for a in args], kw)
                    raise
        self.ops[eng].append(f)
        return inc.n if inc is not None else None

    def mm(self, out, lhsT, rhs, start, stop, inc=None):
        return self.call("tensor", "matmul", out, lhsT, rhs, start=start,
                         stop=stop, skip_group_check=True, inc=inc)

    def dma(self, eng, ch, out, in_, **kw):
        return self.call(eng, "dma_start", inc=ch, inck=16, out=out, in_=in_, **kw)

    def close(self):
        with self.nc.Block() as blk:
            for name in ENGS:
                fns = self.ops[name]
                if fns:
                    getattr(blk, name)(lambda e, fns=fns: [f(e) for f in fns])


class Guard:

    def __init__(self):
        self.d = {}

    def wait(self, ph, eng, key):
        if key in self.d:
            c, v = self.d[key]
            ph.wait(eng, c, v)

    def set(self, key, ch, val):
        self.d[key] = (ch, val)


def build(debug=False):
    nc = bass.Bass("TRN2", target_bir_lowering=False)
    st = ExitStack()
    scratch_kind = "ExternalOutput" if debug else "Internal"

    xf = nc.dram_tensor("xf", [D, T], F32, kind="ExternalInput").ap()
    wqkvT = nc.dram_tensor("wqkvT", [D, 3 * D], F32, kind="ExternalInput").ap()
    wprojT = nc.dram_tensor("wprojT", [D, D], F32, kind="ExternalInput").ap()
    w1T = nc.dram_tensor("w1T", [D, HID], F32, kind="ExternalInput").ap()
    w2T = nc.dram_tensor("w2T", [HID, D], F32, kind="ExternalInput").ap()
    w3T = nc.dram_tensor("w3T", [D, HID], F32, kind="ExternalInput").ap()
    tmask = nc.dram_tensor("tmask", [128, 32 * 128], BF16, kind="ExternalInput").ap()
    hconst = nc.dram_tensor("hconst", [128, 128], BF16, kind="ExternalInput").ap()
    out_d = nc.dram_tensor("out", [D, OWN], F32, kind="ExternalOutput").ap()

    kspill = nc.dram_tensor("kspill", [NH, 128, T], BF16, kind=scratch_kind).ap()
    vspill = nc.dram_tensor("vspill", [NH, 128, T], BF16, kind=scratch_kind).ap()
    qspill = nc.dram_tensor("qspill", [NH, 128, OWN], BF16, kind=scratch_kind).ap()
    aspill = nc.dram_tensor("aspill", [D, OWN], F32, kind=scratch_kind).ap()
    lnrows = nc.dram_tensor("lnrows", [2, OWN], F32, kind=scratch_kind).ap()
    fspill = nc.dram_tensor("fspill", [D, OWN], F32, kind=scratch_kind).ap()
    dbg = {}
    if debug:
        dbg["ln1"] = nc.dram_tensor("dbg_ln1", [D, T], BF16, kind="ExternalOutput").ap()
        dbg["attnout"] = nc.dram_tensor("dbg_attnout", [128, NH * OWN], BF16,
                                        kind="ExternalOutput").ap()
        dbg["pT"] = nc.dram_tensor("dbg_pT", [128, 34816], BF16,
                                   kind="ExternalOutput").ap()

    ch = {}
    for name in ("ldx", "ldw", "ldm", "mm", "v", "sc", "spk", "spv", "spq",
                 "spa", "spf", "out", "lda", "ldf"):
        ch[name] = Chan(st, nc, "ch_" + name)

    ones_f = st.enter_context(nc.sbuf_tensor("ones_f", [128, 1], F32))
    ones_b = st.enter_context(nc.sbuf_tensor("ones_b", [128, 1], BF16))
    onesrow = st.enter_context(nc.sbuf_tensor("onesrow", [1, 128], F32))
    hc_t = st.enter_context(nc.sbuf_tensor("hc_t", [128, 128], BF16))
    bconst = st.enter_context(nc.sbuf_tensor("bconst", [128, 1], F32))

    def ln_rows(ph, ps_sx, ps_sq, mrow, rrow, vrow, trow, n):
        ph.wait("vector", ch["mm"], ch["mm"].n)
        ph.call("vector", "tensor_scalar", mrow, ps_sx, 1.0 / n, None, ALU.mult,
                inc=ch["v"])
        ph.call("vector", "tensor_scalar", vrow, ps_sq, 1.0 / n, EPS, ALU.mult,
                ALU.add, inc=ch["v"])
        ph.call("vector", "tensor_tensor", trow, mrow, mrow, ALU.mult, inc=ch["v"])
        vv = ph.call("vector", "tensor_tensor", vrow, vrow, trow, ALU.subtract,
                     inc=ch["v"])
        ph.wait("scalar", ch["v"], vv)
        ph.call("scalar", "activation", trow, vrow, ACT.Ln, inc=ch["sc"])
        rv = ph.call("scalar", "activation", rrow, trow, ACT.Exp, scale=-0.5,
                     inc=ch["sc"])
        return vv, rv

    ph = Phase(nc)
    for c0 in ch.values():
        ph.call("sync", "sem_clear", c0.sem)
    ph.close()

    ph = Phase(nc)
    ph.call("vector", "memset", ones_f[:], 1.0, inc=ch["v"])
    ph.call("vector", "memset", ones_b[:], 1.0, inc=ch["v"])
    ph.call("vector", "memset", onesrow[:], 1.0, inc=ch["v"])
    ph.call("vector", "memset", bconst[:], -SOFTC, inc=ch["v"])
    ph.dma("sync", ch["ldm"], hc_t[:], hconst)
    ph.wait("sync", ch["ldm"], ch["ldm"].n)
    ph.wait("sync", ch["v"], ch["v"].n)
    ph.close()

    with ExitStack() as sa1:
        xs_t = [sa1.enter_context(nc.sbuf_tensor(f"xs{i}", [128, NDC * 256], F32))
                for i in range(2)]
        ln1 = sa1.enter_context(nc.sbuf_tensor("ln1", [128, NDC * OWN], BF16))
        sqb = [sa1.enter_context(nc.sbuf_tensor(f"sqb{i}", [128, 256], F32))
               for i in range(2)]
        rows = sa1.enter_context(nc.sbuf_tensor("rows", [1, 4 * 256], F32))
        tmp1 = sa1.enter_context(nc.sbuf_tensor("tmp1", [128, 256], F32))
        wq_t = [sa1.enter_context(nc.sbuf_tensor(f"wq{i}", [128, NDC * 128], BF16))
                for i in range(2)]
        wk_t = [sa1.enter_context(nc.sbuf_tensor(f"wk{i}", [128, NDC * 128], BF16))
                for i in range(2)]
        wv_t = [sa1.enter_context(nc.sbuf_tensor(f"wv{i}", [128, NDC * 128], BF16))
                for i in range(2)]
        kst = [sa1.enter_context(nc.sbuf_tensor(f"kst{i}", [128, 2048], BF16))
               for i in range(2)]
        vst = [sa1.enter_context(nc.sbuf_tensor(f"vst{i}", [128, 2048], BF16))
               for i in range(2)]
        qst = [sa1.enter_context(nc.sbuf_tensor(f"qst{i}", [128, 2048], BF16))
               for i in range(2)]
        ps_sx = sa1.enter_context(nc.psum_tensor("ps_sx", [1, 512], F32))
        ps_sq = sa1.enter_context(nc.psum_tensor("ps_sq", [1, 512], F32))
        ps_mb = sa1.enter_context(nc.psum_tensor("ps_mb", [128, 512], F32))
        ps_rb = sa1.enter_context(nc.psum_tensor("ps_rb", [128, 512], F32))
        ps_k = [sa1.enter_context(nc.psum_tensor(f"ps_k{i}", [128, 512], F32))
                for i in range(2)]
        g = Guard()

        for part in range(2):
            ph = Phase(nc)
            for s in range(8):
                buf = s % 2
                col0 = part * OWN + 256 * s
                g.wait(ph, "sync", ("xs", buf))
                ldv = None
                for i in range(4):
                    dcs = slice(4 * i, 4 * i + 4)
                    ldv = ph.dma(
                        "sync", ch["ldx"],
                        xs_t[buf][:].rearrange("p (c t) -> p c t", c=NDC)[:, dcs, :],
                        xf.rearrange("(c p) t -> p c t", p=128)[:, dcs,
                                                               col0:col0 + 256])
                g.wait(ph, "tensor", ("ps_sx",))
                g.wait(ph, "tensor", ("ps_sq",))
                ph.wait("scalar", ch["ldx"], ldv)
                ph.wait("tensor", ch["ldx"], ldv)
                mmq = []
                for dc in range(NDC):
                    xchunk = xs_t[buf][:, 256 * dc:256 * dc + 256]
                    if dc >= 2:
                        ph.wait("scalar", ch["mm"], mmq[dc - 2])
                    sqv = ph.call("scalar", "activation", sqb[dc % 2][:], xchunk,
                                  ACT.Square, inc=ch["sc"])
                    ph.mm(ps_sx[:, 0:256], ones_f[:], xchunk, dc == 0,
                          dc == NDC - 1,
                          inc=(ch["mm"] if dc == NDC - 1 else None))
                    ph.wait("tensor", ch["sc"], sqv)
                    mmq.append(ph.mm(ps_sq[:, 0:256], ones_f[:], sqb[dc % 2][:],
                                     dc == 0, dc == NDC - 1, inc=ch["mm"]))
                mrow, rrow = rows[:, 0:256], rows[:, 256:512]
                vrow, trow = rows[:, 512:768], rows[:, 768:1024]
                vv, rv = ln_rows(ph, ps_sx[:, 0:256], ps_sq[:, 0:256], mrow,
                                 rrow, vrow, trow, D)
                g.set(("ps_sx",), ch["v"], ch["v"].n)
                g.set(("ps_sq",), ch["v"], ch["v"].n)
                g.wait(ph, "tensor", ("ps_mb",))
                g.wait(ph, "tensor", ("ps_rb",))
                ph.wait("tensor", ch["v"], vv)
                ph.wait("tensor", ch["sc"], rv)
                ph.mm(ps_mb[:, 0:256], onesrow[:], mrow, True, True)
                bcv = ph.mm(ps_rb[:, 0:256], onesrow[:], rrow, True, True,
                            inc=ch["mm"])
                ph.wait("vector", ch["mm"], bcv)
                av = None
                for dc in range(NDC):
                    xchunk = xs_t[buf][:, 256 * dc:256 * dc + 256]
                    ph.call("vector", "tensor_tensor", tmp1[:], xchunk,
                            ps_mb[:, 0:256], ALU.subtract)
                    av = ph.call("vector", "tensor_tensor",
                                 ln1[:, OWN * dc + 256 * s:OWN * dc + 256 * s + 256],
                                 tmp1[:], ps_rb[:, 0:256], ALU.mult, inc=ch["v"])
                g.set(("xs", buf), ch["v"], av)
                g.set(("ps_mb",), ch["v"], av)
                g.set(("ps_rb",), ch["v"], av)

            if debug:
                ph.wait("scalar", ch["v"], ch["v"].n)
                dv = ph.dma("scalar", ch["spa"],
                            dbg["ln1"].rearrange("(c p) t -> p c t", p=128)
                            [:, :, part * OWN:(part + 1) * OWN],
                            ln1[:].rearrange("p (c t) -> p c t", c=NDC))
                ph.wait("scalar", ch["spa"], dv)

            ph.wait("tensor", ch["v"], ch["v"].n)
            for h in range(NH):
                buf = h % 2
                for w_t, col in ((wq_t, 128 * h), (wk_t, D + 128 * h),
                                 (wv_t, 2 * D + 128 * h)):
                    g.wait(ph, "gpsimd", ("wbuf", id(w_t), buf))
                    for i in range(4):
                        dcs = slice(4 * i, 4 * i + 4)
                        ldv = ph.dma(
                            "gpsimd", ch["ldw"],
                            w_t[buf][:].rearrange("p (c o) -> p c o", c=NDC)
                            [:, dcs, :],
                            wqkvT.rearrange("(c p) o -> p c o", p=128)
                            [:, dcs, col:col + 128])
                ph.wait("tensor", ch["ldw"], ch["ldw"].n)
                mats = [(wk_t[buf], kst[buf]), (wv_t[buf], vst[buf])]
                if part == 0:
                    mats.append((wq_t[buf], qst[buf]))
                for (w_t_b, stage) in mats:
                    g.wait(ph, "scalar", ("stage", id(stage)))
                    for tw in range(4):
                        pb = tw % 2
                        g.wait(ph, "tensor", ("ps_k", pb))
                        mv = None
                        for dc in range(NDC):
                            mv = ph.mm(
                                ps_k[pb][:], w_t_b[:, 128 * dc:128 * dc + 128],
                                ln1[:, OWN * dc + 512 * tw:OWN * dc + 512 * tw + 512],
                                dc == 0, dc == NDC - 1,
                                inc=(ch["mm"] if dc == NDC - 1 else None))
                        ph.wait("scalar", ch["mm"], mv)
                        cv = ph.call("scalar", "copy",
                                     stage[:, 512 * tw:512 * tw + 512],
                                     ps_k[pb][:], inc=ch["sc"])
                        g.set(("ps_k", pb), ch["sc"], cv)
                    if stage is kst[buf]:
                        spch, dst = ch["spk"], kspill[h, :, part * OWN:(part + 1) * OWN]
                    elif stage is vst[buf]:
                        spch, dst = ch["spv"], vspill[h, :, part * OWN:(part + 1) * OWN]
                    else:
                        spch, dst = ch["spq"], qspill[h, :, :]
                    sv = ph.dma("scalar", spch, dst, stage[:])
                    g.set(("stage", id(stage)), spch, sv)
                for w_t in (wq_t, wk_t, wv_t):
                    g.set(("wbuf", id(w_t), buf), ch["mm"], ch["mm"].n)
            ph.close()

    with ExitStack() as sattn:
        attnout = sattn.enter_context(
            nc.sbuf_tensor("attnout", [128, NH * OWN], BF16))

        off = {}
        acc = 0
        for c in range(32):
            off[c] = acc
            acc += OWN - 128 * (c % 16)
        PTW = acc

        with ExitStack() as sa2:
            ttiles = sa2.enter_context(nc.sbuf_tensor("ttiles", [128, 32 * 128], BF16))
            kT_t = [sa2.enter_context(nc.sbuf_tensor(f"kT_t{i}", [128, T], BF16))
                    for i in range(2)]
            vT_t = sa2.enter_context(nc.sbuf_tensor("vT_t", [128, T], BF16))
            vN_t = sa2.enter_context(nc.sbuf_tensor("vN_t", [128, T], BF16))
            qT_t = [sa2.enter_context(nc.sbuf_tensor(f"qT_t{i}", [128, OWN], BF16))
                    for i in range(2)]
            pT = sa2.enter_context(nc.sbuf_tensor("pT", [128, PTW], BF16))
            rs_r = sa2.enter_context(nc.sbuf_tensor("rs_r", [1, 512], F32))
            rsb = sa2.enter_context(nc.sbuf_tensor("rsb", [128, 512], F32))
            ps_sc = [sa2.enter_context(nc.psum_tensor(f"ps_sc{i}", [128, 1024], F32))
                     for i in range(3)]
            ps_av = sa2.enter_context(nc.psum_tensor("ps_av", [128, 512], F32))
            ps_sum = sa2.enter_context(nc.psum_tensor("ps_sum", [1, 512], F32))
            g = Guard()

            ph = Phase(nc)
            tmv = None
            for i in range(2):
                cs = slice(16 * i, 16 * i + 16)
                tmv = ph.dma("sync", ch["ldm"],
                             ttiles[:].rearrange("p (c f) -> p c f", c=32)[:, cs, :],
                             tmask.rearrange("p (c f) -> p c f", c=32)[:, cs, :])
            ph.wait("vector", ch["ldm"], tmv)
            ph.wait("sync", ch["spk"], ch["spk"].n)
            ph.wait("sync", ch["spv"], ch["spv"].n)
            ph.wait("sync", ch["spq"], ch["spq"].n)

            def tidx(c):
                return 2 * (c % 16) + (0 if c < 16 else 1)

            for h in range(NH):
                buf = h % 2
                g.wait(ph, "sync", ("kT", buf))
                lk = ph.dma("sync", ch["lda"], kT_t[buf][:], kspill[h])
                g.wait(ph, "sync", ("qT", buf))
                lq = ph.dma("sync", ch["lda"], qT_t[buf][:], qspill[h])
                g.wait(ph, "sync", ("vT",))
                lv = ph.dma("sync", ch["lda"], vT_t[:], vspill[h])
                ph.wait("sync", ch["lda"], lv)
                g.wait(ph, "sync", ("vN",))
                tvv = ph.call("sync", "dma_start_transpose",
                              vN_t[:].rearrange("p (c d) -> p c d", c=32),
                              vT_t[:].rearrange("p (c k) -> p c k", c=32),
                              inc=ch["lda"], inck=16)
                g.set(("vT",), ch["lda"], tvv)
                ph.wait("tensor", ch["lda"], lq)
                g.wait(ph, "scalar", ("pT",))

                pending = []

                def flush(n):
                    for _ in range(min(n, len(pending))):
                        pending.pop(0)()

                def av_units(w, buf=buf):
                    chunks = list(range(0, 4 * w + 4)) +                         list(range(16, 16 + 4 * w + 4))
                    units = []

                    def mkmm(c, first, last, w=w, buf=buf):
                        def f():
                            if first:
                                g.wait(ph, "tensor", ("ps_av",))
                                g.wait(ph, "tensor", ("ps_sum",))
                                ph.wait("tensor", ch["lda"], tvv)
                                ph.wait("tensor", ch["sc"], ch["sc"].n)
                                ph.wait("tensor", ch["v"], ch["v"].n)
                            qs = 128 * (c % 16)
                            if qs <= 512 * w:
                                p0 = off[c] + 512 * w - qs
                                rhs = pT[:, p0:p0 + 512]
                                o_av, o_sm = ps_av[:, 0:512], ps_sum[:, 0:512]
                            else:
                                wd = 512 * (w + 1) - qs
                                rhs = pT[:, off[c]:off[c] + wd]
                                o_av = ps_av[:, 512 - wd:512]
                                o_sm = ps_sum[:, 512 - wd:512]
                            ph.mm(o_av, vN_t[:, 128 * c:128 * c + 128], rhs,
                                  first, last, inc=(ch["mm"] if last else None))
                            mv = ph.mm(o_sm, ones_b[:], rhs, first, last,
                                       inc=(ch["mm"] if last else None))
                            if last:
                                g.set(("avmm",), ch["mm"], mv)
                        return f

                    for ci, c in enumerate(chunks):
                        units.append(mkmm(c, ci == 0, ci == len(chunks) - 1))

                    def epilogue(w=w, h=h):
                        c2, v2 = g.d[("avmm",)]
                        ph.wait("vector", c2, v2)
                        rv = ph.call("vector", "reciprocal", rs_r[:], ps_sum[:],
                                     inc=ch["v"])
                        g.set(("ps_sum",), ch["v"], rv)
                        g.wait(ph, "tensor", ("pstile", 0))
                        ph.wait("tensor", ch["v"], rv)
                        mv = ph.mm(ps_sc[0][:, 0:512], onesrow[:], rs_r[:], True,
                                   True, inc=ch["mm"])
                        ph.wait("vector", ch["mm"], mv)
                        ph.call("vector", "tensor_copy", rsb[:],
                                ps_sc[0][:, 0:512], inc=ch["v"])
                        cv = ph.call("vector", "tensor_tensor",
                                     attnout[:, OWN * h + 512 * w:
                                             OWN * h + 512 * w + 512],
                                     rsb[:], ps_av[:], ALU.mult, inc=ch["v"])
                        g.set(("pstile", 0), ch["v"], cv)
                        g.set(("ps_av",), ch["v"], cv)
                    units.append(epilogue)
                    return units

                ntile = 0
                for w in range(4):
                    grp = []
                    for k2 in range(4):
                        grp += [4 * w + k2, 16 + 4 * w + k2]
                    for c in grp:
                        i = c % 16
                        qs = 128 * i
                        pieces = ([(qs, 1024), (1024, 2048)] if i < 8
                                  else [(qs, 2048)])
                        for (p0, p1) in pieces:
                            base = p1 - 1024
                            ps = ps_sc[ntile % 3]
                            tkey = ("pstile", ntile % 3)
                            ntile += 1
                            g.wait(ph, "tensor", tkey)
                            s0 = p0
                            while s0 < p1:
                                s1 = min(p1, 512 * (s0 // 512 + 1))
                                ph.mm(ps[:, s0 - base:s1 - base],
                                      kT_t[buf][:, 128 * c:128 * c + 128],
                                      qT_t[buf][:, s0:s1], True, True,
                                      inc=(ch["mm"] if s1 == p1 else None))
                                s0 = s1
                            ph.wait("scalar", ch["mm"], ch["mm"].n)
                            ev = ph.call("scalar", "activation",
                                         pT[:, off[c] + p0 - qs:off[c] + p1 - qs],
                                         ps[:, p0 - base:p1 - base], ACT.Exp,
                                         scale=SQD, bias=bconst[:], inc=ch["sc"])
                            g.set(tkey, ch["sc"], ev)
                        ph.wait("vector", ch["sc"], ch["sc"].n)
                        ph.call("vector", "tensor_tensor",
                                pT[:, off[c]:off[c] + 128],
                                pT[:, off[c]:off[c] + 128],
                                ttiles[:, tidx(c) * 128:tidx(c) * 128 + 128],
                                ALU.mult, inc=ch["v"])
                        flush(3)
                    pending += av_units(w)
                flush(len(pending))
                g.set(("pT",), ch["mm"], ch["mm"].n)
                g.set(("kT", buf), ch["mm"], ch["mm"].n)
                g.set(("qT", buf), ch["mm"], ch["mm"].n)
                g.set(("vN",), ch["mm"], ch["mm"].n)
            if debug:
                ph.wait("scalar", ch["v"], ch["v"].n)
                dv = ph.dma("scalar", ch["spa"], dbg["attnout"], attnout[:])
                dv = ph.dma("scalar", ch["spa"], dbg["pT"], pT[:])
                ph.wait("scalar", ch["spa"], dv)
            ph.close()

        with ExitStack() as sa3:
            wp_t = sa3.enter_context(nc.sbuf_tensor("wp_t", [128, NDC * D], BF16))
            xw_t = sa3.enter_context(nc.sbuf_tensor("xw_t", [128, NDC * 256], F32))
            projw = sa3.enter_context(nc.sbuf_tensor("projw", [128, NDC * 256], F32))
            awin = sa3.enter_context(nc.sbuf_tensor("awin", [128, NDC * 256], F32))
            sqb3 = [sa3.enter_context(nc.sbuf_tensor(f"sqb3{i}", [128, 256], F32))
                    for i in range(2)]
            tmp3 = sa3.enter_context(nc.sbuf_tensor("tmp3", [128, 256], F32))
            rows3 = sa3.enter_context(nc.sbuf_tensor("rows3", [1, 4 * 256], F32))
            ps_p = [sa3.enter_context(nc.psum_tensor(f"ps_p{i}", [128, 256], F32))
                    for i in range(2)]
            ps_sx = sa3.enter_context(nc.psum_tensor("ps3_sx", [1, 256], F32))
            ps_sq = sa3.enter_context(nc.psum_tensor("ps3_sq", [1, 256], F32))
            ps_mb = sa3.enter_context(nc.psum_tensor("ps3_mb", [128, 256], F32))
            ps_rb = sa3.enter_context(nc.psum_tensor("ps3_rb", [128, 256], F32))
            ps_3x = sa3.enter_context(nc.psum_tensor("ps3_3x", [1, 256], F32))
            ps_3q = sa3.enter_context(nc.psum_tensor("ps3_3q", [1, 256], F32))
            g = Guard()

            ph = Phase(nc)
            for i in range(4):
                fcs = slice(4 * i, 4 * i + 4)
                lw = ph.dma("gpsimd", ch["ldw"],
                            wp_t[:].rearrange("p (c o) -> p c o", c=NDC)[:, fcs, :],
                            wprojT.rearrange("(c p) o -> p c o", p=128)[:, fcs, :])
            for w in range(8):
                g.wait(ph, "sync", ("xw",))
                lx = None
                for i in range(4):
                    dcs = slice(4 * i, 4 * i + 4)
                    lx = ph.dma("sync", ch["ldx"],
                                xw_t[:].rearrange("p (c t) -> p c t", c=NDC)
                                [:, dcs, :],
                                xf.rearrange("(c p) t -> p c t", p=128)
                                [:, dcs, 256 * w:256 * w + 256])
                ph.wait("tensor", ch["ldw"], lw)
                g.wait(ph, "tensor", ("ps_sx",))
                g.wait(ph, "tensor", ("ps_sq",))
                prevq = None
                for oc in range(NDC):
                    pb = oc % 2
                    g.wait(ph, "tensor", ("ps_p", pb))
                    mv = None
                    for fc in range(NDC):
                        mv = ph.mm(
                            ps_p[pb][:],
                            wp_t[:, D * fc + 128 * oc:D * fc + 128 * oc + 128],
                            attnout[:, OWN * fc + 256 * w:OWN * fc + 256 * w + 256],
                            fc == 0, fc == NDC - 1,
                            inc=(ch["mm"] if fc == NDC - 1 else None))
                    ph.wait("vector", ch["mm"], mv)
                    pv = ph.call("vector", "tensor_copy",
                                 projw[:, 256 * oc:256 * oc + 256], ps_p[pb][:],
                                 inc=ch["v"])
                    g.set(("ps_p", pb), ch["v"], pv)
                    ph.wait("scalar", ch["v"], pv)
                    ph.wait("scalar", ch["mm"], prevq)
                    sv = ph.call("scalar", "activation", sqb3[oc % 2][:],
                                 projw[:, 256 * oc:256 * oc + 256], ACT.Square,
                                 inc=ch["sc"])
                    ph.wait("tensor", ch["v"], pv)
                    ph.mm(ps_sx[:], ones_f[:], projw[:, 256 * oc:256 * oc + 256],
                          oc == 0, oc == NDC - 1,
                          inc=(ch["mm"] if oc == NDC - 1 else None))
                    ph.wait("tensor", ch["sc"], sv)
                    prevq = ph.mm(ps_sq[:], ones_f[:], sqb3[oc % 2][:], oc == 0,
                                  oc == NDC - 1, inc=ch["mm"])
                mrow, rrow = rows3[:, 0:256], rows3[:, 256:512]
                vrow, trow = rows3[:, 512:768], rows3[:, 768:1024]
                vv, rv = ln_rows(ph, ps_sx[:], ps_sq[:], mrow, rrow, vrow, trow, D)
                g.set(("ps_sx",), ch["v"], ch["v"].n)
                g.set(("ps_sq",), ch["v"], ch["v"].n)
                g.wait(ph, "tensor", ("ps_mb",))
                g.wait(ph, "tensor", ("ps_rb",))
                ph.wait("tensor", ch["v"], vv)
                ph.wait("tensor", ch["sc"], rv)
                ph.mm(ps_mb[:], onesrow[:], mrow, True, True)
                bcv = ph.mm(ps_rb[:], onesrow[:], rrow, True, True, inc=ch["mm"])
                ph.wait("vector", ch["mm"], bcv)
                ph.wait("vector", ch["ldx"], lx)
                g.wait(ph, "tensor", ("ps_3x",))
                g.wait(ph, "tensor", ("ps_3q",))
                g.wait(ph, "vector", ("awin",))
                av = None
                prevq = None
                for dc in range(NDC):
                    pchunk = projw[:, 256 * dc:256 * dc + 256]
                    ph.call("vector", "tensor_tensor", tmp3[:], pchunk, ps_mb[:],
                            ALU.subtract)
                    ph.call("vector", "tensor_tensor", tmp3[:], tmp3[:], ps_rb[:],
                            ALU.mult)
                    av = ph.call("vector", "tensor_tensor",
                                 awin[:, 256 * dc:256 * dc + 256], tmp3[:],
                                 xw_t[:, 256 * dc:256 * dc + 256], ALU.add,
                                 inc=ch["v"])
                    ph.wait("scalar", ch["v"], av)
                    ph.wait("scalar", ch["mm"], prevq)
                    sv = ph.call("scalar", "activation", sqb3[dc % 2][:],
                                 awin[:, 256 * dc:256 * dc + 256], ACT.Square,
                                 inc=ch["sc"])
                    ph.wait("tensor", ch["v"], av)
                    ph.mm(ps_3x[:], ones_f[:], awin[:, 256 * dc:256 * dc + 256],
                          dc == 0, dc == NDC - 1,
                          inc=(ch["mm"] if dc == NDC - 1 else None))
                    ph.wait("tensor", ch["sc"], sv)
                    prevq = ph.mm(ps_3q[:], ones_f[:], sqb3[dc % 2][:], dc == 0,
                                  dc == NDC - 1, inc=ch["mm"])
                g.set(("ps_mb",), ch["v"], av)
                g.set(("ps_rb",), ch["v"], av)
                g.set(("xw",), ch["v"], av)
                vv, rv = ln_rows(ph, ps_3x[:], ps_3q[:], mrow, rrow, vrow, trow, D)
                g.set(("ps_3x",), ch["v"], ch["v"].n)
                g.set(("ps_3q",), ch["v"], ch["v"].n)
                ph.wait("scalar", ch["v"], vv)
                ph.dma("scalar", ch["spa"], lnrows[0:1, 256 * w:256 * w + 256],
                       mrow)
                ph.dma("scalar", ch["spa"], lnrows[1:2, 256 * w:256 * w + 256],
                       rrow)
                ph.wait("scalar", ch["v"], av)
                sv = None
                for i in range(4):
                    dcs = slice(4 * i, 4 * i + 4)
                    sv = ph.dma("scalar", ch["spa"],
                                aspill.rearrange("(c p) t -> p c t", p=128)
                                [:, dcs, 256 * w:256 * w + 256],
                                awin[:].rearrange("p (c t) -> p c t", c=NDC)
                                [:, dcs, :])
                g.set(("awin",), ch["spa"], sv)
            ph.close()

    for rt in range(2):
        tc0 = 1024 * rt
        with ExitStack() as sf:
            m4_t = sf.enter_context(nc.sbuf_tensor(f"m4_t{rt}", [1, 1024], F32))
            r4_t = sf.enter_context(nc.sbuf_tensor(f"r4_t{rt}", [1, 1024], F32))
            hd = sf.enter_context(nc.sbuf_tensor(f"hd{rt}", [128, NHC * 1024], BF16))
            s_ln3 = sf.enter_context(ExitStack())
            ln3 = s_ln3.enter_context(nc.sbuf_tensor(f"ln3{rt}", [128, NDC * 1024], BF16))

            with ExitStack() as s1:
                aT_t = s1.enter_context(nc.sbuf_tensor(f"aT_t{rt}", [128, NDC * 512], F32))
                tmpf = s1.enter_context(nc.sbuf_tensor(f"tmpf{rt}", [128, 512], F32))
                lnr = s1.enter_context(nc.sbuf_tensor(f"lnr{rt}", [1, 1024], F32))
                ps_mb = s1.enter_context(nc.psum_tensor(f"psf_mb{rt}", [128, 512], F32))
                ps_rb = s1.enter_context(nc.psum_tensor(f"psf_rb{rt}", [128, 512], F32))
                ph = Phase(nc)
                ph.wait("sync", ch["spa"], ch["spa"].n)
                for seg in range(2):
                    col = tc0 + 512 * seg
                    if seg > 0:
                        ph.wait("sync", ch["v"], ch["v"].n)
                    lr = ph.dma("sync", ch["ldm"], lnr[:, 0:512],
                                lnrows[0:1, col:col + 512])
                    lr = ph.dma("sync", ch["ldm"], lnr[:, 512:1024],
                                lnrows[1:2, col:col + 512])
                    ph.wait("tensor", ch["ldm"], lr)
                    la = None
                    for i in range(4):
                        dcs = slice(4 * i, 4 * i + 4)
                        la = ph.dma("sync", ch["lda"],
                                    aT_t[:].rearrange("p (c t) -> p c t", c=NDC)
                                    [:, dcs, :],
                                    aspill.rearrange("(c p) t -> p c t", p=128)
                                    [:, dcs, col:col + 512])
                    if seg > 0:
                        ph.wait("tensor", ch["v"], ch["v"].n)
                    ph.mm(ps_mb[:], onesrow[:], lnr[:, 0:512], True, True)
                    bcv = ph.mm(ps_rb[:], onesrow[:], lnr[:, 512:1024],
                                True, True, inc=ch["mm"])
                    ph.wait("vector", ch["mm"], bcv)
                    ph.wait("vector", ch["lda"], la)
                    for dc in range(NDC):
                        ph.call("vector", "tensor_tensor", tmpf[:],
                                aT_t[:, 512 * dc:512 * dc + 512], ps_mb[:],
                                ALU.subtract)
                        ph.call("vector", "tensor_tensor",
                                ln3[:, 1024 * dc + 512 * seg:1024 * dc + 512 * seg + 512],
                                tmpf[:], ps_rb[:], ALU.mult, inc=ch["v"])
                ph.close()

            with ExitStack() as s2:
                w1g = [s2.enter_context(nc.sbuf_tensor(f"w1g{rt}_{i}", [128, NDC * 256], BF16))
                       for i in range(2)]
                w3g = [s2.enter_context(nc.sbuf_tensor(f"w3g{rt}_{i}", [128, NDC * 256], BF16))
                       for i in range(2)]
                tmpg = [s2.enter_context(nc.sbuf_tensor(f"tmpg{rt}_{i}", [128, 512], F32))
                        for i in range(2)]
                ps_g = [s2.enter_context(nc.psum_tensor(f"psf_g{rt}_{i}", [128, 512], F32))
                        for i in range(2)]
                ps_u = [s2.enter_context(nc.psum_tensor(f"psf_u{rt}_{i}", [128, 512], F32))
                        for i in range(2)]
                g = Guard()
                ph = Phase(nc)
                nib = 0
                for hg in range(32):
                    buf = hg % 2
                    hcol = 256 * hg
                    g.wait(ph, "gpsimd", ("w1g", buf))
                    lw = None
                    for wsrc, wdst in ((w1T, w1g[buf]), (w3T, w3g[buf])):
                        for i in range(2):
                            dcs = slice(8 * i, 8 * i + 8)
                            lw = ph.dma("gpsimd", ch["ldw"],
                                        wdst[:].rearrange("p (c o) -> p c o", c=NDC)
                                        [:, dcs, :],
                                        wsrc.rearrange("(c p) o -> p c o", p=128)
                                        [:, dcs, hcol:hcol + 256])
                    ph.wait("tensor", ch["ldw"], lw)
                    for hcl in range(2):
                        c = 2 * hg + hcl
                        for hf in range(2):
                            pb = nib % 2
                            nib += 1
                            g.wait(ph, "tensor", ("ps_g", pb))
                            g.wait(ph, "tensor", ("ps_u", pb))
                            uv = None
                            for dc in range(NDC):
                                l1 = w1g[buf][:, 256 * dc + 128 * hcl:
                                              256 * dc + 128 * hcl + 128]
                                l3 = w3g[buf][:, 256 * dc + 128 * hcl:
                                              256 * dc + 128 * hcl + 128]
                                rhs = ln3[:, 1024 * dc + 512 * hf:
                                          1024 * dc + 512 * hf + 512]
                                lastf = (dc == NDC - 1)
                                ph.mm(ps_g[pb][:], l1, rhs, dc == 0, lastf,
                                      inc=(ch["mm"] if lastf else None))
                                uv = ph.mm(ps_u[pb][:], l3, rhs, dc == 0, lastf,
                                           inc=(ch["mm"] if lastf else None))
                            g.wait(ph, "scalar", ("tmpg", pb))
                            ph.wait("scalar", ch["mm"], uv)
                            sv = ph.call("scalar", "activation", tmpg[pb][:],
                                         ps_g[pb][:], ACT.Silu, inc=ch["sc"])
                            g.set(("ps_g", pb), ch["sc"], sv)
                            ph.wait("vector", ch["sc"], sv)
                            mv = ph.call("vector", "tensor_tensor",
                                         hd[:, 1024 * c + 512 * hf:
                                            1024 * c + 512 * hf + 512],
                                         tmpg[pb][:], ps_u[pb][:], ALU.mult,
                                         inc=ch["v"])
                            g.set(("ps_u", pb), ch["v"], mv)
                            g.set(("tmpg", pb), ch["v"], mv)
                    g.set(("w1g", buf), ch["mm"], ch["mm"].n)
                ph.close()
            s_ln3.close()

            with ExitStack() as s3:
                z_t = s3.enter_context(nc.sbuf_tensor(f"z_t{rt}", [128, 64 * 256], F32))
                zs_t = s3.enter_context(nc.sbuf_tensor(f"zs_t{rt}", [128, 4 * 256], F32))
                ps_f = [s3.enter_context(nc.psum_tensor(f"psf_f{rt}_{i}", [128, 256], F32))
                        for i in range(2)]
                g = Guard()
                ph = Phase(nc)
                for ts in range(4):
                    scv = None
                    for c in range(NHC):
                        pb = c % 2
                        g.wait(ph, "tensor", ("ps_f", pb))
                        mv = ph.mm(ps_f[pb][:], hc_t[:],
                                   hd[:, 1024 * c + 256 * ts:1024 * c + 256 * ts + 256],
                                   True, True, inc=ch["mm"])
                        ph.wait("scalar", ch["mm"], mv)
                        scv = ph.call("scalar", "mul",
                                      z_t[:, 256 * c:256 * c + 256], ps_f[pb][:],
                                      RS2, inc=ch["sc"])
                        g.set(("ps_f", pb), ch["sc"], scv)
                    ph.wait("vector", ch["sc"], scv)
                    zv = z_t[:].rearrange("p (c t) -> p c t", c=64)
                    lastv = None
                    for d in (32, 16, 8, 4, 2, 1):
                        ilist = [i2 for i2 in range(64) if (i2 & d) == 0]
                        for g0 in range(0, 32, 4):
                            ib = ilist[g0:g0 + 4]
                            if d >= 4:
                                iap = zv[:, ib[0]:ib[0] + 4, :]
                                jap = zv[:, ib[0] + d:ib[0] + d + 4, :]
                                sap = zs_t[:].rearrange("p (c t) -> p c t", c=4)
                            elif d == 2:
                                zz = z_t[:].rearrange(
                                    "p (g two r t) -> p g two r t", two=2, r=2,
                                    t=256)
                                gg0 = ib[0] // 4
                                iap = zz[:, gg0:gg0 + 2, 0, :, :]
                                jap = zz[:, gg0:gg0 + 2, 1, :, :]
                                sap = zs_t[:].rearrange(
                                    "p (g r t) -> p g r t", g=2, r=2, t=256)
                            else:
                                zz = z_t[:].rearrange(
                                    "p (g two t) -> p g two t", two=2, t=256)
                                gg0 = ib[0] // 2
                                iap = zz[:, gg0:gg0 + 4, 0, :]
                                jap = zz[:, gg0:gg0 + 4, 1, :]
                                sap = zs_t[:].rearrange("p (c t) -> p c t", c=4)
                            ph.call("vector", "tensor_copy", sap, iap)
                            if d > 1:
                                ph.call("vector", "tensor_tensor", iap, sap, jap,
                                        ALU.add)
                                lastv = ph.call("vector", "tensor_tensor", jap,
                                                sap, jap, ALU.subtract,
                                                inc=ch["v"])
                            else:
                                hdz = hd[:].rearrange(
                                    "p (g two t) -> p g two t", two=2, t=1024)
                                hi = hdz[:, gg0:gg0 + 4, 0,
                                         256 * ts:256 * ts + 256]
                                hj = hdz[:, gg0:gg0 + 4, 1,
                                         256 * ts:256 * ts + 256]
                                ph.call("vector", "tensor_tensor", hi, sap, jap,
                                        ALU.add)
                                lastv = ph.call("vector", "tensor_tensor", hj,
                                                sap, jap, ALU.subtract,
                                                inc=ch["v"])
                    ph.wait("tensor", ch["v"], lastv)
                ph.close()

            with ExitStack() as s4:
                w2g = [s4.enter_context(nc.sbuf_tensor(f"w2g{rt}_{i}", [128, NHC * 128], BF16))
                       for i in range(2)]
                fst = [s4.enter_context(nc.sbuf_tensor(f"fst{rt}_{i}", [128, 256], F32))
                       for i in range(2)]
                sqb4 = s4.enter_context(nc.sbuf_tensor(f"sqb4{rt}", [128, 256], F32))
                rows4 = s4.enter_context(nc.sbuf_tensor(f"rows4{rt}", [1, 2 * 256], F32))
                ps_w = [s4.enter_context(nc.psum_tensor(f"psf_w{rt}_{i}", [128, 256], F32))
                        for i in range(2)]
                ps_st = s4.enter_context(nc.psum_tensor(f"psf_st{rt}", [1, 2048], F32))
                g = Guard()
                ph = Phase(nc)
                prevq = None
                for oc in range(NDC):
                    buf = oc % 2
                    g.wait(ph, "gpsimd", ("w2g", buf))
                    lw = None
                    for i in range(8):
                        hcs = slice(8 * i, 8 * i + 8)
                        lw = ph.dma("gpsimd", ch["ldw"],
                                    w2g[buf][:].rearrange("p (c o) -> p c o", c=NHC)
                                    [:, hcs, :],
                                    w2T.rearrange("(c p) o -> p c o", p=128)
                                    [:, hcs, 128 * oc:128 * oc + 128])
                    ph.wait("tensor", ch["ldw"], lw)
                    for ts in range(4):
                        pb = ts % 2
                        g.wait(ph, "tensor", ("ps_w", pb))
                        mv = None
                        for c in range(NHC):
                            mv = ph.mm(ps_w[pb][:],
                                       w2g[buf][:, 128 * c:128 * c + 128],
                                       hd[:, 1024 * c + 256 * ts:
                                          1024 * c + 256 * ts + 256],
                                       c == 0, c == NHC - 1,
                                       inc=(ch["mm"] if c == NHC - 1 else None))
                        ph.wait("vector", ch["mm"], mv)
                        g.wait(ph, "vector", ("fst", pb))
                        g.wait(ph, "vector", ("fstmm", pb))
                        fv = ph.call("vector", "tensor_copy", fst[pb][:],
                                     ps_w[pb][:], inc=ch["v"])
                        g.set(("ps_w", pb), ch["v"], fv)
                        ph.wait("scalar", ch["v"], fv)
                        ph.wait("scalar", ch["mm"], prevq)
                        sv = ph.call("scalar", "activation", sqb4[:], fst[pb][:],
                                     ACT.Square, inc=ch["sc"])
                        ph.wait("tensor", ch["v"], fv)
                        ph.mm(ps_st[:, 512 * ts:512 * ts + 256], ones_f[:],
                              fst[pb][:], oc == 0, oc == NDC - 1, inc=ch["mm"])
                        ph.wait("tensor", ch["sc"], sv)
                        prevq = ph.mm(ps_st[:, 512 * ts + 256:512 * ts + 512],
                                      ones_f[:], sqb4[:], oc == 0, oc == NDC - 1,
                                      inc=ch["mm"])
                        g.set(("fstmm", pb), ch["mm"], prevq)
                        dv = ph.dma("scalar", ch["spf"],
                                    fspill[128 * oc:128 * oc + 128,
                                           tc0 + 256 * ts:tc0 + 256 * ts + 256],
                                    fst[pb][:])
                        g.set(("fst", pb), ch["spf"], dv)
                    g.set(("w2g", buf), ch["mm"], ch["mm"].n)
                for ts in range(4):
                    vrow, trow = rows4[:, 0:256], rows4[:, 256:512]
                    ln_rows(ph, ps_st[:, 512 * ts:512 * ts + 256],
                            ps_st[:, 512 * ts + 256:512 * ts + 512],
                            m4_t[:, 256 * ts:256 * ts + 256],
                            r4_t[:, 256 * ts:256 * ts + 256], vrow, trow, D)
                ph.close()

            with ExitStack() as s5:
                fw_t = s5.enter_context(nc.sbuf_tensor(f"fw_t{rt}", [128, NDC * 256], F32))
                aw_t = s5.enter_context(nc.sbuf_tensor(f"aw_t{rt}", [128, NDC * 256], F32))
                ow_t = s5.enter_context(nc.sbuf_tensor(f"ow_t{rt}", [128, NDC * 256], F32))
                tmpo = s5.enter_context(nc.sbuf_tensor(f"tmpo{rt}", [128, 256], F32))
                ps_mb = s5.enter_context(nc.psum_tensor(f"ps5_mb{rt}", [128, 256], F32))
                ps_rb = s5.enter_context(nc.psum_tensor(f"ps5_rb{rt}", [128, 256], F32))
                ph = Phase(nc)
                ph.wait("sync", ch["spf"], ch["spf"].n)
                for ts in range(4):
                    col = tc0 + 256 * ts
                    la = None
                    for i in range(2):
                        dcs = slice(8 * i, 8 * i + 8)
                        ph.dma("sync", ch["ldf"],
                               fw_t[:].rearrange("p (c t) -> p c t", c=NDC)
                               [:, dcs, :],
                               fspill.rearrange("(c p) t -> p c t", p=128)
                               [:, dcs, col:col + 256])
                        la = ph.dma("sync", ch["ldf"],
                                    aw_t[:].rearrange("p (c t) -> p c t", c=NDC)
                                    [:, dcs, :],
                                    aspill.rearrange("(c p) t -> p c t", p=128)
                                    [:, dcs, col:col + 256])
                    if ts > 0:
                        ph.wait("tensor", ch["v"], ch["v"].n)
                    ph.mm(ps_mb[:], onesrow[:], m4_t[:, 256 * ts:256 * ts + 256],
                          True, True)
                    bcv = ph.mm(ps_rb[:], onesrow[:],
                                r4_t[:, 256 * ts:256 * ts + 256], True, True,
                                inc=ch["mm"])
                    ph.wait("vector", ch["mm"], bcv)
                    ph.wait("vector", ch["ldf"], la)
                    ph.wait("vector", ch["out"], ch["out"].n)
                    ov = None
                    for dc in range(NDC):
                        ph.call("vector", "tensor_tensor", tmpo[:],
                                fw_t[:, 256 * dc:256 * dc + 256], ps_mb[:],
                                ALU.subtract)
                        ph.call("vector", "tensor_tensor", tmpo[:], tmpo[:],
                                ps_rb[:], ALU.mult)
                        ov = ph.call("vector", "tensor_tensor",
                                     ow_t[:, 256 * dc:256 * dc + 256], tmpo[:],
                                     aw_t[:, 256 * dc:256 * dc + 256], ALU.add,
                                     inc=ch["v"])
                    ph.wait("sync", ch["v"], ov)
                    for i in range(2):
                        dcs = slice(8 * i, 8 * i + 8)
                        ph.dma("sync", ch["out"],
                               out_d.rearrange("(c p) t -> p c t", p=128)
                               [:, dcs, col:col + 256],
                               ow_t[:].rearrange("p (c t) -> p c t", c=NDC)
                               [:, dcs, :])
                if rt == 1:
                    ph.wait("sync", ch["out"], ch["out"].n)
                    for nm in ("spk", "spv", "spq", "spa", "spf"):
                        ph.wait("sync", ch[nm], ch[nm].n)
                ph.close()

    st.close()
    global LAST_CH
    LAST_CH = {k: (c.sem.num, c.n) for k, c in ch.items()}
    return nc


_NC_CACHE = {}


def _get_nc(debug):
    key = bool(debug)
    if key not in _NC_CACHE:
        _NC_CACHE[key] = build(debug=key)
    return _NC_CACHE[key]


def _hadamard128():
    idx = np.arange(128)
    anded = idx[:, None] & idx[None, :]
    pc = np.zeros((128, 128), dtype=np.int64)
    for b in range(7):
        pc += (anded >> b) & 1
    return np.where(pc % 2 == 0, 1.0, -1.0).astype(np.float32)


def kernel(L0_attn_x, L0_attn_W_qkv, L0_attn_W_proj, L0_ffn_W1, L0_ffn_W2,
           L0_ffn_W3):
    x = np.ascontiguousarray(np.asarray(L0_attn_x, dtype=np.float32))
    wqkvT = np.ascontiguousarray(np.asarray(L0_attn_W_qkv, np.float32).T)
    wprojT = np.ascontiguousarray(np.asarray(L0_attn_W_proj, np.float32).T)
    w1T = np.ascontiguousarray(np.asarray(L0_ffn_W1, np.float32).T)
    w2T = np.ascontiguousarray(np.asarray(L0_ffn_W2, np.float32).T)
    w3T = np.ascontiguousarray(np.asarray(L0_ffn_W3, np.float32).T)
    hmat = (_hadamard128() * (2.0 ** -6)).astype(ml_dtypes.bfloat16)

    nc = _get_nc(DEBUG)
    in_maps = []
    owns = []
    for c in range(8):
        b, e = c >> 1, c & 1
        blocks = np.arange(T).reshape(16, 2, 128)
        own = blocks[:, e, :].ravel()
        other = blocks[:, 1 - e, :].ravel()
        key_order = np.concatenate([own, other])
        owns.append((b, own))
        xfc = np.ascontiguousarray(x[b].T[:, key_order])
        qg = own.reshape(16, 128).astype(np.float64)
        kg = key_order.reshape(32, 128).astype(np.float64)
        tm = np.empty((128, 32, 128), np.float32)
        for j in range(16):
            for d, c in ((0, j), (1, 16 + j)):
                tm[:, 2 * j + d, :] = (
                    kg[c][:, None] <= qg[j][None, :]).astype(np.float32)
        in_maps.append({
            "xf": xfc, "wqkvT": wqkvT, "wprojT": wprojT, "w1T": w1T,
            "w2T": w2T, "w3T": w3T,
            "tmask": tm.reshape(128, 32 * 128).astype(ml_dtypes.bfloat16),
            "hconst": hmat,
        })

    res = run_bass_kernel_spmd(nc, in_maps, core_ids=list(range(8)))
    out = np.empty((4, T, D), dtype=np.float32)
    for c in range(8):
        b, own = owns[c]
        out[b][own, :] = res.results[c]["out"].T
    if DEBUG:
        kernel.last_results = res.results
    return out


# version 10
# speedup vs baseline: 1.1271x; 1.0635x over previous
import numpy as np
import ml_dtypes
from contextlib import ExitStack

import concourse.bass as bass
import concourse.mybir as mybir
from concourse.bass_utils import run_bass_kernel_spmd

F32 = mybir.dt.float32
BF16 = mybir.dt.bfloat16
ALU = mybir.AluOpType
ACT = mybir.ActivationFunctionType
AX = mybir.AxisListType.X

D = 2048
T = 4096
OWN = 2048
NH = 16
HID = 8192
NDC = D // 128
NHC = HID // 128
SQD = 1.0 / np.sqrt(128.0)
LN128 = float(np.log(128.0))
RS2 = float(2.0 ** -0.5)
SOFTC = 25.0
EPS = 1e-5

DEBUG = False
LAST_CH = None

ENGS = ("sync", "tensor", "vector", "scalar", "gpsimd")


class Chan:
    def __init__(self, stack, nc, name):
        self.sem = stack.enter_context(nc.semaphore(name))
        self.n = 0

    def bump(self, k=1):
        self.n += k
        return self.n


class Phase:

    def __init__(self, nc):
        self.nc = nc
        self.ops = {k: [] for k in ENGS}
        self.mark = {}

    def op(self, eng, fn):
        self.ops[eng].append(fn)

    def wait(self, eng, ch, val):
        if val is None or val <= 0:
            return
        key = (eng, id(ch))
        if self.mark.get(key, -1) >= val:
            return
        self.mark[key] = val
        self.ops[eng].append(lambda e, s=ch.sem, v=val: e.wait_ge(s, v))

    def call(self, eng, meth, *args, inc=None, inck=1, **kw):
        if inc is not None:
            inc.bump(inck)

            def f(e, meth=meth, args=args, kw=kw, sem=inc.sem, k=inck):
                try:
                    getattr(e, meth)(*args, **kw).then_inc(sem, k)
                except Exception:
                    print("FAIL", meth, [getattr(a, "shape", a) for a in args], kw)
                    raise
        else:
            def f(e, meth=meth, args=args, kw=kw):
                try:
                    getattr(e, meth)(*args, **kw)
                except Exception:
                    print("FAIL", meth, [getattr(a, "shape", a) for a in args], kw)
                    raise
        self.ops[eng].append(f)
        return inc.n if inc is not None else None

    def mm(self, out, lhsT, rhs, start, stop, inc=None):
        return self.call("tensor", "matmul", out, lhsT, rhs, start=start,
                         stop=stop, skip_group_check=True, inc=inc)

    def dma(self, eng, ch, out, in_, **kw):
        return self.call(eng, "dma_start", inc=ch, inck=16, out=out, in_=in_, **kw)

    def close(self):
        with self.nc.Block() as blk:
            for name in ENGS:
                fns = self.ops[name]
                if fns:
                    getattr(blk, name)(lambda e, fns=fns: [f(e) for f in fns])


class Guard:

    def __init__(self):
        self.d = {}

    def wait(self, ph, eng, key):
        if key in self.d:
            c, v = self.d[key]
            ph.wait(eng, c, v)

    def set(self, key, ch, val):
        self.d[key] = (ch, val)


def build(debug=False):
    nc = bass.Bass("TRN2", target_bir_lowering=False)
    st = ExitStack()
    scratch_kind = "ExternalOutput" if debug else "Internal"

    xf = nc.dram_tensor("xf", [D, T], F32, kind="ExternalInput").ap()
    wqkvT = nc.dram_tensor("wqkvT", [D, 3 * D], F32, kind="ExternalInput").ap()
    wprojT = nc.dram_tensor("wprojT", [D, D], F32, kind="ExternalInput").ap()
    w1T = nc.dram_tensor("w1T", [D, HID], F32, kind="ExternalInput").ap()
    w2T = nc.dram_tensor("w2T", [HID, D], F32, kind="ExternalInput").ap()
    w3T = nc.dram_tensor("w3T", [D, HID], F32, kind="ExternalInput").ap()
    tmask = nc.dram_tensor("tmask", [128, 32 * 128], BF16, kind="ExternalInput").ap()
    hconst = nc.dram_tensor("hconst", [128, 128], BF16, kind="ExternalInput").ap()
    out_d = nc.dram_tensor("out", [D, OWN], F32, kind="ExternalOutput").ap()

    kspill = nc.dram_tensor("kspill", [NH, 128, T], BF16, kind=scratch_kind).ap()
    vspill = nc.dram_tensor("vspill", [NH, 128, T], BF16, kind=scratch_kind).ap()
    qspill = nc.dram_tensor("qspill", [NH, 128, OWN], BF16, kind=scratch_kind).ap()
    aspill = nc.dram_tensor("aspill", [D, OWN], F32, kind=scratch_kind).ap()
    lnrows = nc.dram_tensor("lnrows", [2, OWN], F32, kind=scratch_kind).ap()
    fspill = nc.dram_tensor("fspill", [D, OWN], F32, kind=scratch_kind).ap()
    dbg = {}
    if debug:
        dbg["ln1"] = nc.dram_tensor("dbg_ln1", [D, T], BF16, kind="ExternalOutput").ap()
        dbg["attnout"] = nc.dram_tensor("dbg_attnout", [128, NH * OWN], BF16,
                                        kind="ExternalOutput").ap()
        dbg["pT"] = nc.dram_tensor("dbg_pT", [128, 34816], BF16,
                                   kind="ExternalOutput").ap()

    ch = {}
    for name in ("ldx", "ldw", "ldm", "mm", "v", "sc", "spk", "spv", "spq",
                 "spa", "spf", "out", "lda", "ldf"):
        ch[name] = Chan(st, nc, "ch_" + name)

    ones_f = st.enter_context(nc.sbuf_tensor("ones_f", [128, 1], F32))
    ones_b = st.enter_context(nc.sbuf_tensor("ones_b", [128, 1], BF16))
    onesrow = st.enter_context(nc.sbuf_tensor("onesrow", [1, 128], F32))
    hc_t = st.enter_context(nc.sbuf_tensor("hc_t", [128, 128], BF16))
    bconst = st.enter_context(nc.sbuf_tensor("bconst", [128, 1], F32))

    def ln_rows(ph, ps_sx, ps_sq, mrow, rrow, vrow, trow, n):
        ph.wait("vector", ch["mm"], ch["mm"].n)
        ph.call("vector", "tensor_scalar", mrow, ps_sx, 1.0 / n, None, ALU.mult,
                inc=ch["v"])
        ph.call("vector", "tensor_scalar", vrow, ps_sq, 1.0 / n, EPS, ALU.mult,
                ALU.add, inc=ch["v"])
        ph.call("vector", "tensor_tensor", trow, mrow, mrow, ALU.mult, inc=ch["v"])
        vv = ph.call("vector", "tensor_tensor", vrow, vrow, trow, ALU.subtract,
                     inc=ch["v"])
        ph.wait("scalar", ch["v"], vv)
        ph.call("scalar", "activation", trow, vrow, ACT.Ln, inc=ch["sc"])
        rv = ph.call("scalar", "activation", rrow, trow, ACT.Exp, scale=-0.5,
                     inc=ch["sc"])
        return vv, rv

    ph = Phase(nc)
    for c0 in ch.values():
        ph.call("sync", "sem_clear", c0.sem)
    ph.close()

    ph = Phase(nc)
    ph.call("vector", "memset", ones_f[:], 1.0, inc=ch["v"])
    ph.call("vector", "memset", ones_b[:], 1.0, inc=ch["v"])
    ph.call("vector", "memset", onesrow[:], 1.0, inc=ch["v"])
    ph.call("vector", "memset", bconst[:], -SOFTC, inc=ch["v"])
    ph.dma("sync", ch["ldm"], hc_t[:], hconst)
    ph.wait("sync", ch["ldm"], ch["ldm"].n)
    ph.wait("sync", ch["v"], ch["v"].n)
    ph.close()

    with ExitStack() as sa1:
        xs_t = [sa1.enter_context(nc.sbuf_tensor(f"xs{i}", [128, NDC * 256], F32))
                for i in range(2)]
        ln1 = sa1.enter_context(nc.sbuf_tensor("ln1", [128, NDC * OWN], BF16))
        sqb = [sa1.enter_context(nc.sbuf_tensor(f"sqb{i}", [128, 256], F32))
               for i in range(2)]
        rows = sa1.enter_context(nc.sbuf_tensor("rows", [1, 4 * 256], F32))
        tmp1 = sa1.enter_context(nc.sbuf_tensor("tmp1", [128, 256], F32))
        wq_t = [sa1.enter_context(nc.sbuf_tensor(f"wq{i}", [128, NDC * 128], BF16))
                for i in range(2)]
        wk_t = [sa1.enter_context(nc.sbuf_tensor(f"wk{i}", [128, NDC * 128], BF16))
                for i in range(2)]
        wv_t = [sa1.enter_context(nc.sbuf_tensor(f"wv{i}", [128, NDC * 128], BF16))
                for i in range(2)]
        kst = [sa1.enter_context(nc.sbuf_tensor(f"kst{i}", [128, 2048], BF16))
               for i in range(2)]
        vst = [sa1.enter_context(nc.sbuf_tensor(f"vst{i}", [128, 2048], BF16))
               for i in range(2)]
        qst = [sa1.enter_context(nc.sbuf_tensor(f"qst{i}", [128, 2048], BF16))
               for i in range(2)]
        ps_sx = sa1.enter_context(nc.psum_tensor("ps_sx", [1, 512], F32))
        ps_sq = sa1.enter_context(nc.psum_tensor("ps_sq", [1, 512], F32))
        ps_mb = sa1.enter_context(nc.psum_tensor("ps_mb", [128, 512], F32))
        ps_rb = sa1.enter_context(nc.psum_tensor("ps_rb", [128, 512], F32))
        ps_k = [sa1.enter_context(nc.psum_tensor(f"ps_k{i}", [128, 512], F32))
                for i in range(4)]
        g = Guard()

        for part in range(2):
            ph = Phase(nc)
            for s in range(8):
                buf = s % 2
                col0 = part * OWN + 256 * s
                g.wait(ph, "sync", ("xs", buf))
                ldv = None
                for i in range(4):
                    dcs = slice(4 * i, 4 * i + 4)
                    ldv = ph.dma(
                        "sync", ch["ldx"],
                        xs_t[buf][:].rearrange("p (c t) -> p c t", c=NDC)[:, dcs, :],
                        xf.rearrange("(c p) t -> p c t", p=128)[:, dcs,
                                                               col0:col0 + 256])
                g.wait(ph, "tensor", ("ps_sx",))
                g.wait(ph, "tensor", ("ps_sq",))
                ph.wait("scalar", ch["ldx"], ldv)
                ph.wait("tensor", ch["ldx"], ldv)
                mmq = []
                for dc in range(NDC):
                    xchunk = xs_t[buf][:, 256 * dc:256 * dc + 256]
                    if dc >= 2:
                        ph.wait("scalar", ch["mm"], mmq[dc - 2])
                    sqv = ph.call("scalar", "activation", sqb[dc % 2][:], xchunk,
                                  ACT.Square, inc=ch["sc"])
                    ph.mm(ps_sx[:, 0:256], ones_f[:], xchunk, dc == 0,
                          dc == NDC - 1,
                          inc=(ch["mm"] if dc == NDC - 1 else None))
                    ph.wait("tensor", ch["sc"], sqv)
                    mmq.append(ph.mm(ps_sq[:, 0:256], ones_f[:], sqb[dc % 2][:],
                                     dc == 0, dc == NDC - 1, inc=ch["mm"]))
                mrow, rrow = rows[:, 0:256], rows[:, 256:512]
                vrow, trow = rows[:, 512:768], rows[:, 768:1024]
                vv, rv = ln_rows(ph, ps_sx[:, 0:256], ps_sq[:, 0:256], mrow,
                                 rrow, vrow, trow, D)
                g.set(("ps_sx",), ch["v"], ch["v"].n)
                g.set(("ps_sq",), ch["v"], ch["v"].n)
                g.wait(ph, "tensor", ("ps_mb",))
                g.wait(ph, "tensor", ("ps_rb",))
                ph.wait("tensor", ch["v"], vv)
                ph.wait("tensor", ch["sc"], rv)
                ph.mm(ps_mb[:, 0:256], onesrow[:], mrow, True, True)
                bcv = ph.mm(ps_rb[:, 0:256], onesrow[:], rrow, True, True,
                            inc=ch["mm"])
                ph.wait("vector", ch["mm"], bcv)
                av = None
                for dc in range(NDC):
                    xchunk = xs_t[buf][:, 256 * dc:256 * dc + 256]
                    ph.call("vector", "tensor_tensor", tmp1[:], xchunk,
                            ps_mb[:, 0:256], ALU.subtract)
                    av = ph.call("vector", "tensor_tensor",
                                 ln1[:, OWN * dc + 256 * s:OWN * dc + 256 * s + 256],
                                 tmp1[:], ps_rb[:, 0:256], ALU.mult, inc=ch["v"])
                g.set(("xs", buf), ch["v"], av)
                g.set(("ps_mb",), ch["v"], av)
                g.set(("ps_rb",), ch["v"], av)

            if debug:
                ph.wait("scalar", ch["v"], ch["v"].n)
                dv = ph.dma("scalar", ch["spa"],
                            dbg["ln1"].rearrange("(c p) t -> p c t", p=128)
                            [:, :, part * OWN:(part + 1) * OWN],
                            ln1[:].rearrange("p (c t) -> p c t", c=NDC))
                ph.wait("scalar", ch["spa"], dv)

            ph.wait("tensor", ch["v"], ch["v"].n)
            for h in range(NH):
                buf = h % 2
                for w_t, col in ((wq_t, 128 * h), (wk_t, D + 128 * h),
                                 (wv_t, 2 * D + 128 * h)):
                    g.wait(ph, "gpsimd", ("wbuf", id(w_t), buf))
                    for i in range(4):
                        dcs = slice(4 * i, 4 * i + 4)
                        ldv = ph.dma(
                            "gpsimd", ch["ldw"],
                            w_t[buf][:].rearrange("p (c o) -> p c o", c=NDC)
                            [:, dcs, :],
                            wqkvT.rearrange("(c p) o -> p c o", p=128)
                            [:, dcs, col:col + 128])
                ph.wait("tensor", ch["ldw"], ch["ldw"].n)
                mats = [(wk_t[buf], kst[buf]), (wv_t[buf], vst[buf])]
                if part == 0:
                    mats.append((wq_t[buf], qst[buf]))
                for (w_t_b, stage) in mats:
                    g.wait(ph, "scalar", ("stage", id(stage)))
                    for tw in range(4):
                        pb = tw % 2
                        g.wait(ph, "tensor", ("ps_k", pb))
                        mv = None
                        for dc in range(NDC):
                            mv = ph.mm(
                                ps_k[pb][:], w_t_b[:, 128 * dc:128 * dc + 128],
                                ln1[:, OWN * dc + 512 * tw:OWN * dc + 512 * tw + 512],
                                dc == 0, dc == NDC - 1,
                                inc=(ch["mm"] if dc == NDC - 1 else None))
                        ph.wait("scalar", ch["mm"], mv)
                        cv = ph.call("scalar", "copy",
                                     stage[:, 512 * tw:512 * tw + 512],
                                     ps_k[pb][:], inc=ch["sc"])
                        g.set(("ps_k", pb), ch["sc"], cv)
                    if stage is kst[buf]:
                        spch, dst = ch["spk"], kspill[h, :, part * OWN:(part + 1) * OWN]
                    elif stage is vst[buf]:
                        spch, dst = ch["spv"], vspill[h, :, part * OWN:(part + 1) * OWN]
                    else:
                        spch, dst = ch["spq"], qspill[h, :, :]
                    sv = ph.dma("scalar", spch, dst, stage[:])
                    g.set(("stage", id(stage)), spch, sv)
                for w_t in (wq_t, wk_t, wv_t):
                    g.set(("wbuf", id(w_t), buf), ch["mm"], ch["mm"].n)
            ph.close()

    with ExitStack() as sattn:
        attnout = sattn.enter_context(
            nc.sbuf_tensor("attnout", [128, NH * OWN], BF16))

        off = {}
        acc = 0
        for c in range(32):
            off[c] = acc
            acc += OWN - 128 * (c % 16)
        PTW = acc

        with ExitStack() as sa2:
            ttiles = sa2.enter_context(nc.sbuf_tensor("ttiles", [128, 32 * 128], BF16))
            kT_t = [sa2.enter_context(nc.sbuf_tensor(f"kT_t{i}", [128, T], BF16))
                    for i in range(2)]
            vT_t = sa2.enter_context(nc.sbuf_tensor("vT_t", [128, T], BF16))
            vN_t = sa2.enter_context(nc.sbuf_tensor("vN_t", [128, T], BF16))
            qT_t = [sa2.enter_context(nc.sbuf_tensor(f"qT_t{i}", [128, OWN], BF16))
                    for i in range(2)]
            pT = sa2.enter_context(nc.sbuf_tensor("pT", [128, PTW], BF16))
            rs_r = sa2.enter_context(nc.sbuf_tensor("rs_r", [1, 512], F32))
            rsb = sa2.enter_context(nc.sbuf_tensor("rsb", [128, 512], F32))
            ps_sc = [sa2.enter_context(nc.psum_tensor(f"ps_sc{i}", [128, 1024], F32))
                     for i in range(3)]
            ps_av = sa2.enter_context(nc.psum_tensor("ps_av", [128, 512], F32))
            ps_sum = sa2.enter_context(nc.psum_tensor("ps_sum", [1, 512], F32))
            g = Guard()

            ph = Phase(nc)
            tmv = None
            for i in range(2):
                cs = slice(16 * i, 16 * i + 16)
                tmv = ph.dma("sync", ch["ldm"],
                             ttiles[:].rearrange("p (c f) -> p c f", c=32)[:, cs, :],
                             tmask.rearrange("p (c f) -> p c f", c=32)[:, cs, :])
            ph.wait("vector", ch["ldm"], tmv)
            ph.wait("sync", ch["spk"], ch["spk"].n)
            ph.wait("sync", ch["spv"], ch["spv"].n)
            ph.wait("sync", ch["spq"], ch["spq"].n)

            def tidx(c):
                return 2 * (c % 16) + (0 if c < 16 else 1)

            for h in range(NH):
                buf = h % 2
                g.wait(ph, "sync", ("kT", buf))
                lk = ph.dma("sync", ch["lda"], kT_t[buf][:], kspill[h])
                g.wait(ph, "sync", ("qT", buf))
                lq = ph.dma("sync", ch["lda"], qT_t[buf][:], qspill[h])
                g.wait(ph, "sync", ("vT",))
                lv = ph.dma("sync", ch["lda"], vT_t[:], vspill[h])
                tvh = [None]
                ph.wait("tensor", ch["lda"], lq)
                g.wait(ph, "scalar", ("pT",))

                pending = []

                def flush(n):
                    for _ in range(min(n, len(pending))):
                        pending.pop(0)()

                def av_units(w, buf=buf):
                    chunks = list(range(0, 4 * w + 4)) +                         list(range(16, 16 + 4 * w + 4))
                    units = []

                    def mkmm(c, first, last, w=w, buf=buf):
                        def f():
                            if first:
                                g.wait(ph, "tensor", ("ps_av",))
                                g.wait(ph, "tensor", ("ps_sum",))
                                ph.wait("tensor", ch["lda"], tvh[0])
                                ph.wait("tensor", ch["sc"], ch["sc"].n)
                                ph.wait("tensor", ch["v"], ch["v"].n)
                            qs = 128 * (c % 16)
                            if qs <= 512 * w:
                                p0 = off[c] + 512 * w - qs
                                rhs = pT[:, p0:p0 + 512]
                                o_av, o_sm = ps_av[:, 0:512], ps_sum[:, 0:512]
                            else:
                                wd = 512 * (w + 1) - qs
                                rhs = pT[:, off[c]:off[c] + wd]
                                o_av = ps_av[:, 512 - wd:512]
                                o_sm = ps_sum[:, 512 - wd:512]
                            ph.mm(o_av, vN_t[:, 128 * c:128 * c + 128], rhs,
                                  first, last, inc=(ch["mm"] if last else None))
                            mv = ph.mm(o_sm, ones_b[:], rhs, first, last,
                                       inc=(ch["mm"] if last else None))
                            if last:
                                g.set(("avmm",), ch["mm"], mv)
                        return f

                    for ci, c in enumerate(chunks):
                        units.append(mkmm(c, ci == 0, ci == len(chunks) - 1))

                    def epilogue(w=w, h=h):
                        c2, v2 = g.d[("avmm",)]
                        ph.wait("vector", c2, v2)
                        rv = ph.call("vector", "reciprocal", rs_r[:], ps_sum[:],
                                     inc=ch["v"])
                        g.set(("ps_sum",), ch["v"], rv)
                        g.wait(ph, "tensor", ("pstile", 0))
                        ph.wait("tensor", ch["v"], rv)
                        mv = ph.mm(ps_sc[0][:, 0:512], onesrow[:], rs_r[:], True,
                                   True, inc=ch["mm"])
                        ph.wait("vector", ch["mm"], mv)
                        ph.call("vector", "tensor_copy", rsb[:],
                                ps_sc[0][:, 0:512], inc=ch["v"])
                        cv = ph.call("vector", "tensor_tensor",
                                     attnout[:, OWN * h + 512 * w:
                                             OWN * h + 512 * w + 512],
                                     rsb[:], ps_av[:], ALU.mult, inc=ch["v"])
                        g.set(("pstile", 0), ch["v"], cv)
                        g.set(("ps_av",), ch["v"], cv)
                    units.append(epilogue)
                    return units

                ntile = 0
                for w in range(4):
                    grp = []
                    for k2 in range(4):
                        grp += [4 * w + k2, 16 + 4 * w + k2]
                    for c in grp:
                        i = c % 16
                        qs = 128 * i
                        pieces = ([(qs, 1024), (1024, 2048)] if i < 8
                                  else [(qs, 2048)])
                        for (p0, p1) in pieces:
                            base = p1 - 1024
                            ps = ps_sc[ntile % 3]
                            tkey = ("pstile", ntile % 3)
                            ntile += 1
                            g.wait(ph, "tensor", tkey)
                            s0 = p0
                            while s0 < p1:
                                s1 = min(p1, 512 * (s0 // 512 + 1))
                                ph.mm(ps[:, s0 - base:s1 - base],
                                      kT_t[buf][:, 128 * c:128 * c + 128],
                                      qT_t[buf][:, s0:s1], True, True,
                                      inc=(ch["mm"] if s1 == p1 else None))
                                s0 = s1
                            ph.wait("scalar", ch["mm"], ch["mm"].n)
                            ev = ph.call("scalar", "activation",
                                         pT[:, off[c] + p0 - qs:off[c] + p1 - qs],
                                         ps[:, p0 - base:p1 - base], ACT.Exp,
                                         scale=SQD, bias=bconst[:], inc=ch["sc"])
                            g.set(tkey, ch["sc"], ev)
                        ph.wait("vector", ch["sc"], ch["sc"].n)
                        ph.call("vector", "tensor_tensor",
                                pT[:, off[c]:off[c] + 128],
                                pT[:, off[c]:off[c] + 128],
                                ttiles[:, tidx(c) * 128:tidx(c) * 128 + 128],
                                ALU.mult, inc=ch["v"])
                        flush(3)
                    if w == 0:
                        ph.wait("sync", ch["lda"], lv)
                        g.wait(ph, "sync", ("vN",))
                        tvh[0] = ph.call(
                            "sync", "dma_start_transpose",
                            vN_t[:].rearrange("p (c d) -> p c d", c=32),
                            vT_t[:].rearrange("p (c k) -> p c k", c=32),
                            inc=ch["lda"], inck=16)
                        g.set(("vT",), ch["lda"], tvh[0])
                    pending += av_units(w)
                flush(len(pending))
                g.set(("pT",), ch["mm"], ch["mm"].n)
                g.set(("kT", buf), ch["mm"], ch["mm"].n)
                g.set(("qT", buf), ch["mm"], ch["mm"].n)
                g.set(("vN",), ch["mm"], ch["mm"].n)
            if debug:
                ph.wait("scalar", ch["v"], ch["v"].n)
                dv = ph.dma("scalar", ch["spa"], dbg["attnout"], attnout[:])
                dv = ph.dma("scalar", ch["spa"], dbg["pT"], pT[:])
                ph.wait("scalar", ch["spa"], dv)
            ph.close()

        with ExitStack() as sa3:
            wp_t = sa3.enter_context(nc.sbuf_tensor("wp_t", [128, NDC * D], BF16))
            projw = sa3.enter_context(nc.sbuf_tensor("projw", [128, NDC * 512], F32))
            awin = sa3.enter_context(nc.sbuf_tensor("awin", [128, NDC * 512], F32))
            sqb3 = [sa3.enter_context(nc.sbuf_tensor(f"sqb3{i}", [128, 512], F32))
                    for i in range(2)]
            tmp3 = sa3.enter_context(nc.sbuf_tensor("tmp3", [128, 512], F32))
            rows3 = sa3.enter_context(nc.sbuf_tensor("rows3", [1, 4 * 512], F32))
            ps_p = [sa3.enter_context(nc.psum_tensor(f"ps_p{i}", [128, 512], F32))
                    for i in range(2)]
            ps_sx = sa3.enter_context(nc.psum_tensor("ps3_sx", [1, 512], F32))
            ps_sq = sa3.enter_context(nc.psum_tensor("ps3_sq", [1, 512], F32))
            ps_mb = sa3.enter_context(nc.psum_tensor("ps3_mb", [128, 512], F32))
            ps_rb = sa3.enter_context(nc.psum_tensor("ps3_rb", [128, 512], F32))
            ps_3x = sa3.enter_context(nc.psum_tensor("ps3_3x", [1, 512], F32))
            ps_3q = sa3.enter_context(nc.psum_tensor("ps3_3q", [1, 512], F32))
            g = Guard()

            ph = Phase(nc)
            for i in range(4):
                fcs = slice(4 * i, 4 * i + 4)
                lw = ph.dma("gpsimd", ch["ldw"],
                            wp_t[:].rearrange("p (c o) -> p c o", c=NDC)[:, fcs, :],
                            wprojT.rearrange("(c p) o -> p c o", p=128)[:, fcs, :])
            for w in range(4):
                g.wait(ph, "sync", ("awin",))
                lx = None
                for i in range(4):
                    dcs = slice(4 * i, 4 * i + 4)
                    lx = ph.dma("sync", ch["ldx"],
                                awin[:].rearrange("p (c t) -> p c t", c=NDC)
                                [:, dcs, :],
                                xf.rearrange("(c p) t -> p c t", p=128)
                                [:, dcs, 512 * w:512 * w + 512])
                ph.wait("tensor", ch["ldw"], lw)
                g.wait(ph, "tensor", ("ps_sx",))
                g.wait(ph, "tensor", ("ps_sq",))
                prevq = None
                for oc in range(NDC):
                    pb = oc % 2
                    g.wait(ph, "tensor", ("ps_p", pb))
                    mv = None
                    for fc in range(NDC):
                        mv = ph.mm(
                            ps_p[pb][:],
                            wp_t[:, D * fc + 128 * oc:D * fc + 128 * oc + 128],
                            attnout[:, OWN * fc + 512 * w:OWN * fc + 512 * w + 512],
                            fc == 0, fc == NDC - 1,
                            inc=(ch["mm"] if fc == NDC - 1 else None))
                    ph.wait("vector", ch["mm"], mv)
                    pv = ph.call("vector", "tensor_copy",
                                 projw[:, 512 * oc:512 * oc + 512], ps_p[pb][:],
                                 inc=ch["v"])
                    g.set(("ps_p", pb), ch["v"], pv)
                    ph.wait("scalar", ch["v"], pv)
                    ph.wait("scalar", ch["mm"], prevq)
                    sv = ph.call("scalar", "activation", sqb3[oc % 2][:],
                                 projw[:, 512 * oc:512 * oc + 512], ACT.Square,
                                 inc=ch["sc"])
                    ph.wait("tensor", ch["v"], pv)
                    ph.mm(ps_sx[:], ones_f[:], projw[:, 512 * oc:512 * oc + 512],
                          oc == 0, oc == NDC - 1,
                          inc=(ch["mm"] if oc == NDC - 1 else None))
                    ph.wait("tensor", ch["sc"], sv)
                    prevq = ph.mm(ps_sq[:], ones_f[:], sqb3[oc % 2][:], oc == 0,
                                  oc == NDC - 1, inc=ch["mm"])
                mrow, rrow = rows3[:, 0:512], rows3[:, 512:1024]
                vrow, trow = rows3[:, 1024:1536], rows3[:, 1536:2048]
                vv, rv = ln_rows(ph, ps_sx[:], ps_sq[:], mrow, rrow, vrow, trow, D)
                g.set(("ps_sx",), ch["v"], ch["v"].n)
                g.set(("ps_sq",), ch["v"], ch["v"].n)
                g.wait(ph, "tensor", ("ps_mb",))
                g.wait(ph, "tensor", ("ps_rb",))
                ph.wait("tensor", ch["v"], vv)
                ph.wait("tensor", ch["sc"], rv)
                ph.mm(ps_mb[:], onesrow[:], mrow, True, True)
                bcv = ph.mm(ps_rb[:], onesrow[:], rrow, True, True, inc=ch["mm"])
                ph.wait("vector", ch["mm"], bcv)
                ph.wait("vector", ch["ldx"], lx)
                g.wait(ph, "tensor", ("ps_3x",))
                g.wait(ph, "tensor", ("ps_3q",))
                av = None
                prevq = None
                for dc in range(NDC):
                    pchunk = projw[:, 512 * dc:512 * dc + 512]
                    ph.call("vector", "tensor_tensor", tmp3[:], pchunk, ps_mb[:],
                            ALU.subtract)
                    ph.call("vector", "tensor_tensor", tmp3[:], tmp3[:], ps_rb[:],
                            ALU.mult)
                    av = ph.call("vector", "tensor_tensor",
                                 awin[:, 512 * dc:512 * dc + 512],
                                 awin[:, 512 * dc:512 * dc + 512], tmp3[:],
                                 ALU.add, inc=ch["v"])
                    ph.wait("scalar", ch["v"], av)
                    ph.wait("scalar", ch["mm"], prevq)
                    sv = ph.call("scalar", "activation", sqb3[dc % 2][:],
                                 awin[:, 512 * dc:512 * dc + 512], ACT.Square,
                                 inc=ch["sc"])
                    ph.wait("tensor", ch["v"], av)
                    ph.mm(ps_3x[:], ones_f[:], awin[:, 512 * dc:512 * dc + 512],
                          dc == 0, dc == NDC - 1,
                          inc=(ch["mm"] if dc == NDC - 1 else None))
                    ph.wait("tensor", ch["sc"], sv)
                    prevq = ph.mm(ps_3q[:], ones_f[:], sqb3[dc % 2][:], dc == 0,
                                  dc == NDC - 1, inc=ch["mm"])
                g.set(("ps_mb",), ch["v"], av)
                g.set(("ps_rb",), ch["v"], av)
                vv, rv = ln_rows(ph, ps_3x[:], ps_3q[:], mrow, rrow, vrow, trow, D)
                g.set(("ps_3x",), ch["v"], ch["v"].n)
                g.set(("ps_3q",), ch["v"], ch["v"].n)
                ph.wait("scalar", ch["v"], vv)
                ph.dma("scalar", ch["spa"], lnrows[0:1, 512 * w:512 * w + 512],
                       mrow)
                ph.dma("scalar", ch["spa"], lnrows[1:2, 512 * w:512 * w + 512],
                       rrow)
                ph.wait("scalar", ch["v"], av)
                sv = None
                for i in range(4):
                    dcs = slice(4 * i, 4 * i + 4)
                    sv = ph.dma("scalar", ch["spa"],
                                aspill.rearrange("(c p) t -> p c t", p=128)
                                [:, dcs, 512 * w:512 * w + 512],
                                awin[:].rearrange("p (c t) -> p c t", c=NDC)
                                [:, dcs, :])
                g.set(("awin",), ch["spa"], sv)
            ph.close()

    for rt in range(2):
        tc0 = 1024 * rt
        with ExitStack() as sf:
            m4_t = sf.enter_context(nc.sbuf_tensor(f"m4_t{rt}", [1, 1024], F32))
            r4_t = sf.enter_context(nc.sbuf_tensor(f"r4_t{rt}", [1, 1024], F32))
            s_hd = sf.enter_context(ExitStack())
            hd = s_hd.enter_context(nc.sbuf_tensor(f"hd{rt}", [128, NHC * 1024], BF16))
            s_ln3 = sf.enter_context(ExitStack())
            ln3 = s_ln3.enter_context(nc.sbuf_tensor(f"ln3{rt}", [128, NDC * 1024], BF16))

            with ExitStack() as s1:
                aT_t = s1.enter_context(nc.sbuf_tensor(f"aT_t{rt}", [128, NDC * 512], F32))
                tmpf = s1.enter_context(nc.sbuf_tensor(f"tmpf{rt}", [128, 512], F32))
                lnr = s1.enter_context(nc.sbuf_tensor(f"lnr{rt}", [1, 1024], F32))
                ps_mb = s1.enter_context(nc.psum_tensor(f"psf_mb{rt}", [128, 512], F32))
                ps_rb = s1.enter_context(nc.psum_tensor(f"psf_rb{rt}", [128, 512], F32))
                ph = Phase(nc)
                ph.wait("sync", ch["spa"], ch["spa"].n)
                for seg in range(2):
                    col = tc0 + 512 * seg
                    if seg > 0:
                        ph.wait("sync", ch["v"], ch["v"].n)
                    lr = ph.dma("sync", ch["ldm"], lnr[:, 0:512],
                                lnrows[0:1, col:col + 512])
                    lr = ph.dma("sync", ch["ldm"], lnr[:, 512:1024],
                                lnrows[1:2, col:col + 512])
                    ph.wait("tensor", ch["ldm"], lr)
                    la = None
                    for i in range(4):
                        dcs = slice(4 * i, 4 * i + 4)
                        la = ph.dma("sync", ch["lda"],
                                    aT_t[:].rearrange("p (c t) -> p c t", c=NDC)
                                    [:, dcs, :],
                                    aspill.rearrange("(c p) t -> p c t", p=128)
                                    [:, dcs, col:col + 512])
                    if seg > 0:
                        ph.wait("tensor", ch["v"], ch["v"].n)
                    ph.mm(ps_mb[:], onesrow[:], lnr[:, 0:512], True, True)
                    bcv = ph.mm(ps_rb[:], onesrow[:], lnr[:, 512:1024],
                                True, True, inc=ch["mm"])
                    ph.wait("vector", ch["mm"], bcv)
                    ph.wait("vector", ch["lda"], la)
                    for dc in range(NDC):
                        ph.call("vector", "tensor_tensor", tmpf[:],
                                aT_t[:, 512 * dc:512 * dc + 512], ps_mb[:],
                                ALU.subtract)
                        ph.call("vector", "tensor_tensor",
                                ln3[:, 1024 * dc + 512 * seg:1024 * dc + 512 * seg + 512],
                                tmpf[:], ps_rb[:], ALU.mult, inc=ch["v"])
                ph.close()

            with ExitStack() as s2:
                w1g = [s2.enter_context(nc.sbuf_tensor(f"w1g{rt}_{i}", [128, NDC * 256], BF16))
                       for i in range(2)]
                w3g = [s2.enter_context(nc.sbuf_tensor(f"w3g{rt}_{i}", [128, NDC * 256], BF16))
                       for i in range(2)]
                tmpg = [s2.enter_context(nc.sbuf_tensor(f"tmpg{rt}_{i}", [128, 512], F32))
                        for i in range(2)]
                ps_g = [s2.enter_context(nc.psum_tensor(f"psf_g{rt}_{i}", [128, 512], F32))
                        for i in range(2)]
                ps_u = [s2.enter_context(nc.psum_tensor(f"psf_u{rt}_{i}", [128, 512], F32))
                        for i in range(2)]
                g = Guard()
                ph = Phase(nc)
                nib = 0
                for hg in range(32):
                    buf = hg % 2
                    hcol = 256 * hg
                    g.wait(ph, "gpsimd", ("w1g", buf))
                    lw = None
                    for wsrc, wdst in ((w1T, w1g[buf]), (w3T, w3g[buf])):
                        for i in range(2):
                            dcs = slice(8 * i, 8 * i + 8)
                            lw = ph.dma("gpsimd", ch["ldw"],
                                        wdst[:].rearrange("p (c o) -> p c o", c=NDC)
                                        [:, dcs, :],
                                        wsrc.rearrange("(c p) o -> p c o", p=128)
                                        [:, dcs, hcol:hcol + 256])
                    ph.wait("tensor", ch["ldw"], lw)
                    for hcl in range(2):
                        c = 2 * hg + hcl
                        for hf in range(2):
                            pb = nib % 2
                            nib += 1
                            g.wait(ph, "tensor", ("ps_g", pb))
                            g.wait(ph, "tensor", ("ps_u", pb))
                            uv = None
                            for dc in range(NDC):
                                l1 = w1g[buf][:, 256 * dc + 128 * hcl:
                                              256 * dc + 128 * hcl + 128]
                                l3 = w3g[buf][:, 256 * dc + 128 * hcl:
                                              256 * dc + 128 * hcl + 128]
                                rhs = ln3[:, 1024 * dc + 512 * hf:
                                          1024 * dc + 512 * hf + 512]
                                lastf = (dc == NDC - 1)
                                ph.mm(ps_g[pb][:], l1, rhs, dc == 0, lastf,
                                      inc=(ch["mm"] if lastf else None))
                                uv = ph.mm(ps_u[pb][:], l3, rhs, dc == 0, lastf,
                                           inc=(ch["mm"] if lastf else None))
                            g.wait(ph, "scalar", ("tmpg", pb))
                            ph.wait("scalar", ch["mm"], uv)
                            sv = ph.call("scalar", "activation", tmpg[pb][:],
                                         ps_g[pb][:], ACT.Silu, inc=ch["sc"])
                            g.set(("ps_g", pb), ch["sc"], sv)
                            ph.wait("vector", ch["sc"], sv)
                            mv = ph.call("vector", "tensor_tensor",
                                         hd[:, 1024 * c + 512 * hf:
                                            1024 * c + 512 * hf + 512],
                                         tmpg[pb][:], ps_u[pb][:], ALU.mult,
                                         inc=ch["v"])
                            g.set(("ps_u", pb), ch["v"], mv)
                            g.set(("tmpg", pb), ch["v"], mv)
                    g.set(("w1g", buf), ch["mm"], ch["mm"].n)
                ph.close()
            s_ln3.close()

            with ExitStack() as s3:
                z_t = s3.enter_context(nc.sbuf_tensor(f"z_t{rt}", [128, 64 * 256], BF16))
                zs_t = s3.enter_context(nc.sbuf_tensor(f"zs_t{rt}", [128, 4 * 256], BF16))
                ps_f = [s3.enter_context(nc.psum_tensor(f"psf_f{rt}_{i}", [128, 256], F32))
                        for i in range(2)]
                g = Guard()
                ph = Phase(nc)
                for ts in range(4):
                    scv = None
                    for c in range(NHC):
                        pb = c % 2
                        g.wait(ph, "tensor", ("ps_f", pb))
                        mv = ph.mm(ps_f[pb][:], hc_t[:],
                                   hd[:, 1024 * c + 256 * ts:1024 * c + 256 * ts + 256],
                                   True, True, inc=ch["mm"])
                        ph.wait("scalar", ch["mm"], mv)
                        scv = ph.call("scalar", "mul",
                                      z_t[:, 256 * c:256 * c + 256], ps_f[pb][:],
                                      RS2, inc=ch["sc"])
                        g.set(("ps_f", pb), ch["sc"], scv)
                    ph.wait("vector", ch["sc"], scv)
                    zv = z_t[:].rearrange("p (c t) -> p c t", c=64)
                    lastv = None
                    for d in (32, 16, 8, 4, 2, 1):
                        ilist = [i2 for i2 in range(64) if (i2 & d) == 0]
                        for g0 in range(0, 32, 4):
                            ib = ilist[g0:g0 + 4]
                            if d >= 4:
                                iap = zv[:, ib[0]:ib[0] + 4, :]
                                jap = zv[:, ib[0] + d:ib[0] + d + 4, :]
                                sap = zs_t[:].rearrange("p (c t) -> p c t", c=4)
                            elif d == 2:
                                zz = z_t[:].rearrange(
                                    "p (g two r t) -> p g two r t", two=2, r=2,
                                    t=256)
                                gg0 = ib[0] // 4
                                iap = zz[:, gg0:gg0 + 2, 0, :, :]
                                jap = zz[:, gg0:gg0 + 2, 1, :, :]
                                sap = zs_t[:].rearrange(
                                    "p (g r t) -> p g r t", g=2, r=2, t=256)
                            else:
                                zz = z_t[:].rearrange(
                                    "p (g two t) -> p g two t", two=2, t=256)
                                gg0 = ib[0] // 2
                                iap = zz[:, gg0:gg0 + 4, 0, :]
                                jap = zz[:, gg0:gg0 + 4, 1, :]
                                sap = zs_t[:].rearrange("p (c t) -> p c t", c=4)
                            ph.call("vector", "tensor_copy", sap, iap)
                            if d > 1:
                                ph.call("vector", "tensor_tensor", iap, sap, jap,
                                        ALU.add)
                                lastv = ph.call("vector", "tensor_tensor", jap,
                                                sap, jap, ALU.subtract,
                                                inc=ch["v"])
                            else:
                                hdz = hd[:].rearrange(
                                    "p (g two t) -> p g two t", two=2, t=1024)
                                hi = hdz[:, gg0:gg0 + 4, 0,
                                         256 * ts:256 * ts + 256]
                                hj = hdz[:, gg0:gg0 + 4, 1,
                                         256 * ts:256 * ts + 256]
                                ph.call("vector", "tensor_tensor", hi, sap, jap,
                                        ALU.add)
                                lastv = ph.call("vector", "tensor_tensor", hj,
                                                sap, jap, ALU.subtract,
                                                inc=ch["v"])
                    ph.wait("tensor", ch["v"], lastv)
                ph.close()

            with ExitStack() as s4:
                w2g = [s4.enter_context(nc.sbuf_tensor(f"w2g{rt}_{i}", [128, NHC * 128], BF16))
                       for i in range(2)]
                fst = [s4.enter_context(nc.sbuf_tensor(f"fst{rt}_{i}", [128, 512], F32))
                       for i in range(2)]
                sqb4 = s4.enter_context(nc.sbuf_tensor(f"sqb4{rt}", [128, 512], F32))
                rows4 = s4.enter_context(nc.sbuf_tensor(f"rows4{rt}", [1, 2 * 512], F32))
                ps_w = [s4.enter_context(nc.psum_tensor(f"psf_w{rt}_{i}", [128, 512], F32))
                        for i in range(2)]
                ps_st = s4.enter_context(nc.psum_tensor(f"psf_st{rt}", [1, 2048], F32))
                g = Guard()
                ph = Phase(nc)
                prevq = None
                for oc in range(NDC):
                    buf = oc % 2
                    g.wait(ph, "gpsimd", ("w2g", buf))
                    lw = None
                    for i in range(8):
                        hcs = slice(8 * i, 8 * i + 8)
                        lw = ph.dma("gpsimd", ch["ldw"],
                                    w2g[buf][:].rearrange("p (c o) -> p c o", c=NHC)
                                    [:, hcs, :],
                                    w2T.rearrange("(c p) o -> p c o", p=128)
                                    [:, hcs, 128 * oc:128 * oc + 128])
                    ph.wait("tensor", ch["ldw"], lw)
                    for tp in range(2):
                        pb = tp
                        g.wait(ph, "tensor", ("ps_w", pb))
                        mv = None
                        for c in range(NHC):
                            mv = ph.mm(ps_w[pb][:],
                                       w2g[buf][:, 128 * c:128 * c + 128],
                                       hd[:, 1024 * c + 512 * tp:
                                          1024 * c + 512 * tp + 512],
                                       c == 0, c == NHC - 1,
                                       inc=(ch["mm"] if c == NHC - 1 else None))
                        ph.wait("vector", ch["mm"], mv)
                        g.wait(ph, "vector", ("fst", pb))
                        g.wait(ph, "vector", ("fstmm", pb))
                        fv = ph.call("vector", "tensor_copy", fst[pb][:],
                                     ps_w[pb][:], inc=ch["v"])
                        g.set(("ps_w", pb), ch["v"], fv)
                        ph.wait("scalar", ch["v"], fv)
                        ph.wait("scalar", ch["mm"], prevq)
                        sv = ph.call("scalar", "activation", sqb4[:], fst[pb][:],
                                     ACT.Square, inc=ch["sc"])
                        ph.wait("tensor", ch["v"], fv)
                        ph.mm(ps_st[:, 1024 * tp:1024 * tp + 512], ones_f[:],
                              fst[pb][:], oc == 0, oc == NDC - 1, inc=ch["mm"])
                        ph.wait("tensor", ch["sc"], sv)
                        prevq = ph.mm(ps_st[:, 1024 * tp + 512:1024 * tp + 1024],
                                      ones_f[:], sqb4[:], oc == 0, oc == NDC - 1,
                                      inc=ch["mm"])
                        g.set(("fstmm", pb), ch["mm"], prevq)
                        dv = ph.dma("scalar", ch["spf"],
                                    fspill[128 * oc:128 * oc + 128,
                                           tc0 + 512 * tp:tc0 + 512 * tp + 512],
                                    fst[pb][:])
                        g.set(("fst", pb), ch["spf"], dv)
                    g.set(("w2g", buf), ch["mm"], ch["mm"].n)
                for tp in range(2):
                    vrow, trow = rows4[:, 0:512], rows4[:, 512:1024]
                    ln_rows(ph, ps_st[:, 1024 * tp:1024 * tp + 512],
                            ps_st[:, 1024 * tp + 512:1024 * tp + 1024],
                            m4_t[:, 512 * tp:512 * tp + 512],
                            r4_t[:, 512 * tp:512 * tp + 512], vrow, trow, D)
                ph.close()
            s_hd.close()

            with ExitStack() as s5:
                fw_t = s5.enter_context(nc.sbuf_tensor(f"fw_t{rt}", [128, NDC * 512], F32))
                aw_t = s5.enter_context(nc.sbuf_tensor(f"aw_t{rt}", [128, NDC * 512], F32))
                ow_t = s5.enter_context(nc.sbuf_tensor(f"ow_t{rt}", [128, NDC * 512], F32))
                tmpo = s5.enter_context(nc.sbuf_tensor(f"tmpo{rt}", [128, 512], F32))
                ps_mb = s5.enter_context(nc.psum_tensor(f"ps5_mb{rt}", [128, 512], F32))
                ps_rb = s5.enter_context(nc.psum_tensor(f"ps5_rb{rt}", [128, 512], F32))
                ph = Phase(nc)
                ph.wait("sync", ch["spf"], ch["spf"].n)
                for tp in range(2):
                    col = tc0 + 512 * tp
                    la = None
                    for i in range(4):
                        dcs = slice(4 * i, 4 * i + 4)
                        ph.dma("sync", ch["ldf"],
                               fw_t[:].rearrange("p (c t) -> p c t", c=NDC)
                               [:, dcs, :],
                               fspill.rearrange("(c p) t -> p c t", p=128)
                               [:, dcs, col:col + 512])
                        la = ph.dma("sync", ch["ldf"],
                                    aw_t[:].rearrange("p (c t) -> p c t", c=NDC)
                                    [:, dcs, :],
                                    aspill.rearrange("(c p) t -> p c t", p=128)
                                    [:, dcs, col:col + 512])
                    if tp > 0:
                        ph.wait("tensor", ch["v"], ch["v"].n)
                    ph.mm(ps_mb[:], onesrow[:], m4_t[:, 512 * tp:512 * tp + 512],
                          True, True)
                    bcv = ph.mm(ps_rb[:], onesrow[:],
                                r4_t[:, 512 * tp:512 * tp + 512], True, True,
                                inc=ch["mm"])
                    ph.wait("vector", ch["mm"], bcv)
                    ph.wait("vector", ch["ldf"], la)
                    ph.wait("vector", ch["out"], ch["out"].n)
                    ov = None
                    for dc in range(NDC):
                        ph.call("vector", "tensor_tensor", tmpo[:],
                                fw_t[:, 512 * dc:512 * dc + 512], ps_mb[:],
                                ALU.subtract)
                        ph.call("vector", "tensor_tensor", tmpo[:], tmpo[:],
                                ps_rb[:], ALU.mult)
                        ov = ph.call("vector", "tensor_tensor",
                                     ow_t[:, 512 * dc:512 * dc + 512], tmpo[:],
                                     aw_t[:, 512 * dc:512 * dc + 512], ALU.add,
                                     inc=ch["v"])
                    ph.wait("sync", ch["v"], ov)
                    for i in range(4):
                        dcs = slice(4 * i, 4 * i + 4)
                        ph.dma("sync", ch["out"],
                               out_d.rearrange("(c p) t -> p c t", p=128)
                               [:, dcs, col:col + 512],
                               ow_t[:].rearrange("p (c t) -> p c t", c=NDC)
                               [:, dcs, :])
                if rt == 1:
                    ph.wait("sync", ch["out"], ch["out"].n)
                    for nm in ("spk", "spv", "spq", "spa", "spf"):
                        ph.wait("sync", ch[nm], ch[nm].n)
                ph.close()

    st.close()
    global LAST_CH
    LAST_CH = {k: (c.sem.num, c.n) for k, c in ch.items()}
    return nc


_NC_CACHE = {}


def _get_nc(debug):
    key = bool(debug)
    if key not in _NC_CACHE:
        _NC_CACHE[key] = build(debug=key)
    return _NC_CACHE[key]


def _hadamard128():
    idx = np.arange(128)
    anded = idx[:, None] & idx[None, :]
    pc = np.zeros((128, 128), dtype=np.int64)
    for b in range(7):
        pc += (anded >> b) & 1
    return np.where(pc % 2 == 0, 1.0, -1.0).astype(np.float32)


def kernel(L0_attn_x, L0_attn_W_qkv, L0_attn_W_proj, L0_ffn_W1, L0_ffn_W2,
           L0_ffn_W3):
    x = np.ascontiguousarray(np.asarray(L0_attn_x, dtype=np.float32))
    wqkvT = np.ascontiguousarray(np.asarray(L0_attn_W_qkv, np.float32).T)
    wprojT = np.ascontiguousarray(np.asarray(L0_attn_W_proj, np.float32).T)
    w1T = np.ascontiguousarray(np.asarray(L0_ffn_W1, np.float32).T)
    w2T = np.ascontiguousarray(np.asarray(L0_ffn_W2, np.float32).T)
    w3T = np.ascontiguousarray(np.asarray(L0_ffn_W3, np.float32).T)
    hmat = (_hadamard128() * (2.0 ** -6)).astype(ml_dtypes.bfloat16)

    nc = _get_nc(DEBUG)
    in_maps = []
    owns = []
    for c in range(8):
        b, e = c >> 1, c & 1
        blocks = np.arange(T).reshape(16, 2, 128)
        own = blocks[:, e, :].ravel()
        other = blocks[:, 1 - e, :].ravel()
        key_order = np.concatenate([own, other])
        owns.append((b, own))
        xfc = np.ascontiguousarray(x[b].T[:, key_order])
        qg = own.reshape(16, 128).astype(np.float64)
        kg = key_order.reshape(32, 128).astype(np.float64)
        tm = np.empty((128, 32, 128), np.float32)
        for j in range(16):
            for d, c in ((0, j), (1, 16 + j)):
                tm[:, 2 * j + d, :] = (
                    kg[c][:, None] <= qg[j][None, :]).astype(np.float32)
        in_maps.append({
            "xf": xfc, "wqkvT": wqkvT, "wprojT": wprojT, "w1T": w1T,
            "w2T": w2T, "w3T": w3T,
            "tmask": tm.reshape(128, 32 * 128).astype(ml_dtypes.bfloat16),
            "hconst": hmat,
        })

    res = run_bass_kernel_spmd(nc, in_maps, core_ids=list(range(8)))
    out = np.empty((4, T, D), dtype=np.float32)
    for c in range(8):
        b, own = owns[c]
        out[b][own, :] = res.results[c]["out"].T
    if DEBUG:
        kernel.last_results = res.results
    return out


# version 11
# speedup vs baseline: 1.1748x; 1.0423x over previous
import numpy as np
import ml_dtypes
from contextlib import ExitStack

import concourse.bass as bass
import concourse.mybir as mybir
from concourse.bass_utils import run_bass_kernel_spmd

F32 = mybir.dt.float32
BF16 = mybir.dt.bfloat16
ALU = mybir.AluOpType
ACT = mybir.ActivationFunctionType
AX = mybir.AxisListType.X

D = 2048
T = 4096
OWN = 2048
NH = 16
HID = 8192
NDC = D // 128
NHC = HID // 128
SQD = 1.0 / np.sqrt(128.0)
LN128 = float(np.log(128.0))
RS2 = float(2.0 ** -0.5)
SOFTC = 25.0
EPS = 1e-5

DEBUG = False
LAST_CH = None

ENGS = ("sync", "tensor", "vector", "scalar", "gpsimd")


class Chan:
    def __init__(self, stack, nc, name):
        self.sem = stack.enter_context(nc.semaphore(name))
        self.n = 0

    def bump(self, k=1):
        self.n += k
        return self.n


class Phase:

    def __init__(self, nc):
        self.nc = nc
        self.ops = {k: [] for k in ENGS}
        self.mark = {}

    def op(self, eng, fn):
        self.ops[eng].append(fn)

    def wait(self, eng, ch, val):
        if val is None or val <= 0:
            return
        key = (eng, id(ch))
        if self.mark.get(key, -1) >= val:
            return
        self.mark[key] = val
        self.ops[eng].append(lambda e, s=ch.sem, v=val: e.wait_ge(s, v))

    def call(self, eng, meth, *args, inc=None, inck=1, **kw):
        if inc is not None:
            inc.bump(inck)

            def f(e, meth=meth, args=args, kw=kw, sem=inc.sem, k=inck):
                try:
                    getattr(e, meth)(*args, **kw).then_inc(sem, k)
                except Exception:
                    print("FAIL", meth, [getattr(a, "shape", a) for a in args], kw)
                    raise
        else:
            def f(e, meth=meth, args=args, kw=kw):
                try:
                    getattr(e, meth)(*args, **kw)
                except Exception:
                    print("FAIL", meth, [getattr(a, "shape", a) for a in args], kw)
                    raise
        self.ops[eng].append(f)
        return inc.n if inc is not None else None

    def mm(self, out, lhsT, rhs, start, stop, inc=None):
        return self.call("tensor", "matmul", out, lhsT, rhs, start=start,
                         stop=stop, skip_group_check=True, inc=inc)

    def dma(self, eng, ch, out, in_, **kw):
        return self.call(eng, "dma_start", inc=ch, inck=16, out=out, in_=in_, **kw)

    def close(self):
        with self.nc.Block() as blk:
            for name in ENGS:
                fns = self.ops[name]
                if fns:
                    getattr(blk, name)(lambda e, fns=fns: [f(e) for f in fns])


class Guard:

    def __init__(self):
        self.d = {}

    def wait(self, ph, eng, key):
        if key in self.d:
            c, v = self.d[key]
            ph.wait(eng, c, v)

    def set(self, key, ch, val):
        self.d[key] = (ch, val)


def build(debug=False):
    nc = bass.Bass("TRN2", target_bir_lowering=False)
    st = ExitStack()
    scratch_kind = "ExternalOutput" if debug else "Internal"

    xf = nc.dram_tensor("xf", [D, T], F32, kind="ExternalInput").ap()
    wqkvT = nc.dram_tensor("wqkvT", [D, 3 * D], F32, kind="ExternalInput").ap()
    wprojT = nc.dram_tensor("wprojT", [D, D], F32, kind="ExternalInput").ap()
    w1T = nc.dram_tensor("w1T", [D, HID], F32, kind="ExternalInput").ap()
    w2T = nc.dram_tensor("w2T", [HID, D], F32, kind="ExternalInput").ap()
    w3T = nc.dram_tensor("w3T", [D, HID], F32, kind="ExternalInput").ap()
    tmask = nc.dram_tensor("tmask", [128, 32 * 128], BF16, kind="ExternalInput").ap()
    hconst = nc.dram_tensor("hconst", [128, 128], BF16, kind="ExternalInput").ap()
    out_d = nc.dram_tensor("out", [D, OWN], F32, kind="ExternalOutput").ap()

    kspill = nc.dram_tensor("kspill", [NH, 128, T], BF16, kind=scratch_kind).ap()
    vspill = nc.dram_tensor("vspill", [NH, 128, T], BF16, kind=scratch_kind).ap()
    qspill = nc.dram_tensor("qspill", [NH, 128, OWN], BF16, kind=scratch_kind).ap()
    aspill = nc.dram_tensor("aspill", [D, OWN], F32, kind=scratch_kind).ap()
    lnrows = nc.dram_tensor("lnrows", [2, OWN], F32, kind=scratch_kind).ap()
    fspill = nc.dram_tensor("fspill", [D, OWN], F32, kind=scratch_kind).ap()
    dbg = {}
    if debug:
        dbg["ln1"] = nc.dram_tensor("dbg_ln1", [D, T], BF16, kind="ExternalOutput").ap()
        dbg["attnout"] = nc.dram_tensor("dbg_attnout", [128, NH * OWN], BF16,
                                        kind="ExternalOutput").ap()
        dbg["pT"] = nc.dram_tensor("dbg_pT", [128, 34816], BF16,
                                   kind="ExternalOutput").ap()

    ch = {}
    for name in ("ldx", "ldw", "ldm", "mm", "v", "sc", "spk", "spv", "spq",
                 "spa", "spf", "out", "lda", "ldf"):
        ch[name] = Chan(st, nc, "ch_" + name)

    ones_f = st.enter_context(nc.sbuf_tensor("ones_f", [128, 1], F32))
    ones_b = st.enter_context(nc.sbuf_tensor("ones_b", [128, 1], BF16))
    onesrow = st.enter_context(nc.sbuf_tensor("onesrow", [1, 128], F32))
    hc_t = st.enter_context(nc.sbuf_tensor("hc_t", [128, 128], BF16))
    bconst = st.enter_context(nc.sbuf_tensor("bconst", [128, 1], F32))

    def ln_rows(ph, ps_sx, ps_sq, mrow, rrow, vrow, trow, n):
        ph.wait("vector", ch["mm"], ch["mm"].n)
        ph.call("vector", "tensor_scalar", mrow, ps_sx, 1.0 / n, None, ALU.mult,
                inc=ch["v"])
        ph.call("vector", "tensor_scalar", vrow, ps_sq, 1.0 / n, EPS, ALU.mult,
                ALU.add, inc=ch["v"])
        ph.call("vector", "tensor_tensor", trow, mrow, mrow, ALU.mult, inc=ch["v"])
        vv = ph.call("vector", "tensor_tensor", vrow, vrow, trow, ALU.subtract,
                     inc=ch["v"])
        ph.wait("scalar", ch["v"], vv)
        ph.call("scalar", "activation", trow, vrow, ACT.Ln, inc=ch["sc"])
        rv = ph.call("scalar", "activation", rrow, trow, ACT.Exp, scale=-0.5,
                     inc=ch["sc"])
        return vv, rv

    ph = Phase(nc)
    for c0 in ch.values():
        ph.call("sync", "sem_clear", c0.sem)
    ph.close()

    ph = Phase(nc)
    ph.call("vector", "memset", ones_f[:], 1.0, inc=ch["v"])
    ph.call("vector", "memset", ones_b[:], 1.0, inc=ch["v"])
    ph.call("vector", "memset", onesrow[:], 1.0, inc=ch["v"])
    ph.call("vector", "memset", bconst[:], -SOFTC, inc=ch["v"])
    ph.dma("sync", ch["ldm"], hc_t[:], hconst)
    ph.wait("sync", ch["ldm"], ch["ldm"].n)
    ph.wait("sync", ch["v"], ch["v"].n)
    ph.close()

    with ExitStack() as sa1:
        xs_t = [sa1.enter_context(nc.sbuf_tensor(f"xs{i}", [128, NDC * 256], F32))
                for i in range(2)]
        ln1 = sa1.enter_context(nc.sbuf_tensor("ln1", [128, NDC * OWN], BF16))
        sqb = [sa1.enter_context(nc.sbuf_tensor(f"sqb{i}", [128, 256], F32))
               for i in range(2)]
        rows = sa1.enter_context(nc.sbuf_tensor("rows", [1, 4 * 256], F32))
        tmp1 = sa1.enter_context(nc.sbuf_tensor("tmp1", [128, 256], F32))
        wq_t = [sa1.enter_context(nc.sbuf_tensor(f"wq{i}", [128, NDC * 128], BF16))
                for i in range(2)]
        wk_t = [sa1.enter_context(nc.sbuf_tensor(f"wk{i}", [128, NDC * 128], BF16))
                for i in range(2)]
        wv_t = [sa1.enter_context(nc.sbuf_tensor(f"wv{i}", [128, NDC * 128], BF16))
                for i in range(2)]
        kst = [sa1.enter_context(nc.sbuf_tensor(f"kst{i}", [128, 2048], BF16))
               for i in range(2)]
        vst = [sa1.enter_context(nc.sbuf_tensor(f"vst{i}", [128, 2048], BF16))
               for i in range(2)]
        qst = [sa1.enter_context(nc.sbuf_tensor(f"qst{i}", [128, 2048], BF16))
               for i in range(2)]
        ps_sx = sa1.enter_context(nc.psum_tensor("ps_sx", [1, 512], F32))
        ps_sq = sa1.enter_context(nc.psum_tensor("ps_sq", [1, 512], F32))
        ps_mb = sa1.enter_context(nc.psum_tensor("ps_mb", [128, 512], F32))
        ps_rb = sa1.enter_context(nc.psum_tensor("ps_rb", [128, 512], F32))
        ps_k = [sa1.enter_context(nc.psum_tensor(f"ps_k{i}", [128, 512], F32))
                for i in range(4)]
        g = Guard()

        for part in range(2):
            ph = Phase(nc)
            for s in range(8):
                buf = s % 2
                col0 = part * OWN + 256 * s
                g.wait(ph, "sync", ("xs", buf))
                ldv = None
                for i in range(4):
                    dcs = slice(4 * i, 4 * i + 4)
                    ldv = ph.dma(
                        "sync", ch["ldx"],
                        xs_t[buf][:].rearrange("p (c t) -> p c t", c=NDC)[:, dcs, :],
                        xf.rearrange("(c p) t -> p c t", p=128)[:, dcs,
                                                               col0:col0 + 256])
                g.wait(ph, "tensor", ("ps_sx",))
                g.wait(ph, "tensor", ("ps_sq",))
                ph.wait("scalar", ch["ldx"], ldv)
                ph.wait("tensor", ch["ldx"], ldv)
                mmq = []
                for dc in range(NDC):
                    xchunk = xs_t[buf][:, 256 * dc:256 * dc + 256]
                    if dc >= 2:
                        ph.wait("scalar", ch["mm"], mmq[dc - 2])
                    sqv = ph.call("scalar", "activation", sqb[dc % 2][:], xchunk,
                                  ACT.Square, inc=ch["sc"])
                    ph.mm(ps_sx[:, 0:256], ones_f[:], xchunk, dc == 0,
                          dc == NDC - 1,
                          inc=(ch["mm"] if dc == NDC - 1 else None))
                    ph.wait("tensor", ch["sc"], sqv)
                    mmq.append(ph.mm(ps_sq[:, 0:256], ones_f[:], sqb[dc % 2][:],
                                     dc == 0, dc == NDC - 1, inc=ch["mm"]))
                mrow, rrow = rows[:, 0:256], rows[:, 256:512]
                vrow, trow = rows[:, 512:768], rows[:, 768:1024]
                vv, rv = ln_rows(ph, ps_sx[:, 0:256], ps_sq[:, 0:256], mrow,
                                 rrow, vrow, trow, D)
                g.set(("ps_sx",), ch["v"], ch["v"].n)
                g.set(("ps_sq",), ch["v"], ch["v"].n)
                g.wait(ph, "tensor", ("ps_mb",))
                g.wait(ph, "tensor", ("ps_rb",))
                ph.wait("tensor", ch["v"], vv)
                ph.wait("tensor", ch["sc"], rv)
                ph.mm(ps_mb[:, 0:256], onesrow[:], mrow, True, True)
                bcv = ph.mm(ps_rb[:, 0:256], onesrow[:], rrow, True, True,
                            inc=ch["mm"])
                ph.wait("vector", ch["mm"], bcv)
                av = None
                for dc in range(NDC):
                    xchunk = xs_t[buf][:, 256 * dc:256 * dc + 256]
                    ph.call("vector", "tensor_tensor", tmp1[:], xchunk,
                            ps_mb[:, 0:256], ALU.subtract)
                    av = ph.call("vector", "tensor_tensor",
                                 ln1[:, OWN * dc + 256 * s:OWN * dc + 256 * s + 256],
                                 tmp1[:], ps_rb[:, 0:256], ALU.mult, inc=ch["v"])
                g.set(("xs", buf), ch["v"], av)
                g.set(("ps_mb",), ch["v"], av)
                g.set(("ps_rb",), ch["v"], av)

            if debug:
                ph.wait("scalar", ch["v"], ch["v"].n)
                dv = ph.dma("scalar", ch["spa"],
                            dbg["ln1"].rearrange("(c p) t -> p c t", p=128)
                            [:, :, part * OWN:(part + 1) * OWN],
                            ln1[:].rearrange("p (c t) -> p c t", c=NDC))
                ph.wait("scalar", ch["spa"], dv)

            ph.wait("tensor", ch["v"], ch["v"].n)
            for h in range(NH):
                buf = h % 2
                for w_t, col in ((wq_t, 128 * h), (wk_t, D + 128 * h),
                                 (wv_t, 2 * D + 128 * h)):
                    g.wait(ph, "gpsimd", ("wbuf", id(w_t), buf))
                    for i in range(4):
                        dcs = slice(4 * i, 4 * i + 4)
                        ldv = ph.dma(
                            "gpsimd", ch["ldw"],
                            w_t[buf][:].rearrange("p (c o) -> p c o", c=NDC)
                            [:, dcs, :],
                            wqkvT.rearrange("(c p) o -> p c o", p=128)
                            [:, dcs, col:col + 128])
                ph.wait("tensor", ch["ldw"], ch["ldw"].n)
                mats = [(wk_t[buf], kst[buf]), (wv_t[buf], vst[buf])]
                if part == 0:
                    mats.append((wq_t[buf], qst[buf]))
                for (w_t_b, stage) in mats:
                    g.wait(ph, "scalar", ("stage", id(stage)))
                    for tw in range(4):
                        pb = tw % 2
                        g.wait(ph, "tensor", ("ps_k", pb))
                        mv = None
                        for dc in range(NDC):
                            mv = ph.mm(
                                ps_k[pb][:], w_t_b[:, 128 * dc:128 * dc + 128],
                                ln1[:, OWN * dc + 512 * tw:OWN * dc + 512 * tw + 512],
                                dc == 0, dc == NDC - 1,
                                inc=(ch["mm"] if dc == NDC - 1 else None))
                        ph.wait("scalar", ch["mm"], mv)
                        cv = ph.call("scalar", "copy",
                                     stage[:, 512 * tw:512 * tw + 512],
                                     ps_k[pb][:], inc=ch["sc"])
                        g.set(("ps_k", pb), ch["sc"], cv)
                    if stage is kst[buf]:
                        spch, dst = ch["spk"], kspill[h, :, part * OWN:(part + 1) * OWN]
                    elif stage is vst[buf]:
                        spch, dst = ch["spv"], vspill[h, :, part * OWN:(part + 1) * OWN]
                    else:
                        spch, dst = ch["spq"], qspill[h, :, :]
                    sv = ph.dma("scalar", spch, dst, stage[:])
                    g.set(("stage", id(stage)), spch, sv)
                for w_t in (wq_t, wk_t, wv_t):
                    g.set(("wbuf", id(w_t), buf), ch["mm"], ch["mm"].n)
            ph.close()

    with ExitStack() as sattn:
        attnout = sattn.enter_context(
            nc.sbuf_tensor("attnout", [128, NH * OWN], BF16))

        off = {}
        acc = 0
        for c in range(32):
            off[c] = acc
            acc += OWN - 128 * (c % 16)
        PTW = acc

        with ExitStack() as sa2:
            ttiles = sa2.enter_context(nc.sbuf_tensor("ttiles", [128, 32 * 128], BF16))
            kT_t = [sa2.enter_context(nc.sbuf_tensor(f"kT_t{i}", [128, T], BF16))
                    for i in range(2)]
            vT_t = sa2.enter_context(nc.sbuf_tensor("vT_t", [128, T], BF16))
            vN_t = sa2.enter_context(nc.sbuf_tensor("vN_t", [128, T], BF16))
            qT_t = [sa2.enter_context(nc.sbuf_tensor(f"qT_t{i}", [128, OWN], BF16))
                    for i in range(2)]
            pT = sa2.enter_context(nc.sbuf_tensor("pT", [128, PTW], BF16))
            rs_r = sa2.enter_context(nc.sbuf_tensor("rs_r", [1, 512], F32))
            rsb = sa2.enter_context(nc.sbuf_tensor("rsb", [128, 512], F32))
            ps_sc = [sa2.enter_context(nc.psum_tensor(f"ps_sc{i}", [128, 1024], F32))
                     for i in range(3)]
            ps_av = sa2.enter_context(nc.psum_tensor("ps_av", [128, 512], F32))
            ps_sum = sa2.enter_context(nc.psum_tensor("ps_sum", [1, 512], F32))
            g = Guard()

            ph = Phase(nc)
            tmv = None
            for i in range(2):
                cs = slice(16 * i, 16 * i + 16)
                tmv = ph.dma("sync", ch["ldm"],
                             ttiles[:].rearrange("p (c f) -> p c f", c=32)[:, cs, :],
                             tmask.rearrange("p (c f) -> p c f", c=32)[:, cs, :])
            ph.wait("vector", ch["ldm"], tmv)
            ph.wait("sync", ch["spk"], ch["spk"].n)
            ph.wait("sync", ch["spv"], ch["spv"].n)
            ph.wait("sync", ch["spq"], ch["spq"].n)

            def tidx(c):
                return 2 * (c % 16) + (0 if c < 16 else 1)

            for h in range(NH):
                buf = h % 2
                g.wait(ph, "sync", ("kT", buf))
                lk = ph.dma("sync", ch["lda"], kT_t[buf][:], kspill[h])
                g.wait(ph, "sync", ("qT", buf))
                lq = ph.dma("sync", ch["lda"], qT_t[buf][:], qspill[h])
                g.wait(ph, "sync", ("vT",))
                lv = ph.dma("sync", ch["lda"], vT_t[:], vspill[h])
                ph.wait("sync", ch["lda"], lv)
                g.wait(ph, "sync", ("vN",))
                tvh = [None]
                tvh[0] = ph.call("sync", "dma_start_transpose",
                                 vN_t[:].rearrange("p (c d) -> p c d", c=32),
                                 vT_t[:].rearrange("p (c k) -> p c k", c=32),
                                 inc=ch["lda"], inck=16)
                g.set(("vT",), ch["lda"], tvh[0])
                ph.wait("tensor", ch["lda"], lq)
                g.wait(ph, "scalar", ("pT",))

                pending = []

                def flush(n):
                    for _ in range(min(n, len(pending))):
                        pending.pop(0)()

                def av_units(w, buf=buf):
                    chunks = list(range(0, 4 * w + 4)) +                         list(range(16, 16 + 4 * w + 4))
                    units = []

                    def mkmm(c, first, last, w=w, buf=buf):
                        def f():
                            if first:
                                g.wait(ph, "tensor", ("ps_av",))
                                g.wait(ph, "tensor", ("ps_sum",))
                                ph.wait("tensor", ch["lda"], tvh[0])
                                ph.wait("tensor", ch["sc"], ch["sc"].n)
                                ph.wait("tensor", ch["v"], ch["v"].n)
                            qs = 128 * (c % 16)
                            if qs <= 512 * w:
                                p0 = off[c] + 512 * w - qs
                                rhs = pT[:, p0:p0 + 512]
                                o_av, o_sm = ps_av[:, 0:512], ps_sum[:, 0:512]
                            else:
                                wd = 512 * (w + 1) - qs
                                rhs = pT[:, off[c]:off[c] + wd]
                                o_av = ps_av[:, 512 - wd:512]
                                o_sm = ps_sum[:, 512 - wd:512]
                            ph.mm(o_av, vN_t[:, 128 * c:128 * c + 128], rhs,
                                  first, last, inc=(ch["mm"] if last else None))
                            mv = ph.mm(o_sm, ones_b[:], rhs, first, last,
                                       inc=(ch["mm"] if last else None))
                            if last:
                                g.set(("avmm",), ch["mm"], mv)
                        return f

                    for ci, c in enumerate(chunks):
                        units.append(mkmm(c, ci == 0, ci == len(chunks) - 1))

                    def epilogue(w=w, h=h):
                        c2, v2 = g.d[("avmm",)]
                        ph.wait("vector", c2, v2)
                        rv = ph.call("vector", "reciprocal", rs_r[:], ps_sum[:],
                                     inc=ch["v"])
                        g.set(("ps_sum",), ch["v"], rv)
                        g.wait(ph, "tensor", ("pstile", 0))
                        ph.wait("tensor", ch["v"], rv)
                        mv = ph.mm(ps_sc[0][:, 0:512], onesrow[:], rs_r[:], True,
                                   True, inc=ch["mm"])
                        ph.wait("vector", ch["mm"], mv)
                        ph.call("vector", "tensor_copy", rsb[:],
                                ps_sc[0][:, 0:512], inc=ch["v"])
                        cv = ph.call("vector", "tensor_tensor",
                                     attnout[:, OWN * h + 512 * w:
                                             OWN * h + 512 * w + 512],
                                     rsb[:], ps_av[:], ALU.mult, inc=ch["v"])
                        g.set(("pstile", 0), ch["v"], cv)
                        g.set(("ps_av",), ch["v"], cv)
                    units.append(epilogue)
                    return units

                ntile = 0
                for w in range(4):
                    grp = []
                    for k2 in range(4):
                        grp += [4 * w + k2, 16 + 4 * w + k2]
                    for c in grp:
                        i = c % 16
                        qs = 128 * i
                        pieces = ([(qs, 1024), (1024, 2048)] if i < 8
                                  else [(qs, 2048)])
                        for (p0, p1) in pieces:
                            base = p1 - 1024
                            ps = ps_sc[ntile % 3]
                            tkey = ("pstile", ntile % 3)
                            ntile += 1
                            g.wait(ph, "tensor", tkey)
                            s0 = p0
                            while s0 < p1:
                                s1 = min(p1, 512 * (s0 // 512 + 1))
                                ph.mm(ps[:, s0 - base:s1 - base],
                                      kT_t[buf][:, 128 * c:128 * c + 128],
                                      qT_t[buf][:, s0:s1], True, True,
                                      inc=(ch["mm"] if s1 == p1 else None))
                                s0 = s1
                            ph.wait("scalar", ch["mm"], ch["mm"].n)
                            ev = ph.call("scalar", "activation",
                                         pT[:, off[c] + p0 - qs:off[c] + p1 - qs],
                                         ps[:, p0 - base:p1 - base], ACT.Exp,
                                         scale=SQD, bias=bconst[:], inc=ch["sc"])
                            g.set(tkey, ch["sc"], ev)
                        ph.wait("vector", ch["sc"], ch["sc"].n)
                        ph.call("vector", "tensor_tensor",
                                pT[:, off[c]:off[c] + 128],
                                pT[:, off[c]:off[c] + 128],
                                ttiles[:, tidx(c) * 128:tidx(c) * 128 + 128],
                                ALU.mult, inc=ch["v"])
                        flush(3)
                    pending += av_units(w)
                flush(len(pending))
                g.set(("pT",), ch["mm"], ch["mm"].n)
                g.set(("kT", buf), ch["mm"], ch["mm"].n)
                g.set(("qT", buf), ch["mm"], ch["mm"].n)
                g.set(("vN",), ch["mm"], ch["mm"].n)
            if debug:
                ph.wait("scalar", ch["v"], ch["v"].n)
                dv = ph.dma("scalar", ch["spa"], dbg["attnout"], attnout[:])
                dv = ph.dma("scalar", ch["spa"], dbg["pT"], pT[:])
                ph.wait("scalar", ch["spa"], dv)
            ph.close()

        with ExitStack() as sa3:
            wp_t = sa3.enter_context(nc.sbuf_tensor("wp_t", [128, NDC * D], BF16))
            projw = sa3.enter_context(nc.sbuf_tensor("projw", [128, NDC * 512], F32))
            awin = sa3.enter_context(nc.sbuf_tensor("awin", [128, NDC * 512], F32))
            sqb3 = [sa3.enter_context(nc.sbuf_tensor(f"sqb3{i}", [128, 512], F32))
                    for i in range(2)]
            tmp3 = sa3.enter_context(nc.sbuf_tensor("tmp3", [128, 512], F32))
            rows3 = sa3.enter_context(nc.sbuf_tensor("rows3", [1, 4 * 512], F32))
            ps_p = [sa3.enter_context(nc.psum_tensor(f"ps_p{i}", [128, 512], F32))
                    for i in range(2)]
            ps_sx = sa3.enter_context(nc.psum_tensor("ps3_sx", [1, 512], F32))
            ps_sq = sa3.enter_context(nc.psum_tensor("ps3_sq", [1, 512], F32))
            ps_mb = sa3.enter_context(nc.psum_tensor("ps3_mb", [128, 512], F32))
            ps_rb = sa3.enter_context(nc.psum_tensor("ps3_rb", [128, 512], F32))
            ps_3x = sa3.enter_context(nc.psum_tensor("ps3_3x", [1, 512], F32))
            ps_3q = sa3.enter_context(nc.psum_tensor("ps3_3q", [1, 512], F32))
            g = Guard()

            ph = Phase(nc)
            for i in range(4):
                fcs = slice(4 * i, 4 * i + 4)
                lw = ph.dma("gpsimd", ch["ldw"],
                            wp_t[:].rearrange("p (c o) -> p c o", c=NDC)[:, fcs, :],
                            wprojT.rearrange("(c p) o -> p c o", p=128)[:, fcs, :])
            for w in range(4):
                g.wait(ph, "sync", ("awin",))
                lx = None
                for i in range(4):
                    dcs = slice(4 * i, 4 * i + 4)
                    lx = ph.dma("sync", ch["ldx"],
                                awin[:].rearrange("p (c t) -> p c t", c=NDC)
                                [:, dcs, :],
                                xf.rearrange("(c p) t -> p c t", p=128)
                                [:, dcs, 512 * w:512 * w + 512])
                ph.wait("tensor", ch["ldw"], lw)
                g.wait(ph, "tensor", ("ps_sx",))
                g.wait(ph, "tensor", ("ps_sq",))
                prevq = None
                for oc in range(NDC):
                    pb = oc % 2
                    g.wait(ph, "tensor", ("ps_p", pb))
                    mv = None
                    for fc in range(NDC):
                        mv = ph.mm(
                            ps_p[pb][:],
                            wp_t[:, D * fc + 128 * oc:D * fc + 128 * oc + 128],
                            attnout[:, OWN * fc + 512 * w:OWN * fc + 512 * w + 512],
                            fc == 0, fc == NDC - 1,
                            inc=(ch["mm"] if fc == NDC - 1 else None))
                    ph.wait("vector", ch["mm"], mv)
                    pv = ph.call("vector", "tensor_copy",
                                 projw[:, 512 * oc:512 * oc + 512], ps_p[pb][:],
                                 inc=ch["v"])
                    g.set(("ps_p", pb), ch["v"], pv)
                    ph.wait("scalar", ch["v"], pv)
                    ph.wait("scalar", ch["mm"], prevq)
                    sv = ph.call("scalar", "activation", sqb3[oc % 2][:],
                                 projw[:, 512 * oc:512 * oc + 512], ACT.Square,
                                 inc=ch["sc"])
                    ph.wait("tensor", ch["v"], pv)
                    ph.mm(ps_sx[:], ones_f[:], projw[:, 512 * oc:512 * oc + 512],
                          oc == 0, oc == NDC - 1,
                          inc=(ch["mm"] if oc == NDC - 1 else None))
                    ph.wait("tensor", ch["sc"], sv)
                    prevq = ph.mm(ps_sq[:], ones_f[:], sqb3[oc % 2][:], oc == 0,
                                  oc == NDC - 1, inc=ch["mm"])
                mrow, rrow = rows3[:, 0:512], rows3[:, 512:1024]
                vrow, trow = rows3[:, 1024:1536], rows3[:, 1536:2048]
                vv, rv = ln_rows(ph, ps_sx[:], ps_sq[:], mrow, rrow, vrow, trow, D)
                g.set(("ps_sx",), ch["v"], ch["v"].n)
                g.set(("ps_sq",), ch["v"], ch["v"].n)
                g.wait(ph, "tensor", ("ps_mb",))
                g.wait(ph, "tensor", ("ps_rb",))
                ph.wait("tensor", ch["v"], vv)
                ph.wait("tensor", ch["sc"], rv)
                ph.mm(ps_mb[:], onesrow[:], mrow, True, True)
                bcv = ph.mm(ps_rb[:], onesrow[:], rrow, True, True, inc=ch["mm"])
                ph.wait("vector", ch["mm"], bcv)
                ph.wait("vector", ch["ldx"], lx)
                g.wait(ph, "tensor", ("ps_3x",))
                g.wait(ph, "tensor", ("ps_3q",))
                av = None
                prevq = None
                for dc in range(NDC):
                    pchunk = projw[:, 512 * dc:512 * dc + 512]
                    ph.call("vector", "tensor_tensor", tmp3[:], pchunk, ps_mb[:],
                            ALU.subtract)
                    ph.call("vector", "tensor_tensor", tmp3[:], tmp3[:], ps_rb[:],
                            ALU.mult)
                    av = ph.call("vector", "tensor_tensor",
                                 awin[:, 512 * dc:512 * dc + 512],
                                 awin[:, 512 * dc:512 * dc + 512], tmp3[:],
                                 ALU.add, inc=ch["v"])
                    ph.wait("scalar", ch["v"], av)
                    ph.wait("scalar", ch["mm"], prevq)
                    sv = ph.call("scalar", "activation", sqb3[dc % 2][:],
                                 awin[:, 512 * dc:512 * dc + 512], ACT.Square,
                                 inc=ch["sc"])
                    ph.wait("tensor", ch["v"], av)
                    ph.mm(ps_3x[:], ones_f[:], awin[:, 512 * dc:512 * dc + 512],
                          dc == 0, dc == NDC - 1,
                          inc=(ch["mm"] if dc == NDC - 1 else None))
                    ph.wait("tensor", ch["sc"], sv)
                    prevq = ph.mm(ps_3q[:], ones_f[:], sqb3[dc % 2][:], dc == 0,
                                  dc == NDC - 1, inc=ch["mm"])
                g.set(("ps_mb",), ch["v"], av)
                g.set(("ps_rb",), ch["v"], av)
                vv, rv = ln_rows(ph, ps_3x[:], ps_3q[:], mrow, rrow, vrow, trow, D)
                g.set(("ps_3x",), ch["v"], ch["v"].n)
                g.set(("ps_3q",), ch["v"], ch["v"].n)
                ph.wait("scalar", ch["v"], vv)
                ph.dma("scalar", ch["spa"], lnrows[0:1, 512 * w:512 * w + 512],
                       mrow)
                ph.dma("scalar", ch["spa"], lnrows[1:2, 512 * w:512 * w + 512],
                       rrow)
                ph.wait("scalar", ch["v"], av)
                sv = None
                for i in range(4):
                    dcs = slice(4 * i, 4 * i + 4)
                    sv = ph.dma("scalar", ch["spa"],
                                aspill.rearrange("(c p) t -> p c t", p=128)
                                [:, dcs, 512 * w:512 * w + 512],
                                awin[:].rearrange("p (c t) -> p c t", c=NDC)
                                [:, dcs, :])
                g.set(("awin",), ch["spa"], sv)
            ph.close()

    for rt in range(2):
        tc0 = 1024 * rt
        with ExitStack() as sf:
            m4_t = sf.enter_context(nc.sbuf_tensor(f"m4_t{rt}", [1, 1024], F32))
            r4_t = sf.enter_context(nc.sbuf_tensor(f"r4_t{rt}", [1, 1024], F32))
            s_hd = sf.enter_context(ExitStack())
            hd = s_hd.enter_context(nc.sbuf_tensor(f"hd{rt}", [128, NHC * 1024], BF16))
            s_ln3 = sf.enter_context(ExitStack())
            ln3 = s_ln3.enter_context(nc.sbuf_tensor(f"ln3{rt}", [128, NDC * 1024], BF16))

            with ExitStack() as s1:
                aT_t = s1.enter_context(nc.sbuf_tensor(f"aT_t{rt}", [128, NDC * 512], F32))
                tmpf = s1.enter_context(nc.sbuf_tensor(f"tmpf{rt}", [128, 512], F32))
                lnr = s1.enter_context(nc.sbuf_tensor(f"lnr{rt}", [1, 1024], F32))
                ps_mb = s1.enter_context(nc.psum_tensor(f"psf_mb{rt}", [128, 512], F32))
                ps_rb = s1.enter_context(nc.psum_tensor(f"psf_rb{rt}", [128, 512], F32))
                ph = Phase(nc)
                ph.wait("sync", ch["spa"], ch["spa"].n)
                for seg in range(2):
                    col = tc0 + 512 * seg
                    if seg > 0:
                        ph.wait("sync", ch["v"], ch["v"].n)
                    lr = ph.dma("sync", ch["ldm"], lnr[:, 0:512],
                                lnrows[0:1, col:col + 512])
                    lr = ph.dma("sync", ch["ldm"], lnr[:, 512:1024],
                                lnrows[1:2, col:col + 512])
                    ph.wait("tensor", ch["ldm"], lr)
                    la = None
                    for i in range(4):
                        dcs = slice(4 * i, 4 * i + 4)
                        la = ph.dma("sync", ch["lda"],
                                    aT_t[:].rearrange("p (c t) -> p c t", c=NDC)
                                    [:, dcs, :],
                                    aspill.rearrange("(c p) t -> p c t", p=128)
                                    [:, dcs, col:col + 512])
                    if seg > 0:
                        ph.wait("tensor", ch["v"], ch["v"].n)
                    ph.mm(ps_mb[:], onesrow[:], lnr[:, 0:512], True, True)
                    bcv = ph.mm(ps_rb[:], onesrow[:], lnr[:, 512:1024],
                                True, True, inc=ch["mm"])
                    ph.wait("vector", ch["mm"], bcv)
                    ph.wait("vector", ch["lda"], la)
                    for dc in range(NDC):
                        ph.call("vector", "tensor_tensor", tmpf[:],
                                aT_t[:, 512 * dc:512 * dc + 512], ps_mb[:],
                                ALU.subtract)
                        ph.call("vector", "tensor_tensor",
                                ln3[:, 1024 * dc + 512 * seg:1024 * dc + 512 * seg + 512],
                                tmpf[:], ps_rb[:], ALU.mult, inc=ch["v"])
                ph.close()

            with ExitStack() as s2:
                w1g = [s2.enter_context(nc.sbuf_tensor(f"w1g{rt}_{i}", [128, NDC * 256], BF16))
                       for i in range(2)]
                w3g = [s2.enter_context(nc.sbuf_tensor(f"w3g{rt}_{i}", [128, NDC * 256], BF16))
                       for i in range(2)]
                tmpg = [s2.enter_context(nc.sbuf_tensor(f"tmpg{rt}_{i}", [128, 512], F32))
                        for i in range(2)]
                ps_g = [s2.enter_context(nc.psum_tensor(f"psf_g{rt}_{i}", [128, 512], F32))
                        for i in range(2)]
                ps_u = [s2.enter_context(nc.psum_tensor(f"psf_u{rt}_{i}", [128, 512], F32))
                        for i in range(2)]
                g = Guard()
                ph = Phase(nc)
                nib = 0
                for hg in range(32):
                    buf = hg % 2
                    hcol = 256 * hg
                    g.wait(ph, "gpsimd", ("w1g", buf))
                    lw = None
                    for wsrc, wdst in ((w1T, w1g[buf]), (w3T, w3g[buf])):
                        for i in range(2):
                            dcs = slice(8 * i, 8 * i + 8)
                            lw = ph.dma("gpsimd", ch["ldw"],
                                        wdst[:].rearrange("p (c o) -> p c o", c=NDC)
                                        [:, dcs, :],
                                        wsrc.rearrange("(c p) o -> p c o", p=128)
                                        [:, dcs, hcol:hcol + 256])
                    ph.wait("tensor", ch["ldw"], lw)
                    for hcl in range(2):
                        c = 2 * hg + hcl
                        for hf in range(2):
                            pb = nib % 2
                            nib += 1
                            g.wait(ph, "tensor", ("ps_g", pb))
                            g.wait(ph, "tensor", ("ps_u", pb))
                            uv = None
                            for dc in range(NDC):
                                l1 = w1g[buf][:, 256 * dc + 128 * hcl:
                                              256 * dc + 128 * hcl + 128]
                                l3 = w3g[buf][:, 256 * dc + 128 * hcl:
                                              256 * dc + 128 * hcl + 128]
                                rhs = ln3[:, 1024 * dc + 512 * hf:
                                          1024 * dc + 512 * hf + 512]
                                lastf = (dc == NDC - 1)
                                ph.mm(ps_g[pb][:], l1, rhs, dc == 0, lastf,
                                      inc=(ch["mm"] if lastf else None))
                                uv = ph.mm(ps_u[pb][:], l3, rhs, dc == 0, lastf,
                                           inc=(ch["mm"] if lastf else None))
                            g.wait(ph, "scalar", ("tmpg", pb))
                            ph.wait("scalar", ch["mm"], uv)
                            sv = ph.call("scalar", "activation", tmpg[pb][:],
                                         ps_g[pb][:], ACT.Silu, inc=ch["sc"])
                            g.set(("ps_g", pb), ch["sc"], sv)
                            ph.wait("vector", ch["sc"], sv)
                            mv = ph.call("vector", "tensor_tensor",
                                         hd[:, 1024 * c + 512 * hf:
                                            1024 * c + 512 * hf + 512],
                                         tmpg[pb][:], ps_u[pb][:], ALU.mult,
                                         inc=ch["v"])
                            g.set(("ps_u", pb), ch["v"], mv)
                            g.set(("tmpg", pb), ch["v"], mv)
                    g.set(("w1g", buf), ch["mm"], ch["mm"].n)
                ph.close()
            s_ln3.close()

            with ExitStack() as s3:
                z_t = s3.enter_context(nc.sbuf_tensor(f"z_t{rt}", [128, 64 * 256], BF16))
                zs_t = s3.enter_context(nc.sbuf_tensor(f"zs_t{rt}", [128, 4 * 256], BF16))
                ps_f = [s3.enter_context(nc.psum_tensor(f"psf_f{rt}_{i}", [128, 256], F32))
                        for i in range(2)]
                g = Guard()
                ph = Phase(nc)
                for ts in range(4):
                    scv = None
                    for c in range(NHC):
                        pb = c % 2
                        g.wait(ph, "tensor", ("ps_f", pb))
                        mv = ph.mm(ps_f[pb][:], hc_t[:],
                                   hd[:, 1024 * c + 256 * ts:1024 * c + 256 * ts + 256],
                                   True, True, inc=ch["mm"])
                        ph.wait("scalar", ch["mm"], mv)
                        scv = ph.call("scalar", "mul",
                                      z_t[:, 256 * c:256 * c + 256], ps_f[pb][:],
                                      RS2, inc=ch["sc"])
                        g.set(("ps_f", pb), ch["sc"], scv)
                    ph.wait("vector", ch["sc"], scv)
                    zv = z_t[:].rearrange("p (c t) -> p c t", c=64)
                    lastv = None
                    for d in (32, 16, 8, 4, 2, 1):
                        ilist = [i2 for i2 in range(64) if (i2 & d) == 0]
                        for g0 in range(0, 32, 4):
                            ib = ilist[g0:g0 + 4]
                            if d >= 4:
                                iap = zv[:, ib[0]:ib[0] + 4, :]
                                jap = zv[:, ib[0] + d:ib[0] + d + 4, :]
                                sap = zs_t[:].rearrange("p (c t) -> p c t", c=4)
                            elif d == 2:
                                zz = z_t[:].rearrange(
                                    "p (g two r t) -> p g two r t", two=2, r=2,
                                    t=256)
                                gg0 = ib[0] // 4
                                iap = zz[:, gg0:gg0 + 2, 0, :, :]
                                jap = zz[:, gg0:gg0 + 2, 1, :, :]
                                sap = zs_t[:].rearrange(
                                    "p (g r t) -> p g r t", g=2, r=2, t=256)
                            else:
                                zz = z_t[:].rearrange(
                                    "p (g two t) -> p g two t", two=2, t=256)
                                gg0 = ib[0] // 2
                                iap = zz[:, gg0:gg0 + 4, 0, :]
                                jap = zz[:, gg0:gg0 + 4, 1, :]
                                sap = zs_t[:].rearrange("p (c t) -> p c t", c=4)
                            ph.call("vector", "tensor_copy", sap, iap)
                            if d > 1:
                                ph.call("vector", "tensor_tensor", iap, sap, jap,
                                        ALU.add)
                                lastv = ph.call("vector", "tensor_tensor", jap,
                                                sap, jap, ALU.subtract,
                                                inc=ch["v"])
                            else:
                                hdz = hd[:].rearrange(
                                    "p (g two t) -> p g two t", two=2, t=1024)
                                hi = hdz[:, gg0:gg0 + 4, 0,
                                         256 * ts:256 * ts + 256]
                                hj = hdz[:, gg0:gg0 + 4, 1,
                                         256 * ts:256 * ts + 256]
                                ph.call("vector", "tensor_tensor", hi, sap, jap,
                                        ALU.add)
                                lastv = ph.call("vector", "tensor_tensor", hj,
                                                sap, jap, ALU.subtract,
                                                inc=ch["v"])
                    ph.wait("tensor", ch["v"], lastv)
                ph.close()

            with ExitStack() as s4:
                w2g = [s4.enter_context(nc.sbuf_tensor(f"w2g{rt}_{i}", [128, NHC * 128], BF16))
                       for i in range(2)]
                fst = [s4.enter_context(nc.sbuf_tensor(f"fst{rt}_{i}", [128, 512], F32))
                       for i in range(2)]
                sqb4 = s4.enter_context(nc.sbuf_tensor(f"sqb4{rt}", [128, 512], F32))
                rows4 = s4.enter_context(nc.sbuf_tensor(f"rows4{rt}", [1, 2 * 512], F32))
                ps_w = [s4.enter_context(nc.psum_tensor(f"psf_w{rt}_{i}", [128, 512], F32))
                        for i in range(2)]
                ps_st = s4.enter_context(nc.psum_tensor(f"psf_st{rt}", [1, 2048], F32))
                g = Guard()
                ph = Phase(nc)
                prevq = None
                for oc in range(NDC):
                    buf = oc % 2
                    g.wait(ph, "gpsimd", ("w2g", buf))
                    lw = None
                    for i in range(8):
                        hcs = slice(8 * i, 8 * i + 8)
                        lw = ph.dma("gpsimd", ch["ldw"],
                                    w2g[buf][:].rearrange("p (c o) -> p c o", c=NHC)
                                    [:, hcs, :],
                                    w2T.rearrange("(c p) o -> p c o", p=128)
                                    [:, hcs, 128 * oc:128 * oc + 128])
                    ph.wait("tensor", ch["ldw"], lw)
                    for tp in range(2):
                        pb = tp
                        g.wait(ph, "tensor", ("ps_w", pb))
                        mv = None
                        for c in range(NHC):
                            mv = ph.mm(ps_w[pb][:],
                                       w2g[buf][:, 128 * c:128 * c + 128],
                                       hd[:, 1024 * c + 512 * tp:
                                          1024 * c + 512 * tp + 512],
                                       c == 0, c == NHC - 1,
                                       inc=(ch["mm"] if c == NHC - 1 else None))
                        ph.wait("vector", ch["mm"], mv)
                        g.wait(ph, "vector", ("fst", pb))
                        g.wait(ph, "vector", ("fstmm", pb))
                        fv = ph.call("vector", "tensor_copy", fst[pb][:],
                                     ps_w[pb][:], inc=ch["v"])
                        g.set(("ps_w", pb), ch["v"], fv)
                        ph.wait("scalar", ch["v"], fv)
                        ph.wait("scalar", ch["mm"], prevq)
                        sv = ph.call("scalar", "activation", sqb4[:], fst[pb][:],
                                     ACT.Square, inc=ch["sc"])
                        ph.wait("tensor", ch["v"], fv)
                        ph.mm(ps_st[:, 1024 * tp:1024 * tp + 512], ones_f[:],
                              fst[pb][:], oc == 0, oc == NDC - 1, inc=ch["mm"])
                        ph.wait("tensor", ch["sc"], sv)
                        prevq = ph.mm(ps_st[:, 1024 * tp + 512:1024 * tp + 1024],
                                      ones_f[:], sqb4[:], oc == 0, oc == NDC - 1,
                                      inc=ch["mm"])
                        g.set(("fstmm", pb), ch["mm"], prevq)
                        dv = ph.dma("scalar", ch["spf"],
                                    fspill[128 * oc:128 * oc + 128,
                                           tc0 + 512 * tp:tc0 + 512 * tp + 512],
                                    fst[pb][:])
                        g.set(("fst", pb), ch["spf"], dv)
                    g.set(("w2g", buf), ch["mm"], ch["mm"].n)
                for tp in range(2):
                    vrow, trow = rows4[:, 0:512], rows4[:, 512:1024]
                    ln_rows(ph, ps_st[:, 1024 * tp:1024 * tp + 512],
                            ps_st[:, 1024 * tp + 512:1024 * tp + 1024],
                            m4_t[:, 512 * tp:512 * tp + 512],
                            r4_t[:, 512 * tp:512 * tp + 512], vrow, trow, D)
                ph.close()
            s_hd.close()

            with ExitStack() as s5:
                fw_t = s5.enter_context(nc.sbuf_tensor(f"fw_t{rt}", [128, NDC * 512], F32))
                aw_t = s5.enter_context(nc.sbuf_tensor(f"aw_t{rt}", [128, NDC * 512], F32))
                ow_t = s5.enter_context(nc.sbuf_tensor(f"ow_t{rt}", [128, NDC * 512], F32))
                tmpo = s5.enter_context(nc.sbuf_tensor(f"tmpo{rt}", [128, 512], F32))
                ps_mb = s5.enter_context(nc.psum_tensor(f"ps5_mb{rt}", [128, 512], F32))
                ps_rb = s5.enter_context(nc.psum_tensor(f"ps5_rb{rt}", [128, 512], F32))
                ph = Phase(nc)
                ph.wait("sync", ch["spf"], ch["spf"].n)
                for tp in range(2):
                    col = tc0 + 512 * tp
                    la = None
                    for i in range(4):
                        dcs = slice(4 * i, 4 * i + 4)
                        ph.dma("sync", ch["ldf"],
                               fw_t[:].rearrange("p (c t) -> p c t", c=NDC)
                               [:, dcs, :],
                               fspill.rearrange("(c p) t -> p c t", p=128)
                               [:, dcs, col:col + 512])
                        la = ph.dma("sync", ch["ldf"],
                                    aw_t[:].rearrange("p (c t) -> p c t", c=NDC)
                                    [:, dcs, :],
                                    aspill.rearrange("(c p) t -> p c t", p=128)
                                    [:, dcs, col:col + 512])
                    if tp > 0:
                        ph.wait("tensor", ch["v"], ch["v"].n)
                    ph.mm(ps_mb[:], onesrow[:], m4_t[:, 512 * tp:512 * tp + 512],
                          True, True)
                    bcv = ph.mm(ps_rb[:], onesrow[:],
                                r4_t[:, 512 * tp:512 * tp + 512], True, True,
                                inc=ch["mm"])
                    ph.wait("vector", ch["mm"], bcv)
                    ph.wait("vector", ch["ldf"], la)
                    ph.wait("vector", ch["out"], ch["out"].n)
                    ov = None
                    for dc in range(NDC):
                        ph.call("vector", "tensor_tensor", tmpo[:],
                                fw_t[:, 512 * dc:512 * dc + 512], ps_mb[:],
                                ALU.subtract)
                        ph.call("vector", "tensor_tensor", tmpo[:], tmpo[:],
                                ps_rb[:], ALU.mult)
                        ov = ph.call("vector", "tensor_tensor",
                                     ow_t[:, 512 * dc:512 * dc + 512], tmpo[:],
                                     aw_t[:, 512 * dc:512 * dc + 512], ALU.add,
                                     inc=ch["v"])
                    ph.wait("sync", ch["v"], ov)
                    for i in range(4):
                        dcs = slice(4 * i, 4 * i + 4)
                        ph.dma("sync", ch["out"],
                               out_d.rearrange("(c p) t -> p c t", p=128)
                               [:, dcs, col:col + 512],
                               ow_t[:].rearrange("p (c t) -> p c t", c=NDC)
                               [:, dcs, :])
                if rt == 1:
                    ph.wait("sync", ch["out"], ch["out"].n)
                    for nm in ("spk", "spv", "spq", "spa", "spf"):
                        ph.wait("sync", ch[nm], ch[nm].n)
                ph.close()

    st.close()
    global LAST_CH
    LAST_CH = {k: (c.sem.num, c.n) for k, c in ch.items()}
    return nc


_NC_CACHE = {}


def _get_nc(debug):
    key = bool(debug)
    if key not in _NC_CACHE:
        _NC_CACHE[key] = build(debug=key)
    return _NC_CACHE[key]


def _hadamard128():
    idx = np.arange(128)
    anded = idx[:, None] & idx[None, :]
    pc = np.zeros((128, 128), dtype=np.int64)
    for b in range(7):
        pc += (anded >> b) & 1
    return np.where(pc % 2 == 0, 1.0, -1.0).astype(np.float32)


def kernel(L0_attn_x, L0_attn_W_qkv, L0_attn_W_proj, L0_ffn_W1, L0_ffn_W2,
           L0_ffn_W3):
    x = np.ascontiguousarray(np.asarray(L0_attn_x, dtype=np.float32))
    wqkvT = np.ascontiguousarray(np.asarray(L0_attn_W_qkv, np.float32).T)
    wprojT = np.ascontiguousarray(np.asarray(L0_attn_W_proj, np.float32).T)
    w1T = np.ascontiguousarray(np.asarray(L0_ffn_W1, np.float32).T)
    w2T = np.ascontiguousarray(np.asarray(L0_ffn_W2, np.float32).T)
    w3T = np.ascontiguousarray(np.asarray(L0_ffn_W3, np.float32).T)
    hmat = (_hadamard128() * (2.0 ** -6)).astype(ml_dtypes.bfloat16)

    nc = _get_nc(DEBUG)
    in_maps = []
    owns = []
    for c in range(8):
        b, e = c >> 1, c & 1
        blocks = np.arange(T).reshape(16, 2, 128)
        own = blocks[:, e, :].ravel()
        other = blocks[:, 1 - e, :].ravel()
        key_order = np.concatenate([own, other])
        owns.append((b, own))
        xfc = np.ascontiguousarray(x[b].T[:, key_order])
        qg = own.reshape(16, 128).astype(np.float64)
        kg = key_order.reshape(32, 128).astype(np.float64)
        tm = np.empty((128, 32, 128), np.float32)
        for j in range(16):
            for d, c in ((0, j), (1, 16 + j)):
                tm[:, 2 * j + d, :] = (
                    kg[c][:, None] <= qg[j][None, :]).astype(np.float32)
        in_maps.append({
            "xf": xfc, "wqkvT": wqkvT, "wprojT": wprojT, "w1T": w1T,
            "w2T": w2T, "w3T": w3T,
            "tmask": tm.reshape(128, 32 * 128).astype(ml_dtypes.bfloat16),
            "hconst": hmat,
        })

    res = run_bass_kernel_spmd(nc, in_maps, core_ids=list(range(8)))
    out = np.empty((4, T, D), dtype=np.float32)
    for c in range(8):
        b, own = owns[c]
        out[b][own, :] = res.results[c]["out"].T
    if DEBUG:
        kernel.last_results = res.results
    return out
